# version 22
# baseline (speedup 1.0000x reference)
import numpy as np
from contextlib import ExitStack

import concourse.bass as bass
import concourse.mybir as mybir
from concourse.bass_utils import run_bass_kernel_spmd

F32 = mybir.dt.float32
BF16 = mybir.dt.bfloat16
I32 = mybir.dt.int32
AF = mybir.ActivationFunctionType
ALU = mybir.AluOpType
AX = mybir.AxisListType

NTILE = 16
NGRP = 4
D = 1024
DC = 8
S = 2048
HID = 3072
HC = 24
NSEQ = 2
EPS = 1e-5
H, E = 16, 64

_SECT = {"tensor": "PE", "scalar": "SC", "vector": "DV", "gpsimd": "GP",
         "sync": "SY"}


class Ctx:
    def __init__(self, nc, block, stack):
        self.nc = nc
        self.block = block
        self.stack = stack
        self.sem = {}
        self.cnt = {}
        for name in ("LD", "LD2", "ST", "PE", "SC", "DV", "GP", "SY",
                     "WD0", "WD1", "WL", "WP", "WP2"):
            self.sem[name] = stack.enter_context(nc.semaphore(name))
            self.cnt[name] = 0
        self.bar_s = stack.enter_context(nc.sbuf_tensor("bar_s", [1, 8], F32))

    def t(self, f):
        self.block.tensor(f)

    def s(self, f):
        self.block.scalar(f)

    def v(self, f):
        self.block.vector(f)

    def g(self, f):
        self.block.gpsimd(f)

    def y(self, f):
        self.block.sync(f)

    def u(self, name):
        self._uid = getattr(self, "_uid", 0) + 1
        return f"{name}__{self._uid}"

    def inc(self, instr, name, n=1):
        instr.then_inc(self.sem[name], n)
        self.cnt[name] += n
        return self.cnt[name]

    def wait(self, eng, name, val=None):
        eng.wait_ge(self.sem[name], self.cnt[name] if val is None else val)

    def fence(self, e, instr, name):
        self.inc(instr, name, 1)
        e.wait_ge(self.sem[name], self.cnt[name])

    def tok(self, name):
        return self.cnt[name]

    def wtok(self, e, name, val):
        if val > 0:
            e.wait_ge(self.sem[name], val)

    def barrier(self):
        def mk_inc(engname):
            def f(e):
                if engname == "scalar":
                    ins = e.memzero(self.bar_s[:])
                elif engname in ("vector", "gpsimd"):
                    ins = e.engine_nop()
                else:
                    ins = e.nop()
                self.inc(ins, _SECT[engname], 1)
            return f
        for e in ("scalar", "vector", "gpsimd", "sync"):
            getattr(self, e[0] if e != "sync" else "y")(mk_inc(e))
        snap = dict(self.cnt)

        def mk_wait():
            def f(e):
                for nm, v in snap.items():
                    if v > 0:
                        e.wait_ge(self.sem[nm], v)
            return f
        for e in ("tensor", "scalar", "vector", "gpsimd", "sync"):
            getattr(self, e[0] if e != "sync" else "y")(mk_wait())


def emit_consts(C):
    k = C.k

    def g(e):
        C.fence(e, e.memset(k["ident"][:], 0.0), "GP")
        e.affine_select(out=k["ident"][:], in_=k["ident"][:],
                        compare_op=ALU.not_equal, fill=1.0,
                        base=0, pattern=[[-1, 128]], channel_multiplier=1)
        C.fence(e, e.memset(k["ident65"][:], 0.0), "GP")
        e.affine_select(out=k["ident65"][:], in_=k["ident65"][:],
                        compare_op=ALU.not_equal, fill=1.0,
                        base=0, pattern=[[-1, 65]], channel_multiplier=1)
        e.memset(k["ones_col"][:], 1.0)
        e.memset(k["ones_row"][:], 1.0)
        C.fence(e, e.memset(k["cmask"][:], 0.0), "GP")
        e.affine_select(out=k["cmask"][:], in_=k["cmask"][:],
                        compare_op=ALU.is_ge, fill=-1.0e9,
                        base=0, pattern=[[-1, 128]], channel_multiplier=1)
        e.iota(k["cnt16"][:], [[-128, NTILE]], base=2047, channel_multiplier=-1,
               allow_small_or_imprecise_dtypes=True)
        e.memset(k["eps_v"][:], EPS)
        C.inc(e.engine_nop(), "GP", 1)
    C.g(g)


def emit_weight_prep(C, SCR, jobs, stgw=None):
    if not jobs:
        return
    nc = C.nc
    stgw = stgw or 12 * D
    stgtw = max(R for _, R, _ in jobs)
    pstack = ExitStack()
    stg = pstack.enter_context(nc.sbuf_tensor(C.u("wprep"),
                                              [128, max(stgw, stgtw)], BF16))
    stgT = stg

    for nm, R, Cc in jobs:
        src = C.dram[nm]
        scr1 = SCR[nm + "_s1"]
        scr2 = SCR[nm + "_t"]
        nch = R // 128

        hchunks = max(1, stgw // Cc)
        for c0 in range(0, nch, hchunks):
            c1 = min(c0 + hchunks, nch)

            def g(e, src=src, c0=c0, c1=c1, Cc=Cc):
                C.wait(e, "WP2")
                C.inc(e.dma_start(
                    out=stg[:, 0:(c1 - c0) * Cc].rearrange(
                        "p (c d) -> p c d", d=Cc),
                    in_=src.rearrange("(c p) d -> p c d",
                                      p=128)[:, c0:c1]),
                    "WP", 16)
            C.g(g)

            def y1(e, scr1=scr1, c0=c0, c1=c1, Cc=Cc):
                C.wait(e, "WP")
                C.inc(e.dma_start(
                    out=scr1.rearrange("(c p) d -> p c d", p=128)[:, c0:c1],
                    in_=stg[:, 0:(c1 - c0) * Cc].rearrange(
                        "p (c d) -> p c d", d=Cc)),
                    "WP2", 16)
            C.y(y1)

        for cc in range(Cc // 128):
            def y2(e, scr1=scr1, scr2=scr2, cc=cc, R=R):
                C.wait(e, "WP2")
                C.inc(e.dma_start(out=stgT[:, 0:R],
                                  in_=scr1[:, cc * 128:(cc + 1) * 128],
                                  transpose=True), "WP2", 16)
                C.wait(e, "WP2")
                C.inc(e.dma_start(out=scr2[cc * 128:(cc + 1) * 128, :],
                                  in_=stgT[:, 0:R]), "WP2", 16)
            C.y(y2)
    pstack.close()


def emit_load_x(C, seq):
    def y(e):
        C.inc(e.dma_start(out=C.xs[:].rearrange("p (i d) -> p i d", d=D),
                          in_=C.dram["x"][seq].rearrange("(i p) d -> p i d",
                                                         p=128)),
              "LD2", 16)
    C.y(y)


def emit_store_x(C, seq):
    def y(e):
        C.inc(e.dma_start(
            out=C.dram["out"][seq].rearrange("(i p) d -> p i d", p=128),
            in_=C.xs[:].rearrange("p (i d) -> p i d", d=D)), "ST", 16)
    C.y(y)


def emit_transpose_1024(C, src_fn, psum_tr, dst_fn, nchunk=8):
    k = C.k

    def t(e):
        C.wait(e, "DV")
        C.wait(e, "SC")
        for c in range(nchunk):
            ins = e.transpose(psum_tr[:, c * 128:(c + 1) * 128], src_fn(c),
                              k["ident"][:])
        C.inc(ins, "PE", 1)
    C.t(t)

    def v(e):
        C.wait(e, "PE")
        for c in range(nchunk):
            ins = e.tensor_copy(dst_fn(c), psum_tr[:, c * 128:(c + 1) * 128])
        C.inc(ins, "DV", 1)
    C.v(v)


L0A_STAGE = 5

def emit_l0a(C):
    nc = C.nc
    k = C.k
    xs = C.xs
    with ExitStack() as ph:
        ec = ph.enter_context
        wq = ec(nc.sbuf_tensor(C.u("l0_wq_s"), [128, DC * D], BF16))
        wk = ec(nc.sbuf_tensor(C.u("l0_wk_s"), [128, DC * D], BF16))
        wv = ec(nc.sbuf_tensor(C.u("l0_wv_s"), [128, DC * D], BF16))
        bq = ec(nc.sbuf_tensor(C.u("l0_bq_s"), [128, 8], F32))
        bk = ec(nc.sbuf_tensor(C.u("l0_bk_s"), [128, 8], F32))
        bvr_bf = ec(nc.sbuf_tensor(C.u("l0_bv_row_bf"), [1, D], BF16))
        bvb = ec(nc.sbuf_tensor(C.u("l0_bv_bc"), [128, D], BF16))
        h0 = ec(nc.sbuf_tensor(C.u("h0_fm"), [128, DC * S], BF16))
        kf = ec(nc.sbuf_tensor(C.u("k_fm"), [128, DC * S], BF16))
        qg = ec(nc.sbuf_tensor(C.u("q_fm_g"), [128, DC * 512], BF16))
        e_scr = ec(nc.sbuf_tensor(C.u("e_scr"), [128, S], BF16))
        sm = ec(nc.sbuf_tensor(C.u("l0_sm"), [128, 8], F32))
        lnp = ec(nc.sbuf_tensor(C.u("l0_lnp"), [128, 6 * NTILE], F32))
        htm2 = ec(nc.sbuf_tensor(C.u("l0_htm2"), [128, 2 * D], BF16))

        def g(e):
            C.wait(e, "LD2")
            for nm, dst in (("l0_wq", wq), ("l0_wk", wk), ("l0_wv", wv)):
                C.inc(e.dma_start(out=dst[:].rearrange("p (c d) -> p c d", d=D),
                                  in_=C.dram[nm].rearrange(
                    "(c p) d -> p c d", p=128)), "LD", 16)
            for c in range(8):
                C.inc(e.dma_start(out=bq[:, c:c + 1],
                                  in_=C.dram["l0_bq"][c * 128:(c + 1) * 128][:, None]),
                      "LD", 16)
                C.inc(e.dma_start(out=bk[:, c:c + 1],
                                  in_=C.dram["l0_bk"][c * 128:(c + 1) * 128][:, None]),
                      "LD", 16)
            C.inc(e.dma_start(out=bvr_bf[:], in_=C.dram["l0_bv"][None, :]),
                  "LD", 16)
        C.g(g)
        t_loads = C.tok("LD")

        if getattr(C, "f0_prep_pending", False):
            C.f0_prep_pending = False
            emit_weight_prep(C, C.scr, [("f0_wr", D, D), ("f0_wk", D, D),
                                        ("f0_wv", D, D)], stgw=D)
        emit_pre_stats(C, lnp)
        ap6 = lnp[:].rearrange("p (i c) -> p i c", c=6)
        t_stats = C.tok("DV")

        p_tr = ec(nc.psum_tensor(C.u("p_tr"), [128, 2 * D], BF16))
        p_mm = ec(nc.psum_tensor(C.u("p_mm"), [128, 2 * 512], F32))
        bc_toks = []
        for half in range(2):
            def t0(e, half=half):
                C.wait(e, "LD", t_loads)
                if half == 0:
                    C.wait(e, "DV", )
                C.inc(e.matmul(p_mm[:, half * 512:(half + 1) * 512],
                               k["ones_row"][:],
                               bvr_bf[:, half * 512:(half + 1) * 512],
                               start=True, stop=True), "PE", 1)
            C.t(t0)
            tok_pe = C.tok("PE")

            def v0(e, half=half, tok_pe=tok_pe):
                C.wtok(e, "PE", tok_pe)
                C.inc(e.tensor_copy(bvb[:, half * 512:(half + 1) * 512],
                                    p_mm[:, half * 512:(half + 1) * 512]),
                      "DV", 1)
            C.v(v0)
        t_bvb = C.tok("DV")

        ap_tok = [0] * NTILE
        tr_tok = [0] * NTILE
        kproj_tok = [0] * NGRP
        kev_tok = {}
        for grp in range(NGRP):
            hv_sc, hv_dv = C.cnt["SC"], C.cnt["DV"]

            def s(e, grp=grp, hv_dv=hv_dv,
                  pv=tuple(tr_tok[max(0, grp * 4 - 2): grp * 4])):
                C.wait(e, "DV", t_stats)
                for ti in range(4):
                    i = grp * 4 + ti
                    if ti >= 2:
                        C.wait(e, "DV", hv_dv + ti - 1)
                    elif i >= 2:
                        C.wait(e, "DV", pv[ti])
                    C.inc(e.activation(htm2[:, (i % 2) * D:(i % 2 + 1) * D],
                                       xs[:, i * D:(i + 1) * D], AF.Identity,
                                       scale=ap6[:, i, 4:5],
                                       bias=ap6[:, i, 5:6]), "SC", 1)
            C.s(s)
            for ti in range(4):
                ap_tok[grp * 4 + ti] = hv_sc + ti + 1
            for ti in range(4):
                i = grp * 4 + ti

                def t(e, i=i):
                    C.wtok(e, "SC", ap_tok[i])
                    if i >= 2:
                        C.wtok(e, "DV", tr_tok[i - 2])
                    for c in range(8):
                        ins = e.transpose(
                            p_tr[:, (i % 2) * D + c * 128:
                                 (i % 2) * D + (c + 1) * 128],
                            htm2[:, (i % 2) * D + c * 128:
                                 (i % 2) * D + (c + 1) * 128],
                            k["ident"][:])
                    C.inc(ins, "PE", 1)
                C.t(t)
                tok_pe = C.tok("PE")

                def v(e, i=i, tok_pe=tok_pe):
                    C.wtok(e, "PE", tok_pe)
                    for c in range(8):
                        ins = e.tensor_copy(
                            h0[:, c * S + i * 128: c * S + (i + 1) * 128],
                            p_tr[:, (i % 2) * D + c * 128:
                                 (i % 2) * D + (c + 1) * 128])
                    C.inc(ins, "DV", 1)
                C.v(v)
                tr_tok[i] = C.tok("DV")
                assert tr_tok[i] == hv_dv + ti + 1

            for oc in range(8):
                half = oc % 2

                def t(e, oc=oc, grp=grp, half=half):
                    if oc == 0:
                        C.wtok(e, "DV", tr_tok[grp * 4 + 3])
                        if grp == 0:
                            C.wait(e, "DV", t_bvb)
                    if (grp, oc) in kev_tok:
                        pass
                    prev = kev_tok.get((grp, oc - 2)) or (
                        kev_tok.get((grp - 1, 6 + oc)) if oc < 2 else None)
                    if prev:
                        C.wtok(e, "DV", prev)
                    for dc in range(8):
                        ins = e.matmul(
                            p_mm[:, half * 512:(half + 1) * 512],
                            wk[:, dc * D + oc * 128: dc * D + (oc + 1) * 128],
                            h0[:, dc * S + grp * 512: dc * S + (grp + 1) * 512],
                            start=(dc == 0), stop=(dc == 7))
                    C.inc(ins, "PE", 1)
                C.t(t)
                tok_pe = C.tok("PE")

                def v(e, oc=oc, grp=grp, half=half, tok_pe=tok_pe):
                    C.wtok(e, "PE", tok_pe)
                    C.inc(e.tensor_scalar(
                        kf[:, oc * S + grp * 512: oc * S + (grp + 1) * 512],
                        p_mm[:, half * 512:(half + 1) * 512], bk[:, oc:oc + 1],
                        None, op0=ALU.add), "DV", 1)
                C.v(v)
                kev_tok[(grp, oc)] = C.tok("DV")
            kproj_tok[grp] = C.tok("DV")

        p_s = ec(nc.psum_tensor(C.u("p_s"), [128, S], F32))
        exp_tok = [0] * (NTILE + 1)
        oev_tok = [0] * NTILE
        q_tok = [0] * NGRP
        qev_tok = {}
        for i in range(NTILE):
            grp, ti = divmod(i, 4)
            if ti == 0:
                for oc in range(8):
                    def t(e, oc=oc, grp=grp, i=i):
                        if oc == 0:
                            C.wtok(e, "DV", kproj_tok[NGRP - 1])
                            if i > 0:
                                C.wtok(e, "DV", oev_tok[i - 1])
                        prev = qev_tok.get((grp, oc - 1))
                        if prev:
                            C.wtok(e, "DV", prev)
                        for dc in range(8):
                            ins = e.matmul(
                                p_mm[:, 0:512],
                                wq[:, dc * D + oc * 128: dc * D + (oc + 1) * 128],
                                h0[:, dc * S + grp * 512:
                                   dc * S + (grp + 1) * 512],
                                start=(dc == 0), stop=(dc == 7))
                        C.inc(ins, "PE", 1)
                    C.t(t)
                    tok_pe = C.tok("PE")

                    def vv(e, oc=oc, tok_pe=tok_pe):
                        C.wtok(e, "PE", tok_pe)
                        C.inc(e.tensor_scalar(qg[:, oc * 512:(oc + 1) * 512],
                                              p_mm[:, 0:512], bq[:, oc:oc + 1],
                                              None, op0=ALU.add), "DV", 1)
                    C.v(vv)
                    qev_tok[(grp, oc)] = C.tok("DV")
                q_tok[grp] = C.tok("DV")

            ncols = (i + 1) * 128

            def t(e, ti=ti, ncols=ncols, grp=grp, i=i):
                C.wtok(e, "DV", q_tok[grp])
                C.wtok(e, "SC", exp_tok[i - 1] if i > 0 else 0)
                if i > 0:
                    C.wtok(e, "DV", oev_tok[i - 1] if False else 0)
                for dc in range(8):
                    for blk in range((ncols + 511) // 512):
                        n = min(512, ncols - blk * 512)
                        ins = e.matmul(
                            p_s[:, blk * 512: blk * 512 + n],
                            qg[:, dc * 512 + ti * 128: dc * 512 + (ti + 1) * 128],
                            kf[:, dc * S + blk * 512: dc * S + blk * 512 + n],
                            start=(dc == 0), stop=(dc == 7))
                C.inc(ins, "PE", 1)
            C.t(t)
            sc_pe = C.tok("PE")

            def t2(e, i=i):
                if i > 0:
                    C.wtok(e, "DV", oev_tok[i - 1])
                else:
                    C.wait(e, "DV", t_bvb)
                    C.wtok(e, "DV", kproj_tok[NGRP - 1])
                C.wtok(e, "DV", qev_tok.get((i // 4, 7), 0))
                for dc in range(8):
                    for half in range(2):
                        ins = e.matmul(
                            p_mm[:, half * 512:(half + 1) * 512],
                            h0[:, dc * S + i * 128: dc * S + (i + 1) * 128],
                            wv[:, dc * D + half * 512: dc * D + (half + 1) * 512],
                            start=(dc == 0), stop=(dc == 7))
                C.inc(ins, "PE", 1)
            C.t(t2)
            v_pe = C.tok("PE")

            def v(e, ncols=ncols, sc_pe=sc_pe):
                C.wtok(e, "PE", sc_pe)
                C.fence(e, e.tensor_tensor(p_s[:, ncols - 128: ncols],
                                           p_s[:, ncols - 128: ncols],
                                           k["cmask"][:], op=ALU.add), "DV")
                C.fence(e, e.tensor_reduce(sm[:, 0:1], p_s[:, 0:ncols],
                                           axis=AX.X, op=ALU.max), "DV")
                C.fence(e, e.tensor_scalar(sm[:, 1:2], sm[:, 0:1], 1.0 / 32,
                                           None, op0=ALU.mult), "DV")
                C.fence(e, e.tensor_scalar(sm[:, 1:2], sm[:, 1:2], 0.0, None,
                                           op0=ALU.max), "DV")
                C.inc(e.tensor_scalar(sm[:, 2:3], sm[:, 1:2], -1.0, None,
                                      op0=ALU.mult), "DV", 1)
            C.v(v)
            mx_tok = C.tok("DV")

            def s(e, ncols=ncols, mx_tok=mx_tok):
                C.wtok(e, "DV", mx_tok)
                e.activation(e_scr[:, 0:ncols], p_s[:, 0:ncols], AF.Exp,
                             scale=1.0 / 32, bias=sm[:, 2:3],
                             accum_out=sm[:, 3:4])
                C.inc(e.activation(sm[:, 4:5], sm[:, 2:3], AF.Exp), "SC", 1)
            C.s(s)
            exp_tok[i] = C.tok("SC")

            def v(e, i=i, v_pe=v_pe):
                C.wtok(e, "SC", exp_tok[i])
                C.fence(e, e.tensor_tensor(sm[:, 5:6], k["cnt16"][:, i:i + 1],
                                           sm[:, 4:5], op=ALU.mult), "DV")
                C.fence(e, e.tensor_tensor(sm[:, 5:6], sm[:, 5:6], sm[:, 3:4],
                                           op=ALU.add), "DV")
                C.fence(e, e.reciprocal(sm[:, 6:7], sm[:, 5:6]), "DV")
                C.fence(e, e.tensor_tensor(sm[:, 6:7], sm[:, 6:7], sm[:, 5:6],
                                           op=ALU.mult), "DV")
                C.wtok(e, "PE", v_pe)
                C.fence(e, e.tensor_tensor(k["scr_a"][:], p_mm[:], bvb[:],
                                           op=ALU.add), "DV")
                C.inc(e.tensor_scalar(k["scr_a"][:], k["scr_a"][:], sm[:, 6:7], None,
                                      op0=ALU.mult), "DV", 1)
            C.v(v)
            oev_tok[i] = C.tok("DV")

            emit_ln_post(C, k["scr_a"][:], xs[:, i * D:(i + 1) * D])


def emit_ln_post(C, o_ap, xs_ap):
    k = C.k
    st = k["st8"]

    def v1(e):
        C.wait(e, "DV")
        C.wait(e, "SC")
        C.fence(e, e.tensor_reduce(st[:, 0:1], o_ap, axis=AX.X, op=ALU.add),
                "DV")
        C.fence(e, e.tensor_scalar(st[:, 2:3], st[:, 0:1], 1.0 / D, None,
                                   op0=ALU.mult), "DV")
        C.inc(e.scalar_tensor_tensor(st[:, 4:5], st[:, 2:3], -1.0,
                                     st[:, 2:3], op0=ALU.mult,
                                     op1=ALU.mult), "DV", 1)
    C.v(v1)
    t3 = C.tok("DV")
    t3c = t3 + 2

    def s1(e, t3=t3, t3c=t3c):
        C.wait(e, "DV", t3)
        C.inc(e.activation(k["scr_b"][:], o_ap, AF.Square,
                           accum_out=st[:, 1:2]), "SC", 1)
        C.wait(e, "DV", t3c)
        C.fence(e, e.activation(st[:, 4:5], st[:, 3:4], AF.Ln,
                                bias=k["eps_v"][:, 0:1]), "SC")
        C.inc(e.activation(st[:, 5:6], st[:, 4:5], AF.Exp, scale=-0.5),
              "SC", 1)
    C.s(s1)
    t3b = C.tok("SC") - 2
    t4 = C.tok("SC")

    def v2(e, t3b=t3b):
        C.wait(e, "SC", t3b)
        C.fence(e, e.tensor_scalar(st[:, 3:4], st[:, 1:2], 1.0 / D, None,
                                   op0=ALU.mult), "DV")
        C.inc(e.tensor_tensor(st[:, 3:4], st[:, 3:4], st[:, 4:5], op=ALU.add),
              "DV", 1)
    C.v(v2)

    def v3(e, t4=t4):
        C.wait(e, "SC", t4)
        C.fence(e, e.scalar_tensor_tensor(st[:, 6:7], st[:, 2:3], -1.0,
                                          st[:, 5:6], op0=ALU.mult,
                                          op1=ALU.mult), "DV")
        C.fence(e, e.tensor_scalar(k["scr_b"][:], o_ap, st[:, 5:6],
                                   st[:, 6:7], op0=ALU.mult, op1=ALU.add),
                "DV")
        C.inc(e.tensor_tensor(xs_ap, xs_ap, k["scr_b"][:], op=ALU.add),
              "DV", 1)
    C.v(v3)


def emit_l0f(C, SCR):
    nc = C.nc
    k = C.k
    xs = C.xs
    with ExitStack() as ph:
        ec = ph.enter_context
        wrT = ec(nc.sbuf_tensor(C.u("f0_wrT"), [128, DC * D], BF16))
        wkT = ec(nc.sbuf_tensor(C.u("f0_wkT"), [128, DC * D], BF16))
        wvT = ec(nc.sbuf_tensor(C.u("f0_wvT"), [128, DC * D], BF16))
        mur = ec(nc.sbuf_tensor(C.u("f0_mur_s"), [128, 8], F32))
        muk = ec(nc.sbuf_tensor(C.u("f0_muk_s"), [128, 8], F32))
        murc = ec(nc.sbuf_tensor(C.u("f0_murc"), [128, 8], F32))
        mukc = ec(nc.sbuf_tensor(C.u("f0_mukc"), [128, 8], F32))
        hext = ec(nc.sbuf_tensor(C.u("f0_hext"), [128, 8 * 513], BF16))
        xr = ec(nc.sbuf_tensor(C.u("f0_xr"), [128, 8 * 512], BF16))
        xk = ec(nc.sbuf_tensor(C.u("f0_xk"), [128, 8 * 512], BF16))
        kact = ec(nc.sbuf_tensor(C.u("f0_kact"), [128, 8 * 512], BF16))
        tmp_bf = ec(nc.sbuf_tensor(C.u("f0_tmp"), [128, 512], BF16))
        krel = ec(nc.sbuf_tensor(C.u("f0_krel"), [128, 2 * 512], BF16))
        rtm = ec(nc.sbuf_tensor(C.u("f0_rtm"), [128, D], F32))
        lnp = ec(nc.sbuf_tensor(C.u("f0_lnp"), [128, 6 * NTILE], F32))
        htm2 = ec(nc.sbuf_tensor(C.u("f0_htm2"), [128, 2 * D], BF16))
        p_tr = ec(nc.psum_tensor(C.u("f0_p_tr"), [128, 2 * D], BF16))
        p_k = ec(nc.psum_tensor(C.u("f0_p_k"), [128, 2 * 512], F32))
        p_r = ec(nc.psum_tensor(C.u("f0_p_r"), [128, D], F32))
        p_kv = ec(nc.psum_tensor(C.u("f0_p_kv"), [128, D], F32))

        def y(e):
            C.wait(e, "LD2")
            for nm, dst in (("f0_wr_t", wrT), ("f0_wk_t", wkT),
                            ("f0_wv_t", wvT)):
                C.inc(e.dma_start(out=dst[:].rearrange("p (c d) -> p c d", d=D),
                                  in_=C.scr[nm].rearrange(
                    "(c p) d -> p c d", p=128)), "LD2", 16)
        C.y(y)
        t_wl = C.tok("LD2")

        def g(e):
            C.wait(e, "LD2")
            for c in range(8):
                C.inc(e.dma_start(out=mur[:, c:c + 1],
                                  in_=C.dram["f0_mur"][c * 128:(c + 1) * 128][:, None]),
                      "LD", 16)
                C.inc(e.dma_start(out=muk[:, c:c + 1],
                                  in_=C.dram["f0_muk"][c * 128:(c + 1) * 128][:, None]),
                      "LD", 16)
            for c in range(8):
                ins = e.memset(hext[:, c * 513: c * 513 + 1], 0.0)
            C.inc(ins, "GP", 1)
        C.g(g)
        t_mu = C.tok("LD")
        t_hx0 = C.tok("GP")

        if getattr(C, "f1_prep_pending", False):
            C.f1_prep_pending = False
            emit_weight_prep(C, SCR, [("f1_wg", HID, D), ("f1_wu", HID, D),
                                      ("f1_wd", D, HID)])

        def v0(e):
            C.wait(e, "LD", t_mu)
            C.fence(e, e.tensor_scalar(murc[:], mur[:], -1.0, 1.0,
                                       op0=ALU.mult, op1=ALU.add), "DV")
            C.inc(e.tensor_scalar(mukc[:], muk[:], -1.0, 1.0, op0=ALU.mult,
                                  op1=ALU.add), "DV", 1)
        C.v(v0)
        t_muc = C.tok("DV")

        emit_pre_stats(C, lnp)
        ap6 = lnp[:].rearrange("p (i c) -> p i c", c=6)
        t_stats = C.tok("DV")

        tr_tok = [0] * NTILE
        mix_tok = [0] * NGRP
        carry_tok = [0] * NGRP
        kact_tok = {}
        kev_sc = {}
        o_tok = [0] * NTILE
        rr_tok = [0] * NTILE
        kvp_tok = [0] * NTILE
        for grp in range(NGRP):
            if grp > 0:
                def v(e, grp=grp):
                    C.wtok(e, "DV", mix_tok[grp - 1])
                    for c in range(8):
                        ins = e.tensor_copy(hext[:, c * 513: c * 513 + 1],
                                            hext[:, c * 513 + 512:
                                                 c * 513 + 513])
                    C.inc(ins, "DV", 1)
                C.v(v)
                carry_tok[grp] = C.tok("DV")

            hv_sc, hv_dv = C.cnt["SC"], C.cnt["DV"]

            def s(e, grp=grp, hv_dv=hv_dv,
                  pv=tuple(tr_tok[max(0, grp * 4 - 2): grp * 4])):
                C.wait(e, "DV", t_stats)
                for ti in range(4):
                    i = grp * 4 + ti
                    if ti >= 2:
                        C.wait(e, "DV", hv_dv + ti - 1)
                    elif i >= 2:
                        C.wait(e, "DV", pv[ti])
                    C.inc(e.activation(htm2[:, (i % 2) * D:(i % 2 + 1) * D],
                                       xs[:, i * D:(i + 1) * D], AF.Identity,
                                       scale=ap6[:, i, 4:5],
                                       bias=ap6[:, i, 5:6]), "SC", 1)
            C.s(s)
            for ti in range(4):
                i = grp * 4 + ti
                ap_t = hv_sc + ti + 1

                def t(e, i=i, ap_t=ap_t, grp=grp, ti=ti):
                    C.wtok(e, "SC", ap_t)
                    if i >= 2:
                        C.wtok(e, "DV", tr_tok[i - 2])
                    for c in range(8):
                        ins = e.transpose(
                            p_tr[:, (i % 2) * D + c * 128:
                                 (i % 2) * D + (c + 1) * 128],
                            htm2[:, (i % 2) * D + c * 128:
                                 (i % 2) * D + (c + 1) * 128],
                            k["ident"][:])
                    C.inc(ins, "PE", 1)
                C.t(t)
                pe_t = C.tok("PE")

                def v(e, i=i, ti=ti, grp=grp, pe_t=pe_t):
                    C.wtok(e, "PE", pe_t)
                    if ti == 0 and grp > 0:
                        C.wtok(e, "DV", carry_tok[grp])
                    if grp == 0 and ti == 0:
                        C.wait(e, "GP", t_hx0)
                    for c in range(8):
                        ins = e.tensor_copy(
                            hext[:, c * 513 + 1 + ti * 128:
                                 c * 513 + 1 + (ti + 1) * 128],
                            p_tr[:, (i % 2) * D + c * 128:
                                 (i % 2) * D + (c + 1) * 128])
                    C.inc(ins, "DV", 1)
                C.v(v)
                tr_tok[i] = C.tok("DV")

            def v(e, grp=grp):
                C.wtok(e, "DV", tr_tok[grp * 4 + 3])
                C.wait(e, "DV", t_muc)
                if grp > 0:
                    C.wtok(e, "PE", kvp_tok[grp * 4 - 1])
                for c in range(8):
                    h1 = hext[:, c * 513 + 1: c * 513 + 513]
                    h1p = hext[:, c * 513: c * 513 + 512]
                    C.fence(e, e.tensor_scalar(tmp_bf[:], h1p, murc[:, c:c + 1],
                                               None, op0=ALU.mult), "DV")
                    C.fence(e, e.scalar_tensor_tensor(
                        xr[:, c * 512:(c + 1) * 512], h1, mur[:, c:c + 1],
                        tmp_bf[:], op0=ALU.mult, op1=ALU.add), "DV")
                    C.fence(e, e.tensor_scalar(tmp_bf[:], h1p, mukc[:, c:c + 1],
                                               None, op0=ALU.mult), "DV")
                    ins = e.scalar_tensor_tensor(xk[:, c * 512:(c + 1) * 512],
                                                 h1, muk[:, c:c + 1], tmp_bf[:],
                                                 op0=ALU.mult, op1=ALU.add)
                    if c < 7:
                        C.fence(e, ins, "DV")
                C.inc(ins, "DV", 1)
            C.v(v)
            mix_tok[grp] = C.tok("DV")

            ka_pe, ka_sc = C.cnt["PE"], C.cnt["SC"]

            def s(e, ka_pe=ka_pe):
                for oc in range(8):
                    pb = oc % 2
                    C.wait(e, "PE", ka_pe + oc + 1)
                    C.inc(e.activation(krel[:, pb * 512:(pb + 1) * 512],
                                       p_k[:, pb * 512:(pb + 1) * 512],
                                       AF.Relu), "SC", 1)
            C.s(s)
            for oc in range(8):
                pb = oc % 2

                def t(e, oc=oc, grp=grp, pb=pb):
                    if oc == 0:
                        C.wtok(e, "DV", mix_tok[grp])
                        C.wait(e, "LD2", t_wl)
                    prev = kact_tok.get((grp, oc - 2)) or (
                        kact_tok.get((grp - 1, 6 + oc))
                        if oc < 2 and grp > 0 else None)
                    if prev:
                        C.wtok(e, "DV", prev)
                    for dc in range(8):
                        ins = e.matmul(
                            p_k[:, pb * 512:(pb + 1) * 512],
                            wkT[:, dc * D + oc * 128: dc * D + (oc + 1) * 128],
                            xk[:, dc * 512:(dc + 1) * 512],
                            start=(dc == 0), stop=(dc == 7))
                    C.inc(ins, "PE", 1)
                C.t(t)
                assert C.cnt["PE"] == ka_pe + oc + 1

                def v(e, oc=oc, pb=pb, sc_t=ka_sc + oc + 1):
                    C.wait(e, "SC", sc_t)
                    C.inc(e.tensor_tensor(kact[:, oc * 512:(oc + 1) * 512],
                                          krel[:, pb * 512:(pb + 1) * 512],
                                          krel[:, pb * 512:(pb + 1) * 512],
                                          op=ALU.mult), "DV", 1)
                C.v(v)
                kact_tok[(grp, oc)] = C.tok("DV")

            for ti in range(4):
                i = grp * 4 + ti

                def t(e, ti=ti, grp=grp, i=i):
                    if ti == 0:
                        C.wtok(e, "DV", mix_tok[grp])
                    if i > 0:
                        C.wtok(e, "SC", rr_tok[i - 1])
                    for dc in range(8):
                        for half in range(2):
                            ins = e.matmul(
                                p_r[:, half * 512:(half + 1) * 512],
                                xr[:, dc * 512 + ti * 128:
                                   dc * 512 + (ti + 1) * 128],
                                wrT[:, dc * D + half * 512:
                                    dc * D + (half + 1) * 512],
                                start=(dc == 0), stop=(dc == 7))
                    C.inc(ins, "PE", 1)
                C.t(t)
                pe_r = C.tok("PE")

                def s(e, pe_r=pe_r, i=i):
                    C.wtok(e, "PE", pe_r)
                    if i > 0:
                        C.wtok(e, "DV", o_tok[i - 1])
                    C.inc(e.activation(rtm[:], p_r[:], AF.Sigmoid), "SC", 1)
                C.s(s)
                rr_tok[i] = C.tok("SC")

                def t2(e, ti=ti, grp=grp, i=i):
                    C.wtok(e, "DV", kact_tok[(grp, 7)])
                    if i > 0:
                        C.wtok(e, "DV", o_tok[i - 1])
                    for fc in range(8):
                        for half in range(2):
                            ins = e.matmul(
                                p_kv[:, half * 512:(half + 1) * 512],
                                kact[:, fc * 512 + ti * 128:
                                     fc * 512 + (ti + 1) * 128],
                                wvT[:, fc * D + half * 512:
                                    fc * D + (half + 1) * 512],
                                start=(fc == 0), stop=(fc == 7))
                    C.inc(ins, "PE", 1)
                C.t(t2)
                kvp_tok[i] = C.tok("PE")

                def v(e, i=i):
                    C.wtok(e, "PE", kvp_tok[i])
                    C.wtok(e, "SC", rr_tok[i])
                    C.wait(e, "DV")
                    C.inc(e.tensor_tensor(k["scr_a"][:], p_kv[:], rtm[:],
                                          op=ALU.mult), "DV", 1)
                C.v(v)
                o_tok[i] = C.tok("DV")

                emit_ln_post(C, k["scr_a"][:], xs[:, i * D:(i + 1) * D])


def emit_l1a(C):
    nc = C.nc
    k = C.k
    xs = C.xs
    qscr = C.scr["qscr"]
    with ExitStack() as ph:
        ec = ph.enter_context
        kp = ec(nc.sbuf_tensor(C.u("l1_k"), [128, DC * S], BF16))
        vst = ec(nc.sbuf_tensor(C.u("l1_v"), [128, NTILE * H * 65], BF16))
        h2f = ec(nc.sbuf_tensor(C.u("l1_h2f"), [128, DC * 512], BF16))
        lnp = ec(nc.sbuf_tensor(C.u("l1_lnp"), [128, 6 * NTILE], F32))
        htm2 = ec(nc.sbuf_tensor(C.u("l1_htm2"), [128, 2 * D], BF16))
        sqs = ec(nc.sbuf_tensor(C.u("l1_sq"), [128, 512], BF16))
        nrm = ec(nc.sbuf_tensor(C.u("l1_nrm"), [1, 80], F32))
        negm = ec(nc.sbuf_tensor(C.u("l1_negm"), [128, 1], F32))
        mrow = ec(nc.sbuf_tensor(C.u("l1_mrow"), [1, 1], BF16))
        ap6 = None

        emit_pre_stats(C, lnp)
        ap6 = lnp[:].rearrange("p (i c) -> p i c", c=6)
        t_stats = C.tok("DV")

        with ExitStack() as pa:
            eca = pa.enter_context
            p_tr = eca(nc.psum_tensor(C.u("l1_p_tr"), [128, 2 * D], BF16))
            p_mm = eca(nc.psum_tensor(C.u("l1_p_mm"), [128, 4 * 512], F32))
            p_v = eca(nc.psum_tensor(C.u("l1_p_v"), [128, D], F32))
            wq = eca(nc.sbuf_tensor(C.u("l1_wq_s"), [128, DC * D], BF16))
            wk = eca(nc.sbuf_tensor(C.u("l1_wk_s"), [128, DC * D], BF16))
            wv = eca(nc.sbuf_tensor(C.u("l1_wv_s"), [128, DC * D], BF16))
            qgb = eca(nc.sbuf_tensor(C.u("l1_qgb"), [128, DC * 512], BF16))

            def g(e):
                C.wait(e, "LD2")
                for nm, dst in [("l1_wq", wq), ("l1_wk", wk), ("l1_wv", wv)]:
                    C.inc(e.dma_start(
                        out=dst[:].rearrange("p (c d) -> p c d", d=D),
                        in_=C.dram[nm].rearrange("(c p) d -> p c d", p=128)),
                        "LD", 16)
                C.inc(e.memset(vst[:], 1.0), "GP", 1)
            C.g(g)
            t_wload = C.tok("LD")
            t_vmem = C.tok("GP")

            tr_tok = [0] * NTILE
            qk_tok = {}
            vev_tok = [0] * NTILE
            qship_tok = [0] * NGRP
            nq_tok = {}
            for grp in range(NGRP):
                hv_sc, hv_dv = C.cnt["SC"], C.cnt["DV"]

                def s(e, grp=grp, hv_dv=hv_dv,
                      pv=tuple(tr_tok[max(0, grp * 4 - 2): grp * 4])):
                    C.wait(e, "DV", t_stats)
                    for ti in range(4):
                        i = grp * 4 + ti
                        if ti >= 2:
                            C.wait(e, "DV", hv_dv + ti - 1)
                        elif i >= 2:
                            C.wait(e, "DV", pv[ti])
                        C.inc(e.activation(
                            htm2[:, (i % 2) * D:(i % 2 + 1) * D],
                            xs[:, i * D:(i + 1) * D],
                            AF.Identity, scale=ap6[:, i, 4:5],
                            bias=ap6[:, i, 5:6]), "SC", 1)
                C.s(s)
                for ti in range(4):
                    i = grp * 4 + ti
                    ap_t = hv_sc + ti + 1

                    def t(e, i=i, ti=ti, ap_t=ap_t):
                        C.wtok(e, "SC", ap_t)
                        if i >= 2:
                            C.wtok(e, "DV", tr_tok[i - 2])
                        for c in range(8):
                            ins = e.transpose(
                                p_tr[:, (i % 2) * D + c * 128:
                                     (i % 2) * D + (c + 1) * 128],
                                htm2[:, (i % 2) * D + c * 128:
                                     (i % 2) * D + (c + 1) * 128],
                                k["ident"][:])
                        C.inc(ins, "PE", 1)
                    C.t(t)
                    pe_t = C.tok("PE")

                    def v(e, i=i, ti=ti, pe_t=pe_t):
                        C.wtok(e, "PE", pe_t)
                        for c in range(8):
                            ins = e.tensor_copy(
                                h2f[:, c * 512 + ti * 128:
                                    c * 512 + (ti + 1) * 128],
                                p_tr[:, (i % 2) * D + c * 128:
                                     (i % 2) * D + (c + 1) * 128])
                        C.inc(ins, "DV", 1)
                    C.v(v)
                    tr_tok[i] = C.tok("DV")
                    assert tr_tok[i] == hv_dv + ti + 1

                for oc in range(8):
                    for wsb, isq in ((wq, True), (wk, False)):
                        qtr = (oc % 2) + (0 if isq else 2)

                        def t(e, oc=oc, grp=grp, qtr=qtr, wsb=wsb, isq=isq):
                            if oc == 0 and isq:
                                C.wtok(e, "DV", tr_tok[grp * 4 + 3])
                                C.wait(e, "LD", t_wload)
                                if grp > 0:
                                    C.wtok(e, "ST", qship_tok[grp - 1])
                                    C.wtok(e, "DV", nq_tok[(grp - 1, 7)][1])
                            prev = qk_tok.get((grp, oc - 2, isq)) or (
                                qk_tok.get((grp - 1, 6 + oc, isq))
                                if oc < 2 and grp > 0 else None)
                            if prev:
                                nm, v_ = prev
                                C.wtok(e, nm, v_)
                            for dc in range(8):
                                ins = e.matmul(
                                    p_mm[:, qtr * 512:(qtr + 1) * 512],
                                    wsb[:, dc * D + oc * 128:
                                        dc * D + (oc + 1) * 128],
                                    h2f[:, dc * 512:(dc + 1) * 512],
                                    start=(dc == 0), stop=(dc == 7))
                            C.inc(ins, "PE", 1)
                        C.t(t)
                        pe_t = C.tok("PE")

                        if isq:
                            def s(e, oc=oc, qtr=qtr, pe_t=pe_t):
                                C.wtok(e, "PE", pe_t)
                                C.inc(e.activation(
                                    qg_dst(qgb, oc), p_mm[:, qtr * 512:
                                                          (qtr + 1) * 512],
                                    AF.Identity), "SC", 1)
                            C.s(s)
                            qk_tok[(grp, oc, True)] = ("SC", C.tok("SC"))
                        else:
                            def v(e, oc=oc, grp=grp, qtr=qtr, pe_t=pe_t):
                                C.wtok(e, "PE", pe_t)
                                C.inc(e.tensor_scalar(
                                    kp[:, oc * S + grp * 512:
                                       oc * S + (grp + 1) * 512],
                                    p_mm[:, qtr * 512:(qtr + 1) * 512],
                                    0.125, None, op0=ALU.mult), "DV", 1)
                            C.v(v)
                            qk_tok[(grp, oc, False)] = ("DV", C.tok("DV"))

                for ti in range(4):
                    i = grp * 4 + ti

                    def t(e, i=i, ti=ti, grp=grp):
                        if ti == 0:
                            nm, v_ = qk_tok[(grp, 7, False)]
                            C.wtok(e, nm, v_)
                            nm, v_ = qk_tok[(grp, 7, True)]
                            C.wtok(e, nm, v_)
                        if i >= 1:
                            C.wtok(e, "SC", vev_tok[i - 1])
                        for dc in range(8):
                            for hh in range(2):
                                ins = e.matmul(
                                    p_v[:, hh * 512:(hh + 1) * 512],
                                    h2f[:, dc * 512 + ti * 128:
                                        dc * 512 + (ti + 1) * 128],
                                    wv[:, dc * D + hh * 512:
                                        dc * D + (hh + 1) * 512],
                                    start=(dc == 0), stop=(dc == 7))
                        C.inc(ins, "PE", 1)
                    C.t(t)
                    pe_t = C.tok("PE")

                    def s(e, i=i, pe_t=pe_t):
                        C.wtok(e, "PE", pe_t)
                        C.wait(e, "GP", t_vmem)
                        dst = vst[:, i * H * 65:(i + 1) * H * 65].rearrange(
                            "p (h e) -> p h e", e=65)[:, :, 0:64]
                        srcp = p_v[:].rearrange("p (h e) -> p h e", e=64)
                        C.inc(e.activation(dst, srcp, AF.Identity), "SC", 1)
                    C.s(s)
                    vev_tok[i] = C.tok("SC")

                qn_pe, qn_sc, qn_dv = C.cnt["PE"], C.cnt["SC"], C.cnt["DV"]

                def s(e, qn_pe=qn_pe):
                    for pc in range(8):
                        C.wait(e, "PE", qn_pe + pc)
                        C.inc(e.activation(
                            sqs[:],
                            qgb[:, pc * 512:(pc + 1) * 512],
                            AF.Square), "SC", 1)
                C.s(s)
                assert C.cnt["SC"] == qn_sc + 8

                for pc in range(8):
                    def t(e, pc=pc, qn_sc=qn_sc, qn_dv=qn_dv):
                        C.wait(e, "SC", qn_sc + pc + 1)
                        C.wait(e, "DV", qn_dv + pc)
                        C.inc(e.matmul(p_mm[0:1, 0:512], k["ones_col"][:],
                                       sqs[:],
                                       start=True, stop=True), "PE", 1)
                    C.t(t)

                    def v(e, pc=pc, grp=grp, qn_pe=qn_pe):
                        C.wait(e, "PE", qn_pe + pc + 1)
                        C.inc(e.tensor_reduce(
                            nrm[:, pc * 4 + grp: pc * 4 + grp + 1],
                            p_mm[0:1, 0:512], axis=AX.X, op=ALU.max), "DV", 1)
                    C.v(v)
                    nq_tok[(grp, pc)] = (qn_pe + pc + 1, C.tok("DV"))
                assert C.cnt["PE"] == qn_pe + 8

                def y(e, grp=grp):
                    C.wtok(e, "SC", qk_tok[(grp, 7, True)][1])
                    C.inc(e.dma_start(out=qscr[grp], in_=qgb[:]), "ST", 16)
                C.y(y)
                qship_tok[grp] = C.tok("ST")

            kn_pe, kn_sc, kn_dv = C.cnt["PE"], C.cnt["SC"], C.cnt["DV"]

            def s(e, kn_pe=kn_pe):
                for j in range(32):
                    C.wait(e, "PE", kn_pe + j)
                    pc, g4 = divmod(j, 4)
                    C.inc(e.activation(
                        sqs[:],
                        kp[:, pc * S + g4 * 512: pc * S + (g4 + 1) * 512],
                        AF.Square), "SC", 1)
            C.s(s)
            assert C.cnt["SC"] == kn_sc + 32

            for j in range(32):
                pc, g4 = divmod(j, 4)

                def t(e, j=j, kn_sc=kn_sc, kn_dv=kn_dv):
                    C.wait(e, "SC", kn_sc + j + 1)
                    C.wait(e, "DV", kn_dv + j)
                    C.inc(e.matmul(p_mm[0:1, 0:512], k["ones_col"][:],
                                   sqs[:],
                                   start=True, stop=True), "PE", 1)
                C.t(t)

                def v(e, j=j, pc=pc, g4=g4, kn_pe=kn_pe):
                    C.wait(e, "PE", kn_pe + j + 1)
                    C.inc(e.tensor_reduce(
                        nrm[:, 40 + pc * 4 + g4: 40 + pc * 4 + g4 + 1],
                        p_mm[0:1, 0:512], axis=AX.X, op=ALU.max), "DV", 1)
                C.v(v)
            assert C.cnt["PE"] == kn_pe + 32

            def v(e):
                C.wait(e, "DV")
                e.tensor_reduce(nrm[:, 32:33], nrm[:, 0:32], axis=AX.X,
                                op=ALU.max)
                C.fence(e, e.tensor_reduce(nrm[:, 72:73], nrm[:, 40:72],
                                           axis=AX.X, op=ALU.max), "DV")
                C.inc(e.tensor_tensor(nrm[:, 33:34], nrm[:, 32:33],
                                      nrm[:, 72:73], op=ALU.mult), "DV", 1)
            C.v(v)

            def s(e):
                C.wait(e, "DV")
                C.fence(e, e.activation(nrm[:, 34:35], nrm[:, 33:34],
                                        AF.Ln), "SC")
                C.inc(e.activation(mrow[:], nrm[:, 34:35], AF.Exp, scale=0.5),
                      "SC", 1)
            C.s(s)
            sq_t = C.tok("SC")

            def t(e, sq_t=sq_t):
                C.wtok(e, "SC", sq_t)
                C.inc(e.matmul(p_mm[:, 0:1], k["ones_row"][:], mrow[:],
                               start=True, stop=True), "PE", 1)
            C.t(t)
            mm_t = C.tok("PE")

            def v(e, mm_t=mm_t):
                C.wtok(e, "PE", mm_t)
                C.inc(e.tensor_scalar(negm[:], p_mm[:, 0:1], -1.0, None,
                                      op0=ALU.mult), "DV", 1)
            C.v(v)
        t_negm = C.tok("DV")

        with ExitStack() as pb:
            ecb = pb.enter_context
            wp = ecb(nc.sbuf_tensor(C.u("l1_wp_s"), [128, DC * D], BF16))
            qgrp = ecb(nc.sbuf_tensor(C.u("l1_qgrp"), [128, DC * 512], BF16))
            pt = ecb(nc.sbuf_tensor(C.u("l1_pt"), [128, 2 * 1024], BF16))
            au = ecb(nc.sbuf_tensor(C.u("l1_au"), [65, 2 * 512], BF16))
            an = ecb(nc.sbuf_tensor(C.u("l1_an"), [128, 4 * D], F32))
            ag = ecb(nc.sbuf_tensor(C.u("l1_ag"), [128, 2 * D], BF16))
            af = ecb(nc.sbuf_tensor(C.u("l1_af"), [128, 2 * D], BF16))
            p_sp = ecb(nc.psum_tensor(C.u("l1_p_s"), [128, 2 * 1024], F32))
            p_o = ecb(nc.psum_tensor(C.u("l1_p_o"), [65, 2 * 512], F32))
            p_at = ecb(nc.psum_tensor(C.u("l1_p_at"), [128, 2 * 1024], BF16))
            p_w = p_at[:].bitcast(F32)[:, 0:512]

            def g(e):
                C.wait(e, "PE")
                C.wait(e, "SC")
                C.wait(e, "DV")
                C.wait(e, "ST")
                C.inc(e.dma_start(
                    out=wp[:].rearrange("p (c d) -> p c d", d=D),
                    in_=C.dram["l1_wp"].rearrange("(c p) d -> p c d", p=128)),
                    "LD", 16)
            C.g(g)
            t_wp = C.tok("LD")

            qld_tok = [0] * NGRP
            exp_tok = {}
            pv_tok = {}
            auev_tok = {}
            anrm_tok = {}
            agl_tok = [0] * 4
            aft_tok = [0] * 4
            wpo_tok = [0] * NTILE
            for grp in range(NGRP):
                def y(e, grp=grp):
                    C.wait(e, "PE")
                    if grp == 0:
                        C.wait(e, "SC")
                        C.wait(e, "DV")
                        C.wait(e, "ST")
                    C.inc(e.dma_start(out=qgrp[:], in_=qscr[grp]), "LD2", 16)
                C.y(y)
                qld_tok[grp] = C.tok("LD2")

                for pc in range(8):
                    sc_base = C.cnt["SC"]
                    for tc in range(NTILE):
                        exp_tok[(grp, pc, tc)] = sc_base + tc + 1
                    st_pe_l = [0] * NTILE

                    def mk_st(tc, pc=pc, grp=grp):
                        buf = tc % 2

                        def t(e, grp=grp, pc=pc, tc=tc, buf=buf):
                            if pc == 0 and tc == 0:
                                C.wait(e, "DV", t_negm)
                                C.wait(e, "LD2", qld_tok[grp])
                            prev = exp_tok.get((grp, pc, tc - 2)) or (
                                exp_tok.get((grp, pc - 1, NTILE - 2))
                                if tc < 2 else None)
                            if prev:
                                C.wtok(e, "SC", prev)
                            e.matmul(
                                p_sp[:, buf * 1024: buf * 1024 + 512],
                                kp[0:64, pc * S + tc * 128:
                                   pc * S + (tc + 1) * 128],
                                qgrp[0:64, pc * 512:(pc + 1) * 512],
                                start=True, stop=True, tile_position=(0, 0))
                            C.inc(e.matmul(
                                p_sp[:, buf * 1024 + 512: buf * 1024 + 1024],
                                kp[64:128, pc * S + tc * 128:
                                   pc * S + (tc + 1) * 128],
                                qgrp[64:128, pc * 512:(pc + 1) * 512],
                                start=True, stop=True,
                                tile_position=(64, 0)), "PE", 1)
                        C.t(t)
                        st_pe_l[tc] = C.tok("PE")

                    def mk_pv(tc, pc=pc, grp=grp):
                        buf = tc % 2

                        def t2(e, pc=pc, tc=tc, buf=buf, grp=grp):
                            C.wtok(e, "SC", exp_tok[(grp, pc, tc)])
                            if tc == 0:
                                prev = auev_tok.get((grp, pc - 1)) or (
                                    auev_tok.get((grp - 1, 7)) if grp > 0
                                    else None)
                                if prev:
                                    C.wtok(e, prev[0], prev[1])
                            e.matmul(p_o[0:65, 0:512],
                                     vst[:, tc * H * 65 + (2 * pc) * 65:
                                         tc * H * 65 + (2 * pc) * 65 + 65],
                                     pt[:, buf * 1024: buf * 1024 + 512],
                                     start=(tc == 0), stop=(tc == NTILE - 1),
                                     skip_group_check=True)
                            C.inc(e.matmul(
                                p_o[0:65, 512:1024],
                                vst[:, tc * H * 65 + (2 * pc + 1) * 65:
                                    tc * H * 65 + (2 * pc + 1) * 65 + 65],
                                pt[:, buf * 1024 + 512: buf * 1024 + 1024],
                                start=(tc == 0), stop=(tc == NTILE - 1),
                                skip_group_check=True), "PE", 1)
                        C.t(t2)

                    mk_st(0)
                    for tc in range(1, NTILE):
                        mk_st(tc)
                        mk_pv(tc - 1)
                    mk_pv(NTILE - 1)
                    pv_tok[(grp, pc)] = C.tok("PE")

                    def s(e, st_pe_l=tuple(st_pe_l), sc_base=sc_base):
                        for tc in range(NTILE):
                            buf = tc % 2
                            C.wtok(e, "PE", st_pe_l[tc])
                            ins = e.activation(pt[:, buf * 1024:
                                                  (buf + 1) * 1024],
                                               p_sp[:, buf * 1024:
                                                    (buf + 1) * 1024],
                                               AF.Exp, bias=negm[:, 0:1])
                            C.inc(ins, "SC", 1)
                            assert C.cnt["SC"] == sc_base + tc + 1
                    C.s(s)

                    def vv(e, pc=pc, grp=grp):
                        C.wtok(e, "PE", pv_tok[(grp, pc)])
                        prev = anrm_tok.get((grp, pc - 1)) or (
                            anrm_tok.get((grp - 1, 7)) if grp > 0 else None)
                        if prev:
                            C.wtok(e, "DV", prev)
                        e.tensor_copy(au[0:65, 0:512], p_o[0:65, 0:512])
                        C.inc(e.tensor_copy(au[0:65, 512:1024],
                                            p_o[0:65, 512:1024]), "DV", 1)
                    C.v(vv)
                    auev_tok[(grp, pc)] = ("DV", C.tok("DV"))

                    for hh_i in range(2):
                        h = 2 * pc + hh_i
                        for sub in range(4):
                            ab = (hh_i * 4 + sub) % 2

                            def t(e, hh_i=hh_i, sub=sub, ab=ab, grp=grp,
                                  pc=pc):
                                nm_, v_ = auev_tok[(grp, pc)]
                                C.wtok(e, nm_, v_)
                                prev = anrm_tok.get((grp, pc, hh_i, sub - 2)) \
                                    or (anrm_tok.get((grp, pc, hh_i - 1,
                                                      sub + 2))
                                        if sub < 2 else None)
                                if prev:
                                    C.wtok(e, "DV", prev)
                                else:
                                    C.wait(e, "DV")
                                C.inc(e.transpose(
                                    p_at[:, ab * 1024: ab * 1024 + 65],
                                    au[0:65, hh_i * 512 + sub * 128:
                                       hh_i * 512 + (sub + 1) * 128],
                                    k["ident65"][:]), "PE", 1)
                            C.t(t)
                            at_t = C.tok("PE")

                            def v(e, sub=sub, h=h, ab=ab, at_t=at_t,
                                  hh_i=hh_i, grp=grp, pc=pc):
                                pr = anrm_tok.get((grp, pc, hh_i, sub - 1)) \
                                    or (anrm_tok.get((grp, pc, hh_i - 1, 3))
                                        if sub == 0 else None)
                                if pr:
                                    C.wtok(e, "DV", pr)
                                C.wtok(e, "PE", at_t)
                                C.fence(e, e.reciprocal(
                                    k["st8"][:, 7:8],
                                    p_at[:, ab * 1024 + 64: ab * 1024 + 65]),
                                    "DV")
                                C.inc(e.tensor_scalar(
                                    an[:, sub * D + h * E:
                                       sub * D + (h + 1) * E],
                                    p_at[:, ab * 1024: ab * 1024 + 64],
                                    k["st8"][:, 7:8], None, op0=ALU.mult),
                                    "DV", 1)
                            C.v(v)
                            anrm_tok[(grp, pc, hh_i, sub)] = C.tok("DV")
                        anrm_tok[(grp, pc)] = C.tok("DV")

                for sub in range(4):
                    i = grp * 4 + sub
                    sb = sub % 2

                    def s(e, sub=sub, sb=sb, grp=grp):
                        C.wtok(e, "DV", anrm_tok[(grp, 7)])
                        if sub >= 2:
                            C.wtok(e, "DV", aft_tok[sub - 2])
                        C.inc(e.activation(ag[:, sb * D:(sb + 1) * D],
                                           an[:, sub * D:(sub + 1) * D],
                                           AF.Gelu), "SC", 1)
                    C.s(s)
                    agl_tok[sub] = C.tok("SC")

                    def t(e, sub=sub, sb=sb):
                        C.wtok(e, "SC", agl_tok[sub])
                        if sub >= 2:
                            C.wtok(e, "DV", aft_tok[sub - 2])
                        for c in range(8):
                            ins = e.transpose(
                                p_at_tr(p_sp, sb, c),
                                ag[:, sb * D + c * 128: sb * D + (c + 1) * 128],
                                k["ident"][:])
                        C.inc(ins, "PE", 1)
                    C.t(t)
                    trf_t = C.tok("PE")

                    def v(e, sub=sub, sb=sb, trf_t=trf_t):
                        C.wtok(e, "PE", trf_t)
                        for c in range(8):
                            ins = e.tensor_copy(
                                af[:, sb * D + c * 128: sb * D + (c + 1) * 128],
                                p_at_tr(p_sp, sb, c))
                        C.inc(ins, "DV", 1)
                    C.v(v)
                    aft_tok[sub] = C.tok("DV")

                    def s2(e, i=i):
                        C.wait(e, "SC")
                        C.wait(e, "DV")
                        C.inc(e.activation(k["scr_b"][:],
                                           xs[:, i * D:(i + 1) * D],
                                           AF.Identity,
                                           scale=ap6[:, i, 4:5],
                                           bias=ap6[:, i, 5:6]), "SC", 1)
                    C.s(s2)
                    h2r_t = C.tok("SC")

                    for hh in range(2):
                        def t2(e, sub=sub, sb=sb, i=i, hh=hh):
                            C.wtok(e, "DV", aft_tok[sub])
                            C.wait(e, "LD", t_wp)
                            if i > 0 or hh > 0:
                                C.wait(e, "DV")
                            for fc in range(8):
                                ins = e.matmul(
                                    p_w,
                                    af[:, sb * D + fc * 128:
                                       sb * D + (fc + 1) * 128],
                                    wp[:, fc * D + hh * 512:
                                       fc * D + (hh + 1) * 512],
                                    start=(fc == 0), stop=(fc == 7))
                            C.inc(ins, "PE", 1)
                        C.t(t2)
                        wp_pe = C.tok("PE")

                        def v2(e, hh=hh, h2r_t=h2r_t, wp_pe=wp_pe):
                            C.wtok(e, "PE", wp_pe)
                            C.wtok(e, "SC", h2r_t)
                            C.inc(e.tensor_tensor(
                                k["scr_a"][:, hh * 512:(hh + 1) * 512], p_w,
                                k["scr_b"][:, hh * 512:(hh + 1) * 512],
                                op=ALU.add), "DV", 1)
                        C.v(v2)

                    emit_ln_post(C, k["scr_a"][:], xs[:, i * D:(i + 1) * D])
                    wpo_tok[i] = C.tok("SC")


def qg_dst(qgb, oc):
    return qgb[:, oc * 512:(oc + 1) * 512]


def p_at_tr(p_sp, sb, c):
    return p_sp[:].bitcast(BF16)[:, sb * 1024 + c * 128:
                                 sb * 1024 + (c + 1) * 128]


def emit_pre_stats(C, lnp):
    k = C.k
    xs = C.xs
    ap6 = lnp[:].rearrange("p (i c) -> p i c", c=6)

    def v(e):
        C.wait(e, "LD2")
        C.wait(e, "DV")
        for i in range(NTILE):
            ins = e.tensor_reduce(lnp[:, 6 * i: 6 * i + 1],
                                  xs[:, i * D:(i + 1) * D], axis=AX.X,
                                  op=ALU.add)
        C.inc(ins, "DV", 1)
    C.v(v)
    tsum = C.tok("DV")

    def s(e, tsum=tsum):
        C.wait(e, "LD2")
        C.wait(e, "SC")
        C.wait(e, "DV", tsum)
        for i in range(NTILE):
            ins = e.activation(k["scr_a" if i % 2 else "scr_b"][:],
                               xs[:, i * D:(i + 1) * D],
                               AF.Square, accum_out=lnp[:, 6 * i + 1:
                                                        6 * i + 2])
            if i < NTILE - 1:
                C.fence(e, ins, "SC")
        C.inc(ins, "SC", 1)
    C.s(s)
    tsq = C.tok("SC")

    def v2(e, tsq=tsq):
        C.wait(e, "SC", tsq)
        C.fence(e, e.tensor_scalar(ap6[:, :, 2:3], ap6[:, :, 0:1], 1.0 / D,
                                   None, op0=ALU.mult), "DV")
        C.fence(e, e.tensor_scalar(ap6[:, :, 3:4], ap6[:, :, 1:2], 1.0 / D,
                                   None, op0=ALU.mult), "DV")
        C.fence(e, e.scalar_tensor_tensor(ap6[:, :, 4:5], ap6[:, :, 2:3],
                                          -1.0, ap6[:, :, 2:3], op0=ALU.mult,
                                          op1=ALU.mult), "DV")
        C.inc(e.tensor_tensor(ap6[:, :, 3:4], ap6[:, :, 3:4], ap6[:, :, 4:5],
                              op=ALU.add), "DV", 1)
    C.v(v2)
    t3 = C.tok("DV")

    def s2(e, t3=t3):
        C.wait(e, "DV", t3)
        C.fence(e, e.activation(ap6[:, :, 4:5], ap6[:, :, 3:4], AF.Ln,
                                bias=k["eps_v"][:, 0:1]), "SC")
        C.inc(e.activation(ap6[:, :, 4:5], ap6[:, :, 4:5], AF.Exp,
                           scale=-0.5), "SC", 1)
    C.s(s2)
    t4 = C.tok("SC")

    def v3(e, t4=t4):
        C.wait(e, "SC", t4)
        C.inc(e.scalar_tensor_tensor(ap6[:, :, 5:6], ap6[:, :, 2:3], -1.0,
                                     ap6[:, :, 4:5], op0=ALU.mult,
                                     op1=ALU.mult), "DV", 1)
    C.v(v3)


def emit_l1f(C, SCR):
    nc = C.nc
    k = C.k
    xs = C.xs
    if getattr(C, "f1_prep_pending", False):
        C.f1_prep_pending = False
        emit_weight_prep(C, SCR, [("f1_wg", HID, D), ("f1_wu", HID, D),
                                  ("f1_wd", D, HID)])
    HH2 = HID // 2
    with ExitStack() as ph:
        ec = ph.enter_context
        wgh = ec(nc.sbuf_tensor(C.u("f1_wgh"), [128, DC * HH2], BF16))
        wuh = ec(nc.sbuf_tensor(C.u("f1_wuh"), [128, DC * HH2], BF16))
        wdb = ec(nc.sbuf_tensor(C.u("f1_wdb"), [128, 2 * 4 * 512], BF16))
        h3f = ec(nc.sbuf_tensor(C.u("f1_h3f"), [128, DC * 512], BF16))
        hh = ec(nc.sbuf_tensor(C.u("f1_hh"), [128, HC * 512], BF16))
        sg = ec(nc.sbuf_tensor(C.u("f1_sg"), [128, 2 * 512], BF16))
        od = ec(nc.sbuf_tensor(C.u("f1_od"), [128, 4 * D], F32))
        lnp = ec(nc.sbuf_tensor(C.u("f1_lnp"), [128, 6 * NTILE], F32))
        htm2 = ec(nc.sbuf_tensor(C.u("f1_htm2"), [128, 2 * D], BF16))
        p_big = ec(nc.psum_tensor(C.u("f1_p_big"), [128, 4 * 512], F32))
        p_g = ec(nc.psum_tensor(C.u("f1_p_g"), [128, 2 * 512], F32))
        p_u = ec(nc.psum_tensor(C.u("f1_p_u"), [128, 2 * 512], F32))
        p_trv = p_big[:].bitcast(BF16)

        emit_pre_stats(C, lnp)
        ap6 = lnp[:].rearrange("p (i c) -> p i c", c=6)
        t_stats = C.tok("DV")

        tr_tok = [0] * NTILE
        dn_tok = {}
        odv_tok = {}
        lnq_tok = {}
        hhev_tok = {}
        wl_tok = {}
        gu_tok = {}
        for grp in range(NGRP):
            for ti in range(4):
                i = grp * 4 + ti

                def s(e, i=i):
                    C.wait(e, "DV", t_stats)
                    if i >= 2:
                        C.wtok(e, "DV", tr_tok[i - 2])
                    C.inc(e.activation(htm2[:, (i % 2) * D:(i % 2 + 1) * D],
                                       xs[:, i * D:(i + 1) * D], AF.Identity,
                                       scale=ap6[:, i, 4:5],
                                       bias=ap6[:, i, 5:6]), "SC", 1)
                C.s(s)
                ap_t = C.tok("SC")

                def t(e, i=i, ti=ti, ap_t=ap_t, grp=grp):
                    C.wtok(e, "SC", ap_t)
                    if i >= 2:
                        C.wtok(e, "DV", tr_tok[i - 2])
                    if grp > 0 and ti < 2:
                        C.wtok(e, "SC", odv_tok[(grp - 1, 3)])
                    for c in range(8):
                        ins = e.transpose(
                            p_trv[:, (i % 2) * 1024 + c * 128:
                                  (i % 2) * 1024 + (c + 1) * 128],
                            htm2[:, (i % 2) * D + c * 128:
                                 (i % 2) * D + (c + 1) * 128],
                            k["ident"][:])
                    C.inc(ins, "PE", 1)
                C.t(t)
                pe_t = C.tok("PE")

                def v(e, i=i, ti=ti, pe_t=pe_t):
                    C.wtok(e, "PE", pe_t)
                    for c in range(8):
                        ins = e.tensor_copy(
                            h3f[:, c * 512 + ti * 128: c * 512 + (ti + 1) * 128],
                            p_trv[:, (i % 2) * 1024 + c * 128:
                                  (i % 2) * 1024 + (c + 1) * 128])
                    C.inc(ins, "DV", 1)
                C.v(v)
                tr_tok[i] = C.tok("DV")

            for half in range(2):
                def y(e, grp=grp, half=half):
                    prev = gu_tok.get((grp, half * HHALF_G - 1)) if half else \
                        (gu_tok.get((grp - 1, HC - 1)) if grp > 0 else None)
                    if prev:
                        C.wtok(e, "PE", prev)
                    for dc in range(8):
                        C.inc(e.dma_start(
                            out=wgh[:, dc * HH2:(dc + 1) * HH2],
                            in_=C.scr["f1_wg_t"][dc * 128:(dc + 1) * 128,
                                                 half * HH2:(half + 1) * HH2]),
                            "WL", 16)
                        C.inc(e.dma_start(
                            out=wuh[:, dc * HH2:(dc + 1) * HH2],
                            in_=C.scr["f1_wu_t"][dc * 128:(dc + 1) * 128,
                                                 half * HH2:(half + 1) * HH2]),
                            "WL", 16)
                C.y(y)
                wl_tok[(grp, half)] = C.tok("WL")

                sc_base = C.cnt["SC"]
                sl_toks = {}
                gu_l = [0] * (HC // 2)
                for hc_l in range(HC // 2):
                    hc = half * (HC // 2) + hc_l
                    sl_toks[hc] = sc_base + hc_l + 1
                for hc_l in range(HC // 2):
                    hc = half * (HC // 2) + hc_l
                    pb = hc % 2

                    def t(e, grp=grp, half=half, hc_l=hc_l, hc=hc, pb=pb):
                        if hc_l == 0:
                            C.wait(e, "WL", wl_tok[(grp, half)])
                            C.wtok(e, "DV", tr_tok[grp * 4 + 3])
                        prev = hhev_tok.get((grp, hc - 2)) or (
                            hhev_tok.get((grp - 1, HC - 2 + hc))
                            if grp > 0 and hc < 2 else None)
                        if prev:
                            C.wtok(e, "DV", prev)
                        for dc in range(8):
                            e.matmul(p_g[:, pb * 512:(pb + 1) * 512],
                                     wgh[:, dc * HH2 + hc_l * 128:
                                         dc * HH2 + (hc_l + 1) * 128],
                                     h3f[:, dc * 512:(dc + 1) * 512],
                                     start=(dc == 0), stop=(dc == 7))
                        for dc in range(8):
                            ins = e.matmul(
                                p_u[:, pb * 512:(pb + 1) * 512],
                                wuh[:, dc * HH2 + hc_l * 128:
                                    dc * HH2 + (hc_l + 1) * 128],
                                h3f[:, dc * 512:(dc + 1) * 512],
                                start=(dc == 0), stop=(dc == 7))
                        C.inc(ins, "PE", 1)
                    C.t(t)
                    gu_tok[(grp, hc)] = C.tok("PE")
                    gu_l[hc_l] = C.tok("PE")

                    def v(e, hc=hc, pb=pb, sl_t=sl_toks[hc]):
                        C.wtok(e, "SC", sl_t)
                        C.inc(e.tensor_tensor(hh[:, hc * 512:(hc + 1) * 512],
                                              sg[:, pb * 512:(pb + 1) * 512],
                                              p_u[:, pb * 512:(pb + 1) * 512],
                                              op=ALU.mult), "DV", 1)
                    C.v(v)
                    hhev_tok[(grp, hc)] = C.tok("DV")

                def s(e, gu_l=tuple(gu_l), half=half, sc_base=sc_base):
                    for hc_l in range(HC // 2):
                        hc = half * (HC // 2) + hc_l
                        pb = hc % 2
                        C.wtok(e, "PE", gu_l[hc_l])
                        C.inc(e.activation(sg[:, pb * 512:(pb + 1) * 512],
                                           p_g[:, pb * 512:(pb + 1) * 512],
                                           AF.Silu, scale=1.0 / 127), "SC", 1)
                        assert C.cnt["SC"] == sc_base + hc_l + 1
                C.s(s)

            for dhalf in range(2):
                for hb in range(HC // 4):
                    buf = hb % 2

                    def y(e, grp=grp, dhalf=dhalf, hb=hb, buf=buf):
                        prev = dn_tok.get((grp, dhalf, hb - 2)) or (
                            dn_tok.get((grp, dhalf - 1, HC // 4 - 2 + hb))
                            if hb < 2 and dhalf > 0 else
                            (dn_tok.get((grp - 1, 1, HC // 4 - 2 + hb))
                             if hb < 2 and dhalf == 0 and grp > 0 else None))
                        if prev:
                            C.wtok(e, "PE", prev)
                        C.inc(e.dma_start(
                            out=wdb[:, buf * 2048:(buf + 1) * 2048].rearrange(
                                "p (c d) -> p c d", d=512),
                            in_=C.scr["f1_wd_t"][hb * 512:(hb + 1) * 512,
                                                 dhalf * 512:(dhalf + 1) * 512
                                                 ].rearrange(
                                "(c p) d -> p c d", p=128)),
                            "WD%d" % buf, 16)
                    C.y(y)
                    wd_t = C.tok("WD%d" % buf)

                    for hc_l in range(4):
                        hc = hb * 4 + hc_l

                        def t(e, grp=grp, dhalf=dhalf, hc=hc, hc_l=hc_l,
                              buf=buf, wd_t=wd_t):
                            if hc_l == 0:
                                C.wait(e, "WD%d" % buf, wd_t)
                            if hc == 0:
                                C.wtok(e, "DV", hhev_tok[(grp, HC - 1)])
                                if dhalf == 0:
                                    C.wtok(e, "DV", tr_tok[grp * 4 + 3])
                                else:
                                    C.wtok(e, "SC", odv_tok[(grp, 3, 0)])
                            for ti in range(4):
                                ins = e.matmul(
                                    p_big[:, ti * 512:(ti + 1) * 512],
                                    hh[:, hc * 512 + ti * 128:
                                       hc * 512 + (ti + 1) * 128],
                                    wdb[:, buf * 2048 + hc_l * 512:
                                        buf * 2048 + (hc_l + 1) * 512],
                                    start=(hc == 0), stop=(hc == HC - 1),
                                    skip_group_check=True)
                            C.inc(ins, "PE", 1)
                        C.t(t)
                    dn_tok[(grp, dhalf, hb)] = C.tok("PE")

                for ti in range(4):
                    def s(e, grp=grp, ti=ti, dhalf=dhalf):
                        C.wtok(e, "PE", dn_tok[(grp, dhalf, HC // 4 - 1)])
                        if grp > 0 and dhalf == 0:
                            C.wtok(e, "DV", lnq_tok[(grp - 1, ti)])
                        C.inc(e.activation(
                            od[:, ti * D + dhalf * 512: ti * D + (dhalf + 1) * 512],
                            p_big[:, ti * 512:(ti + 1) * 512], AF.Identity,
                            scale=1.0 / (127.0 * 127.0)), "SC", 1)
                    C.s(s)
                    odv_tok[(grp, ti, dhalf)] = C.tok("SC")
                odv_tok[(grp, 3)] = C.tok("SC")

            for ti in range(4):
                i = grp * 4 + ti

                def dummy(e, grp=grp, ti=ti):
                    C.wtok(e, "SC", odv_tok[(grp, ti, 1)])
                    C.inc(e.engine_nop(), "DV", 1)
                C.v(dummy)
                emit_ln_post(C, od[:, ti * D:(ti + 1) * D],
                             xs[:, i * D:(i + 1) * D])
                lnq_tok[(grp, ti)] = C.tok("DV")


HHALF_G = HC // 2


def build_program(phases):
    nc = bass.Bass("TRN2", target_bir_lowering=False)

    xin = nc.dram_tensor("x", [NSEQ, S, D], F32, kind="ExternalInput").ap()
    out = nc.dram_tensor("out", [NSEQ, S, D], F32, kind="ExternalOutput").ap()
    dbg = nc.dram_tensor("dbg", [NSEQ, 4, 128, D], F32, kind="ExternalOutput").ap()
    W = {}
    for nm, shp, dt in [
        ("l0_wq", [D, D], F32), ("l0_wk", [D, D], F32), ("l0_wv", [D, D], F32),
        ("l0_bq", [D], F32), ("l0_bk", [D], F32), ("l0_bv", [D], F32),
        ("f0_wr", [D, D], F32), ("f0_wk", [D, D], F32), ("f0_wv", [D, D], F32),
        ("f0_mur", [D], F32), ("f0_muk", [D], F32),
        ("l1_wq", [D, D], F32), ("l1_wk", [D, D], F32), ("l1_wv", [D, D], F32),
        ("l1_wp", [D, D], F32),
        ("f1_wg", [HID, D], I32), ("f1_wu", [HID, D], I32),
        ("f1_wd", [D, HID], I32),
    ]:
        W[nm] = nc.dram_tensor(nm, shp, dt, kind="ExternalInput").ap()

    need_f0 = "l0f" in phases
    need_f1 = "l1f" in phases
    SCR = {}
    if "l1a" in phases:
        SCR["qscr"] = nc.dram_tensor("qscr", [NGRP, 128, DC * 512], BF16,
                                     kind="Internal").ap()
    if need_f0:
        for nm in ("f0_wr", "f0_wk", "f0_wv"):
            SCR[nm + "_s1"] = nc.dram_tensor("s1_" + nm, [D, D], BF16,
                                             kind="Internal").ap()
            SCR[nm + "_t"] = nc.dram_tensor("t_" + nm, [D, D], BF16,
                                            kind="Internal").ap()
    if need_f1:
        for nm, R, Cc in (("f1_wg", HID, D), ("f1_wu", HID, D),
                          ("f1_wd", D, HID)):
            SCR[nm + "_s1"] = nc.dram_tensor("s1_" + nm, [R, Cc], BF16,
                                             kind="Internal").ap()
            SCR[nm + "_t"] = nc.dram_tensor("t_" + nm, [Cc, R], BF16,
                                            kind="Internal").ap()

    with ExitStack() as stack:
        ec = stack.enter_context
        xs = ec(nc.sbuf_tensor("xs", [128, NTILE * D], F32))
        ident = ec(nc.sbuf_tensor("ident", [128, 128], BF16))
        ident65 = ec(nc.sbuf_tensor("ident65", [65, 65], BF16))
        ones_col = ec(nc.sbuf_tensor("ones_col", [128, 1], BF16))
        ones_row = ec(nc.sbuf_tensor("ones_row", [1, 128], BF16))
        cmask = ec(nc.sbuf_tensor("cmask", [128, 128], F32))
        cnt16 = ec(nc.sbuf_tensor("cnt16", [128, NTILE], F32))
        eps_v = ec(nc.sbuf_tensor("eps_v", [128, 1], F32))
        st8 = ec(nc.sbuf_tensor("st8", [128, 8], F32))
        scr_a = ec(nc.sbuf_tensor("scr_a", [128, D], F32))
        scr_b = ec(nc.sbuf_tensor("scr_b", [128, D], F32))

        block = ec(nc.Block())
        C = Ctx(nc, block, stack)
        C.dram = {"x": xin, "out": out, "dbg": dbg, **W}
        C.scr = SCR
        C.xs = xs
        C.k = dict(ident=ident, ident65=ident65, ones_col=ones_col,
                   ones_row=ones_row, cmask=cmask, cnt16=cnt16, eps_v=eps_v,
                   st8=st8, scr_a=scr_a, scr_b=scr_b)

        emit_consts(C)
        C.f0_prep_pending = need_f0
        C.f1_prep_pending = need_f1

        for seq in range(NSEQ):
            C.seq = seq
            emit_load_x(C, seq)
            C.barrier()
            if "l0a" in phases:
                emit_l0a(C)
                C.barrier()
            if "l0f" in phases:
                emit_l0f(C, SCR)
                C.barrier()
            if "l1a" in phases:
                emit_l1a(C)
                C.barrier()
            if "l1f" in phases:
                emit_l1f(C, SCR)
                C.barrier()
            emit_store_x(C, seq)
            C.barrier()
    return nc


def in_map_for_core(inputs, core):
    x = np.ascontiguousarray(inputs["L0_attn_X"][2 * core: 2 * core + 2])
    m = {"x": x}
    ren = {"l0_wq": "L0_attn_Wq", "l0_wk": "L0_attn_Wk", "l0_wv": "L0_attn_Wv",
           "l0_bq": "L0_attn_bq", "l0_bk": "L0_attn_bk", "l0_bv": "L0_attn_bv",
           "f0_wr": "L0_ffn_W_r", "f0_wk": "L0_ffn_W_k", "f0_wv": "L0_ffn_W_v",
           "f0_mur": "L0_ffn_mu_r", "f0_muk": "L0_ffn_mu_k",
           "l1_wq": "L1_attn_Wq", "l1_wk": "L1_attn_Wk", "l1_wv": "L1_attn_Wv",
           "l1_wp": "L1_attn_Wpost",
           "f1_wg": "L1_ffn_w_gate", "f1_wu": "L1_ffn_w_up",
           "f1_wd": "L1_ffn_w_down"}
    for a, b in ren.items():
        m[a] = np.ascontiguousarray(inputs[b])
    return m


def kernel(**inputs):
    nc = build_program(("l0a", "l0f", "l1a", "l1f"))
    in_maps = [in_map_for_core(inputs, c) for c in range(8)]
    res = run_bass_kernel_spmd(nc, in_maps, core_ids=list(range(8)))
    return np.concatenate([np.asarray(r["out"]) for r in res.results], axis=0)


# version 23
# speedup vs baseline: 1.1859x; 1.1859x over previous
import numpy as np
from contextlib import ExitStack

import concourse.bass as bass
import concourse.mybir as mybir
from concourse.bass_utils import run_bass_kernel_spmd

F32 = mybir.dt.float32
BF16 = mybir.dt.bfloat16
I32 = mybir.dt.int32
AF = mybir.ActivationFunctionType
ALU = mybir.AluOpType
AX = mybir.AxisListType

NTILE = 16
NGRP = 4
D = 1024
DC = 8
S = 2048
HID = 3072
HC = 24
NSEQ = 2
EPS = 1e-5
H, E = 16, 64

_SECT = {"tensor": "PE", "scalar": "SC", "vector": "DV", "gpsimd": "GP",
         "sync": "SY"}


class Ctx:
    def __init__(self, nc, block, stack):
        self.nc = nc
        self.block = block
        self.stack = stack
        self.sem = {}
        self.cnt = {}
        for name in ("LD", "LD2", "ST", "PE", "SC", "DV", "GP", "SY",
                     "WD0", "WD1", "WL", "WP", "WP2"):
            self.sem[name] = stack.enter_context(nc.semaphore(name))
            self.cnt[name] = 0
        self.bar_s = stack.enter_context(nc.sbuf_tensor("bar_s", [1, 8], F32))

    def t(self, f):
        self.block.tensor(f)

    def s(self, f):
        self.block.scalar(f)

    def v(self, f):
        self.block.vector(f)

    def g(self, f):
        self.block.gpsimd(f)

    def y(self, f):
        self.block.sync(f)

    def u(self, name):
        self._uid = getattr(self, "_uid", 0) + 1
        return f"{name}__{self._uid}"

    def inc(self, instr, name, n=1):
        instr.then_inc(self.sem[name], n)
        self.cnt[name] += n
        return self.cnt[name]

    def wait(self, eng, name, val=None):
        eng.wait_ge(self.sem[name], self.cnt[name] if val is None else val)

    def fence(self, e, instr, name):
        self.inc(instr, name, 1)
        e.wait_ge(self.sem[name], self.cnt[name])

    def tok(self, name):
        return self.cnt[name]

    def wtok(self, e, name, val):
        if val > 0:
            e.wait_ge(self.sem[name], val)

    def barrier(self):
        def mk_inc(engname):
            def f(e):
                if engname == "scalar":
                    ins = e.memzero(self.bar_s[:])
                elif engname in ("vector", "gpsimd"):
                    ins = e.engine_nop()
                else:
                    ins = e.nop()
                self.inc(ins, _SECT[engname], 1)
            return f
        for e in ("scalar", "vector", "gpsimd", "sync"):
            getattr(self, e[0] if e != "sync" else "y")(mk_inc(e))
        snap = dict(self.cnt)

        def mk_wait():
            def f(e):
                for nm, v in snap.items():
                    if v > 0:
                        e.wait_ge(self.sem[nm], v)
            return f
        for e in ("tensor", "scalar", "vector", "gpsimd", "sync"):
            getattr(self, e[0] if e != "sync" else "y")(mk_wait())


def emit_consts(C):
    k = C.k

    def g(e):
        C.fence(e, e.memset(k["ident"][:], 0.0), "GP")
        e.affine_select(out=k["ident"][:], in_=k["ident"][:],
                        compare_op=ALU.not_equal, fill=1.0,
                        base=0, pattern=[[-1, 128]], channel_multiplier=1)
        C.fence(e, e.memset(k["ident65"][:], 0.0), "GP")
        e.affine_select(out=k["ident65"][:], in_=k["ident65"][:],
                        compare_op=ALU.not_equal, fill=1.0,
                        base=0, pattern=[[-1, 65]], channel_multiplier=1)
        e.memset(k["ones_col"][:], 1.0)
        e.memset(k["ones_row"][:], 1.0)
        C.fence(e, e.memset(k["cmask"][:], 0.0), "GP")
        e.affine_select(out=k["cmask"][:], in_=k["cmask"][:],
                        compare_op=ALU.is_ge, fill=-1.0e9,
                        base=0, pattern=[[-1, 128]], channel_multiplier=1)
        e.iota(k["cnt16"][:], [[-128, NTILE]], base=2047, channel_multiplier=-1,
               allow_small_or_imprecise_dtypes=True)
        e.memset(k["eps_v"][:], EPS)
        C.inc(e.engine_nop(), "GP", 1)
    C.g(g)


def emit_weight_prep(C, SCR, jobs, stgw=None):
    if not jobs:
        return
    nc = C.nc
    stgw = stgw or 12 * D
    stgtw = max(R for _, R, _ in jobs)
    pstack = ExitStack()
    stg = pstack.enter_context(nc.sbuf_tensor(C.u("wprep"),
                                              [128, max(stgw, stgtw)], BF16))
    stgT = stg

    for nm, R, Cc in jobs:
        src = C.dram[nm]
        scr1 = SCR[nm + "_s1"]
        scr2 = SCR[nm + "_t"]
        nch = R // 128

        hchunks = max(1, stgw // Cc)
        for c0 in range(0, nch, hchunks):
            c1 = min(c0 + hchunks, nch)

            def g(e, src=src, c0=c0, c1=c1, Cc=Cc):
                C.wait(e, "WP2")
                C.inc(e.dma_start(
                    out=stg[:, 0:(c1 - c0) * Cc].rearrange(
                        "p (c d) -> p c d", d=Cc),
                    in_=src.rearrange("(c p) d -> p c d",
                                      p=128)[:, c0:c1]),
                    "WP", 16)
            C.g(g)

            def y1(e, scr1=scr1, c0=c0, c1=c1, Cc=Cc):
                C.wait(e, "WP")
                C.inc(e.dma_start(
                    out=scr1.rearrange("(c p) d -> p c d", p=128)[:, c0:c1],
                    in_=stg[:, 0:(c1 - c0) * Cc].rearrange(
                        "p (c d) -> p c d", d=Cc)),
                    "WP2", 16)
            C.y(y1)

        for cc in range(Cc // 128):
            def y2(e, scr1=scr1, scr2=scr2, cc=cc, R=R):
                C.wait(e, "WP2")
                C.inc(e.dma_start(out=stgT[:, 0:R],
                                  in_=scr1[:, cc * 128:(cc + 1) * 128],
                                  transpose=True), "WP2", 16)
                C.wait(e, "WP2")
                C.inc(e.dma_start(out=scr2[cc * 128:(cc + 1) * 128, :],
                                  in_=stgT[:, 0:R]), "WP2", 16)
            C.y(y2)
    pstack.close()


def emit_load_x(C, seq):
    def y(e):
        C.inc(e.dma_start(out=C.xs[:].rearrange("p (i d) -> p i d", d=D),
                          in_=C.dram["x"][seq].rearrange("(i p) d -> p i d",
                                                         p=128)),
              "LD2", 16)
    C.y(y)


def emit_store_x(C, seq):
    def y(e):
        C.inc(e.dma_start(
            out=C.dram["out"][seq].rearrange("(i p) d -> p i d", p=128),
            in_=C.xs[:].rearrange("p (i d) -> p i d", d=D)), "ST", 16)
    C.y(y)


def emit_transpose_1024(C, src_fn, psum_tr, dst_fn, nchunk=8):
    k = C.k

    def t(e):
        C.wait(e, "DV")
        C.wait(e, "SC")
        for c in range(nchunk):
            ins = e.transpose(psum_tr[:, c * 128:(c + 1) * 128], src_fn(c),
                              k["ident"][:])
        C.inc(ins, "PE", 1)
    C.t(t)

    def v(e):
        C.wait(e, "PE")
        for c in range(nchunk):
            ins = e.tensor_copy(dst_fn(c), psum_tr[:, c * 128:(c + 1) * 128])
        C.inc(ins, "DV", 1)
    C.v(v)


L0A_STAGE = 5

def emit_l0a(C):
    nc = C.nc
    k = C.k
    xs = C.xs
    with ExitStack() as ph:
        ec = ph.enter_context
        wq = ec(nc.sbuf_tensor(C.u("l0_wq_s"), [128, DC * D], BF16))
        wk = ec(nc.sbuf_tensor(C.u("l0_wk_s"), [128, DC * D], BF16))
        wv = ec(nc.sbuf_tensor(C.u("l0_wv_s"), [128, DC * D], BF16))
        bq = ec(nc.sbuf_tensor(C.u("l0_bq_s"), [128, 8], F32))
        bk = ec(nc.sbuf_tensor(C.u("l0_bk_s"), [128, 8], F32))
        bvr_bf = ec(nc.sbuf_tensor(C.u("l0_bv_row_bf"), [1, D], BF16))
        bvb = ec(nc.sbuf_tensor(C.u("l0_bv_bc"), [128, D], BF16))
        h0 = ec(nc.sbuf_tensor(C.u("h0_fm"), [128, DC * S], BF16))
        kf = ec(nc.sbuf_tensor(C.u("k_fm"), [128, DC * S], BF16))
        qg = ec(nc.sbuf_tensor(C.u("q_fm_g"), [128, DC * 512], BF16))
        e_scr = ec(nc.sbuf_tensor(C.u("e_scr"), [128, S], BF16))
        sm = ec(nc.sbuf_tensor(C.u("l0_sm"), [128, 8], F32))
        lnp = ec(nc.sbuf_tensor(C.u("l0_lnp"), [128, 6 * NTILE], F32))
        htm2 = ec(nc.sbuf_tensor(C.u("l0_htm2"), [128, 2 * D], BF16))

        def g(e):
            C.wait(e, "LD2")
            for nm, dst in (("l0_wq", wq), ("l0_wk", wk), ("l0_wv", wv)):
                C.inc(e.dma_start(out=dst[:].rearrange("p (c d) -> p c d", d=D),
                                  in_=C.dram[nm].rearrange(
                    "(c p) d -> p c d", p=128)), "LD", 16)
            for c in range(8):
                C.inc(e.dma_start(out=bq[:, c:c + 1],
                                  in_=C.dram["l0_bq"][c * 128:(c + 1) * 128][:, None]),
                      "LD", 16)
                C.inc(e.dma_start(out=bk[:, c:c + 1],
                                  in_=C.dram["l0_bk"][c * 128:(c + 1) * 128][:, None]),
                      "LD", 16)
            C.inc(e.dma_start(out=bvr_bf[:], in_=C.dram["l0_bv"][None, :]),
                  "LD", 16)
        C.g(g)
        t_loads = C.tok("LD")

        if getattr(C, "f0_prep_pending", False):
            C.f0_prep_pending = False
            emit_weight_prep(C, C.scr, [("f0_wr", D, D), ("f0_wk", D, D),
                                        ("f0_wv", D, D)], stgw=D)
        emit_pre_stats(C, lnp)
        ap6 = lnp[:].rearrange("p (i c) -> p i c", c=6)
        t_stats = C.tok("DV")

        p_tr = ec(nc.psum_tensor(C.u("p_tr"), [128, 2 * D], BF16))
        p_mm = ec(nc.psum_tensor(C.u("p_mm"), [128, 2 * 512], F32))
        bc_toks = []
        for half in range(2):
            def t0(e, half=half):
                C.wait(e, "LD", t_loads)
                if half == 0:
                    C.wait(e, "DV", )
                C.inc(e.matmul(p_mm[:, half * 512:(half + 1) * 512],
                               k["ones_row"][:],
                               bvr_bf[:, half * 512:(half + 1) * 512],
                               start=True, stop=True), "PE", 1)
            C.t(t0)
            tok_pe = C.tok("PE")

            def v0(e, half=half, tok_pe=tok_pe):
                C.wtok(e, "PE", tok_pe)
                C.inc(e.tensor_copy(bvb[:, half * 512:(half + 1) * 512],
                                    p_mm[:, half * 512:(half + 1) * 512]),
                      "DV", 1)
            C.v(v0)
        t_bvb = C.tok("DV")

        ap_tok = [0] * NTILE
        tr_tok = [0] * NTILE
        kproj_tok = [0] * NGRP
        kev_tok = {}
        for grp in range(NGRP):
            hv_sc, hv_dv = C.cnt["SC"], C.cnt["DV"]

            def s(e, grp=grp, hv_dv=hv_dv,
                  pv=tuple(tr_tok[max(0, grp * 4 - 2): grp * 4])):
                C.wait(e, "DV", t_stats)
                for ti in range(4):
                    i = grp * 4 + ti
                    if ti >= 2:
                        C.wait(e, "DV", hv_dv + ti - 1)
                    elif i >= 2:
                        C.wait(e, "DV", pv[ti])
                    C.inc(e.activation(htm2[:, (i % 2) * D:(i % 2 + 1) * D],
                                       xs[:, i * D:(i + 1) * D], AF.Identity,
                                       scale=ap6[:, i, 4:5],
                                       bias=ap6[:, i, 5:6]), "SC", 1)
            C.s(s)
            for ti in range(4):
                ap_tok[grp * 4 + ti] = hv_sc + ti + 1
            for ti in range(4):
                i = grp * 4 + ti

                def t(e, i=i):
                    C.wtok(e, "SC", ap_tok[i])
                    if i >= 2:
                        C.wtok(e, "DV", tr_tok[i - 2])
                    for c in range(8):
                        ins = e.transpose(
                            p_tr[:, (i % 2) * D + c * 128:
                                 (i % 2) * D + (c + 1) * 128],
                            htm2[:, (i % 2) * D + c * 128:
                                 (i % 2) * D + (c + 1) * 128],
                            k["ident"][:])
                    C.inc(ins, "PE", 1)
                C.t(t)
                tok_pe = C.tok("PE")

                def v(e, i=i, tok_pe=tok_pe):
                    C.wtok(e, "PE", tok_pe)
                    for c in range(8):
                        ins = e.tensor_copy(
                            h0[:, c * S + i * 128: c * S + (i + 1) * 128],
                            p_tr[:, (i % 2) * D + c * 128:
                                 (i % 2) * D + (c + 1) * 128])
                    C.inc(ins, "DV", 1)
                C.v(v)
                tr_tok[i] = C.tok("DV")
                assert tr_tok[i] == hv_dv + ti + 1

            for oc in range(8):
                half = oc % 2

                def t(e, oc=oc, grp=grp, half=half):
                    if oc == 0:
                        C.wtok(e, "DV", tr_tok[grp * 4 + 3])
                        if grp == 0:
                            C.wait(e, "DV", t_bvb)
                    if (grp, oc) in kev_tok:
                        pass
                    prev = kev_tok.get((grp, oc - 2)) or (
                        kev_tok.get((grp - 1, 6 + oc)) if oc < 2 else None)
                    if prev:
                        C.wtok(e, "DV", prev)
                    for dc in range(8):
                        ins = e.matmul(
                            p_mm[:, half * 512:(half + 1) * 512],
                            wk[:, dc * D + oc * 128: dc * D + (oc + 1) * 128],
                            h0[:, dc * S + grp * 512: dc * S + (grp + 1) * 512],
                            start=(dc == 0), stop=(dc == 7))
                    C.inc(ins, "PE", 1)
                C.t(t)
                tok_pe = C.tok("PE")

                def v(e, oc=oc, grp=grp, half=half, tok_pe=tok_pe):
                    C.wtok(e, "PE", tok_pe)
                    C.inc(e.tensor_scalar(
                        kf[:, oc * S + grp * 512: oc * S + (grp + 1) * 512],
                        p_mm[:, half * 512:(half + 1) * 512], bk[:, oc:oc + 1],
                        None, op0=ALU.add), "DV", 1)
                C.v(v)
                kev_tok[(grp, oc)] = C.tok("DV")
            kproj_tok[grp] = C.tok("DV")

        p_s = ec(nc.psum_tensor(C.u("p_s"), [128, S], F32))
        exp_tok = [0] * (NTILE + 1)
        oev_tok = [0] * NTILE
        q_tok = [0] * NGRP
        qev_tok = {}
        for i in range(NTILE):
            grp, ti = divmod(i, 4)
            if ti == 0:
                for oc in range(8):
                    def t(e, oc=oc, grp=grp, i=i):
                        if oc == 0:
                            C.wtok(e, "DV", kproj_tok[NGRP - 1])
                            if i > 0:
                                C.wtok(e, "DV", oev_tok[i - 1])
                        prev = qev_tok.get((grp, oc - 1))
                        if prev:
                            C.wtok(e, "DV", prev)
                        for dc in range(8):
                            ins = e.matmul(
                                p_mm[:, 0:512],
                                wq[:, dc * D + oc * 128: dc * D + (oc + 1) * 128],
                                h0[:, dc * S + grp * 512:
                                   dc * S + (grp + 1) * 512],
                                start=(dc == 0), stop=(dc == 7))
                        C.inc(ins, "PE", 1)
                    C.t(t)
                    tok_pe = C.tok("PE")

                    def vv(e, oc=oc, tok_pe=tok_pe):
                        C.wtok(e, "PE", tok_pe)
                        C.inc(e.tensor_scalar(qg[:, oc * 512:(oc + 1) * 512],
                                              p_mm[:, 0:512], bq[:, oc:oc + 1],
                                              None, op0=ALU.add), "DV", 1)
                    C.v(vv)
                    qev_tok[(grp, oc)] = C.tok("DV")
                q_tok[grp] = C.tok("DV")

            ncols = (i + 1) * 128

            def t(e, ti=ti, ncols=ncols, grp=grp, i=i):
                C.wtok(e, "DV", q_tok[grp])
                C.wtok(e, "SC", exp_tok[i - 1] if i > 0 else 0)
                if i > 0:
                    C.wtok(e, "DV", oev_tok[i - 1] if False else 0)
                for dc in range(8):
                    for blk in range((ncols + 511) // 512):
                        n = min(512, ncols - blk * 512)
                        ins = e.matmul(
                            p_s[:, blk * 512: blk * 512 + n],
                            qg[:, dc * 512 + ti * 128: dc * 512 + (ti + 1) * 128],
                            kf[:, dc * S + blk * 512: dc * S + blk * 512 + n],
                            start=(dc == 0), stop=(dc == 7))
                C.inc(ins, "PE", 1)
            C.t(t)
            sc_pe = C.tok("PE")

            def t2(e, i=i):
                if i > 0:
                    C.wtok(e, "DV", oev_tok[i - 1])
                else:
                    C.wait(e, "DV", t_bvb)
                    C.wtok(e, "DV", kproj_tok[NGRP - 1])
                C.wtok(e, "DV", qev_tok.get((i // 4, 7), 0))
                for dc in range(8):
                    for half in range(2):
                        ins = e.matmul(
                            p_mm[:, half * 512:(half + 1) * 512],
                            h0[:, dc * S + i * 128: dc * S + (i + 1) * 128],
                            wv[:, dc * D + half * 512: dc * D + (half + 1) * 512],
                            start=(dc == 0), stop=(dc == 7))
                C.inc(ins, "PE", 1)
            C.t(t2)
            v_pe = C.tok("PE")

            def v(e, ncols=ncols, sc_pe=sc_pe):
                C.wtok(e, "PE", sc_pe)
                C.fence(e, e.tensor_tensor(p_s[:, ncols - 128: ncols],
                                           p_s[:, ncols - 128: ncols],
                                           k["cmask"][:], op=ALU.add), "DV")
                C.fence(e, e.tensor_reduce(sm[:, 0:1], p_s[:, 0:ncols],
                                           axis=AX.X, op=ALU.max), "DV")
                C.fence(e, e.tensor_scalar(sm[:, 1:2], sm[:, 0:1], 1.0 / 32,
                                           None, op0=ALU.mult), "DV")
                C.fence(e, e.tensor_scalar(sm[:, 1:2], sm[:, 1:2], 0.0, None,
                                           op0=ALU.max), "DV")
                C.inc(e.tensor_scalar(sm[:, 2:3], sm[:, 1:2], -1.0, None,
                                      op0=ALU.mult), "DV", 1)
            C.v(v)
            mx_tok = C.tok("DV")

            def s(e, ncols=ncols, mx_tok=mx_tok):
                C.wtok(e, "DV", mx_tok)
                e.activation(e_scr[:, 0:ncols], p_s[:, 0:ncols], AF.Exp,
                             scale=1.0 / 32, bias=sm[:, 2:3],
                             accum_out=sm[:, 3:4])
                C.inc(e.activation(sm[:, 4:5], sm[:, 2:3], AF.Exp), "SC", 1)
            C.s(s)
            exp_tok[i] = C.tok("SC")

            def v(e, i=i, v_pe=v_pe):
                C.wtok(e, "SC", exp_tok[i])
                C.fence(e, e.tensor_tensor(sm[:, 5:6], k["cnt16"][:, i:i + 1],
                                           sm[:, 4:5], op=ALU.mult), "DV")
                C.fence(e, e.tensor_tensor(sm[:, 5:6], sm[:, 5:6], sm[:, 3:4],
                                           op=ALU.add), "DV")
                C.fence(e, e.reciprocal(sm[:, 6:7], sm[:, 5:6]), "DV")
                C.fence(e, e.tensor_tensor(sm[:, 6:7], sm[:, 6:7], sm[:, 5:6],
                                           op=ALU.mult), "DV")
                C.wtok(e, "PE", v_pe)
                C.fence(e, e.tensor_tensor(k["scr_a"][:], p_mm[:], bvb[:],
                                           op=ALU.add), "DV")
                C.inc(e.tensor_scalar(k["scr_a"][:], k["scr_a"][:], sm[:, 6:7], None,
                                      op0=ALU.mult), "DV", 1)
            C.v(v)
            oev_tok[i] = C.tok("DV")

            emit_ln_post(C, k["scr_a"][:], xs[:, i * D:(i + 1) * D])


def emit_ln_post(C, o_ap, xs_ap):
    k = C.k
    st = k["st8"]

    def v1(e):
        C.wait(e, "DV")
        C.wait(e, "SC")
        C.fence(e, e.tensor_reduce(st[:, 0:1], o_ap, axis=AX.X, op=ALU.add),
                "DV")
        C.fence(e, e.tensor_scalar(st[:, 2:3], st[:, 0:1], 1.0 / D, None,
                                   op0=ALU.mult), "DV")
        C.inc(e.scalar_tensor_tensor(st[:, 4:5], st[:, 2:3], -1.0,
                                     st[:, 2:3], op0=ALU.mult,
                                     op1=ALU.mult), "DV", 1)
    C.v(v1)
    t3 = C.tok("DV")
    t3c = t3 + 2

    def s1(e, t3=t3, t3c=t3c):
        C.wait(e, "DV", t3)
        C.inc(e.activation(k["scr_b"][:], o_ap, AF.Square,
                           accum_out=st[:, 1:2]), "SC", 1)
        C.wait(e, "DV", t3c)
        C.fence(e, e.activation(st[:, 4:5], st[:, 3:4], AF.Ln,
                                bias=k["eps_v"][:, 0:1]), "SC")
        C.inc(e.activation(st[:, 5:6], st[:, 4:5], AF.Exp, scale=-0.5),
              "SC", 1)
    C.s(s1)
    t3b = C.tok("SC") - 2
    t4 = C.tok("SC")

    def v2(e, t3b=t3b):
        C.wait(e, "SC", t3b)
        C.fence(e, e.tensor_scalar(st[:, 3:4], st[:, 1:2], 1.0 / D, None,
                                   op0=ALU.mult), "DV")
        C.inc(e.tensor_tensor(st[:, 3:4], st[:, 3:4], st[:, 4:5], op=ALU.add),
              "DV", 1)
    C.v(v2)

    def v3(e, t4=t4):
        C.wait(e, "SC", t4)
        C.fence(e, e.scalar_tensor_tensor(st[:, 6:7], st[:, 2:3], -1.0,
                                          st[:, 5:6], op0=ALU.mult,
                                          op1=ALU.mult), "DV")
        C.fence(e, e.tensor_scalar(k["scr_b"][:], o_ap, st[:, 5:6],
                                   st[:, 6:7], op0=ALU.mult, op1=ALU.add),
                "DV")
        C.inc(e.tensor_tensor(xs_ap, xs_ap, k["scr_b"][:], op=ALU.add),
              "DV", 1)
    C.v(v3)


def emit_l0f(C, SCR):
    nc = C.nc
    k = C.k
    xs = C.xs
    with ExitStack() as ph:
        ec = ph.enter_context
        wrT = ec(nc.sbuf_tensor(C.u("f0_wrT"), [128, DC * D], BF16))
        wkT = ec(nc.sbuf_tensor(C.u("f0_wkT"), [128, DC * D], BF16))
        wvT = ec(nc.sbuf_tensor(C.u("f0_wvT"), [128, DC * D], BF16))
        mur = ec(nc.sbuf_tensor(C.u("f0_mur_s"), [128, 8], F32))
        muk = ec(nc.sbuf_tensor(C.u("f0_muk_s"), [128, 8], F32))
        murc = ec(nc.sbuf_tensor(C.u("f0_murc"), [128, 8], F32))
        mukc = ec(nc.sbuf_tensor(C.u("f0_mukc"), [128, 8], F32))
        hext = ec(nc.sbuf_tensor(C.u("f0_hext"), [128, 8 * 513], BF16))
        xr = ec(nc.sbuf_tensor(C.u("f0_xr"), [128, 8 * 512], BF16))
        xk = ec(nc.sbuf_tensor(C.u("f0_xk"), [128, 8 * 512], BF16))
        kact = ec(nc.sbuf_tensor(C.u("f0_kact"), [128, 8 * 512], BF16))
        tmp_bf = ec(nc.sbuf_tensor(C.u("f0_tmp"), [128, 512], BF16))
        krel = ec(nc.sbuf_tensor(C.u("f0_krel"), [128, 2 * 512], BF16))
        rtm = ec(nc.sbuf_tensor(C.u("f0_rtm"), [128, D], F32))
        lnp = ec(nc.sbuf_tensor(C.u("f0_lnp"), [128, 6 * NTILE], F32))
        htm2 = ec(nc.sbuf_tensor(C.u("f0_htm2"), [128, 2 * D], BF16))
        p_tr = ec(nc.psum_tensor(C.u("f0_p_tr"), [128, 2 * D], BF16))
        p_k = ec(nc.psum_tensor(C.u("f0_p_k"), [128, 2 * 512], F32))
        p_r = ec(nc.psum_tensor(C.u("f0_p_r"), [128, D], F32))
        p_kv = ec(nc.psum_tensor(C.u("f0_p_kv"), [128, D], F32))

        def y(e):
            C.wait(e, "LD2")
            for nm, dst in (("f0_wr_t", wrT), ("f0_wk_t", wkT),
                            ("f0_wv_t", wvT)):
                C.inc(e.dma_start(out=dst[:].rearrange("p (c d) -> p c d", d=D),
                                  in_=C.scr[nm].rearrange(
                    "(c p) d -> p c d", p=128)), "LD2", 16)
        C.y(y)
        t_wl = C.tok("LD2")

        def g(e):
            C.wait(e, "LD2")
            for c in range(8):
                C.inc(e.dma_start(out=mur[:, c:c + 1],
                                  in_=C.dram["f0_mur"][c * 128:(c + 1) * 128][:, None]),
                      "LD", 16)
                C.inc(e.dma_start(out=muk[:, c:c + 1],
                                  in_=C.dram["f0_muk"][c * 128:(c + 1) * 128][:, None]),
                      "LD", 16)
            for c in range(8):
                ins = e.memset(hext[:, c * 513: c * 513 + 1], 0.0)
            C.inc(ins, "GP", 1)
        C.g(g)
        t_mu = C.tok("LD")
        t_hx0 = C.tok("GP")

        if getattr(C, "f1_prep_pending", False):
            C.f1_prep_pending = False
            emit_weight_prep(C, SCR, [("f1_wg", HID, D), ("f1_wu", HID, D),
                                      ("f1_wd", D, HID)])

        def v0(e):
            C.wait(e, "LD", t_mu)
            C.fence(e, e.tensor_scalar(murc[:], mur[:], -1.0, 1.0,
                                       op0=ALU.mult, op1=ALU.add), "DV")
            C.inc(e.tensor_scalar(mukc[:], muk[:], -1.0, 1.0, op0=ALU.mult,
                                  op1=ALU.add), "DV", 1)
        C.v(v0)
        t_muc = C.tok("DV")

        emit_pre_stats(C, lnp)
        ap6 = lnp[:].rearrange("p (i c) -> p i c", c=6)
        t_stats = C.tok("DV")

        tr_tok = [0] * NTILE
        mix_tok = [0] * NGRP
        carry_tok = [0] * NGRP
        kact_tok = {}
        kev_sc = {}
        o_tok = [0] * NTILE
        rr_tok = [0] * NTILE
        kvp_tok = [0] * NTILE
        for grp in range(NGRP):
            if grp > 0:
                def v(e, grp=grp):
                    C.wtok(e, "DV", mix_tok[grp - 1])
                    for c in range(8):
                        ins = e.tensor_copy(hext[:, c * 513: c * 513 + 1],
                                            hext[:, c * 513 + 512:
                                                 c * 513 + 513])
                    C.inc(ins, "DV", 1)
                C.v(v)
                carry_tok[grp] = C.tok("DV")

            hv_sc, hv_dv = C.cnt["SC"], C.cnt["DV"]

            def s(e, grp=grp, hv_dv=hv_dv,
                  pv=tuple(tr_tok[max(0, grp * 4 - 2): grp * 4])):
                C.wait(e, "DV", t_stats)
                for ti in range(4):
                    i = grp * 4 + ti
                    if ti >= 2:
                        C.wait(e, "DV", hv_dv + ti - 1)
                    elif i >= 2:
                        C.wait(e, "DV", pv[ti])
                    C.inc(e.activation(htm2[:, (i % 2) * D:(i % 2 + 1) * D],
                                       xs[:, i * D:(i + 1) * D], AF.Identity,
                                       scale=ap6[:, i, 4:5],
                                       bias=ap6[:, i, 5:6]), "SC", 1)
            C.s(s)
            for ti in range(4):
                i = grp * 4 + ti
                ap_t = hv_sc + ti + 1

                def t(e, i=i, ap_t=ap_t, grp=grp, ti=ti):
                    C.wtok(e, "SC", ap_t)
                    if i >= 2:
                        C.wtok(e, "DV", tr_tok[i - 2])
                    for c in range(8):
                        ins = e.transpose(
                            p_tr[:, (i % 2) * D + c * 128:
                                 (i % 2) * D + (c + 1) * 128],
                            htm2[:, (i % 2) * D + c * 128:
                                 (i % 2) * D + (c + 1) * 128],
                            k["ident"][:])
                    C.inc(ins, "PE", 1)
                C.t(t)
                pe_t = C.tok("PE")

                def v(e, i=i, ti=ti, grp=grp, pe_t=pe_t):
                    C.wtok(e, "PE", pe_t)
                    if ti == 0 and grp > 0:
                        C.wtok(e, "DV", carry_tok[grp])
                    if grp == 0 and ti == 0:
                        C.wait(e, "GP", t_hx0)
                    for c in range(8):
                        ins = e.tensor_copy(
                            hext[:, c * 513 + 1 + ti * 128:
                                 c * 513 + 1 + (ti + 1) * 128],
                            p_tr[:, (i % 2) * D + c * 128:
                                 (i % 2) * D + (c + 1) * 128])
                    C.inc(ins, "DV", 1)
                C.v(v)
                tr_tok[i] = C.tok("DV")

            def v(e, grp=grp):
                C.wtok(e, "DV", tr_tok[grp * 4 + 3])
                C.wait(e, "DV", t_muc)
                if grp > 0:
                    C.wtok(e, "PE", kvp_tok[grp * 4 - 1])
                for c in range(8):
                    h1 = hext[:, c * 513 + 1: c * 513 + 513]
                    h1p = hext[:, c * 513: c * 513 + 512]
                    C.fence(e, e.tensor_scalar(tmp_bf[:], h1p, murc[:, c:c + 1],
                                               None, op0=ALU.mult), "DV")
                    C.fence(e, e.scalar_tensor_tensor(
                        xr[:, c * 512:(c + 1) * 512], h1, mur[:, c:c + 1],
                        tmp_bf[:], op0=ALU.mult, op1=ALU.add), "DV")
                    C.fence(e, e.tensor_scalar(tmp_bf[:], h1p, mukc[:, c:c + 1],
                                               None, op0=ALU.mult), "DV")
                    ins = e.scalar_tensor_tensor(xk[:, c * 512:(c + 1) * 512],
                                                 h1, muk[:, c:c + 1], tmp_bf[:],
                                                 op0=ALU.mult, op1=ALU.add)
                    if c < 7:
                        C.fence(e, ins, "DV")
                C.inc(ins, "DV", 1)
            C.v(v)
            mix_tok[grp] = C.tok("DV")

            ka_pe, ka_sc = C.cnt["PE"], C.cnt["SC"]

            def s(e, ka_pe=ka_pe):
                for oc in range(8):
                    pb = oc % 2
                    C.wait(e, "PE", ka_pe + oc + 1)
                    C.inc(e.activation(krel[:, pb * 512:(pb + 1) * 512],
                                       p_k[:, pb * 512:(pb + 1) * 512],
                                       AF.Relu), "SC", 1)
            C.s(s)
            for oc in range(8):
                pb = oc % 2

                def t(e, oc=oc, grp=grp, pb=pb):
                    if oc == 0:
                        C.wtok(e, "DV", mix_tok[grp])
                        C.wait(e, "LD2", t_wl)
                    prev = kact_tok.get((grp, oc - 2)) or (
                        kact_tok.get((grp - 1, 6 + oc))
                        if oc < 2 and grp > 0 else None)
                    if prev:
                        C.wtok(e, "DV", prev)
                    for dc in range(8):
                        ins = e.matmul(
                            p_k[:, pb * 512:(pb + 1) * 512],
                            wkT[:, dc * D + oc * 128: dc * D + (oc + 1) * 128],
                            xk[:, dc * 512:(dc + 1) * 512],
                            start=(dc == 0), stop=(dc == 7))
                    C.inc(ins, "PE", 1)
                C.t(t)
                assert C.cnt["PE"] == ka_pe + oc + 1

                def v(e, oc=oc, pb=pb, sc_t=ka_sc + oc + 1):
                    C.wait(e, "SC", sc_t)
                    C.inc(e.tensor_tensor(kact[:, oc * 512:(oc + 1) * 512],
                                          krel[:, pb * 512:(pb + 1) * 512],
                                          krel[:, pb * 512:(pb + 1) * 512],
                                          op=ALU.mult), "DV", 1)
                C.v(v)
                kact_tok[(grp, oc)] = C.tok("DV")

            for ti in range(4):
                i = grp * 4 + ti

                def t(e, ti=ti, grp=grp, i=i):
                    if ti == 0:
                        C.wtok(e, "DV", mix_tok[grp])
                    if i > 0:
                        C.wtok(e, "SC", rr_tok[i - 1])
                    for dc in range(8):
                        for half in range(2):
                            ins = e.matmul(
                                p_r[:, half * 512:(half + 1) * 512],
                                xr[:, dc * 512 + ti * 128:
                                   dc * 512 + (ti + 1) * 128],
                                wrT[:, dc * D + half * 512:
                                    dc * D + (half + 1) * 512],
                                start=(dc == 0), stop=(dc == 7))
                    C.inc(ins, "PE", 1)
                C.t(t)
                pe_r = C.tok("PE")

                def s(e, pe_r=pe_r, i=i):
                    C.wtok(e, "PE", pe_r)
                    if i > 0:
                        C.wtok(e, "DV", o_tok[i - 1])
                    C.inc(e.activation(rtm[:], p_r[:], AF.Sigmoid), "SC", 1)
                C.s(s)
                rr_tok[i] = C.tok("SC")

                def t2(e, ti=ti, grp=grp, i=i):
                    C.wtok(e, "DV", kact_tok[(grp, 7)])
                    if i > 0:
                        C.wtok(e, "DV", o_tok[i - 1])
                    for fc in range(8):
                        for half in range(2):
                            ins = e.matmul(
                                p_kv[:, half * 512:(half + 1) * 512],
                                kact[:, fc * 512 + ti * 128:
                                     fc * 512 + (ti + 1) * 128],
                                wvT[:, fc * D + half * 512:
                                    fc * D + (half + 1) * 512],
                                start=(fc == 0), stop=(fc == 7))
                    C.inc(ins, "PE", 1)
                C.t(t2)
                kvp_tok[i] = C.tok("PE")

                def v(e, i=i):
                    C.wtok(e, "PE", kvp_tok[i])
                    C.wtok(e, "SC", rr_tok[i])
                    C.wait(e, "DV")
                    C.inc(e.tensor_tensor(k["scr_a"][:], p_kv[:], rtm[:],
                                          op=ALU.mult), "DV", 1)
                C.v(v)
                o_tok[i] = C.tok("DV")

                emit_ln_post(C, k["scr_a"][:], xs[:, i * D:(i + 1) * D])


def emit_l1a(C):
    nc = C.nc
    k = C.k
    xs = C.xs
    qscr = C.scr["qscr"]
    with ExitStack() as ph:
        ec = ph.enter_context
        kp = ec(nc.sbuf_tensor(C.u("l1_k"), [128, DC * S], BF16))
        vst = ec(nc.sbuf_tensor(C.u("l1_v"), [128, NTILE * H * 65], BF16))
        h2f = ec(nc.sbuf_tensor(C.u("l1_h2f"), [128, DC * 512], BF16))
        lnp = ec(nc.sbuf_tensor(C.u("l1_lnp"), [128, 6 * NTILE], F32))
        htm2 = ec(nc.sbuf_tensor(C.u("l1_htm2"), [128, 2 * D], BF16))
        sqs = ec(nc.sbuf_tensor(C.u("l1_sq"), [128, 512], BF16))
        nrm = ec(nc.sbuf_tensor(C.u("l1_nrm"), [1, 80], F32))
        negm = ec(nc.sbuf_tensor(C.u("l1_negm"), [128, 1], F32))
        mrow = ec(nc.sbuf_tensor(C.u("l1_mrow"), [1, 1], BF16))
        ap6 = None

        emit_pre_stats(C, lnp)
        ap6 = lnp[:].rearrange("p (i c) -> p i c", c=6)
        t_stats = C.tok("DV")

        with ExitStack() as pa:
            eca = pa.enter_context
            p_tr = eca(nc.psum_tensor(C.u("l1_p_tr"), [128, 2 * D], BF16))
            p_mm = eca(nc.psum_tensor(C.u("l1_p_mm"), [128, 4 * 512], F32))
            p_v = eca(nc.psum_tensor(C.u("l1_p_v"), [128, D], F32))
            wq = eca(nc.sbuf_tensor(C.u("l1_wq_s"), [128, DC * D], BF16))
            wk = eca(nc.sbuf_tensor(C.u("l1_wk_s"), [128, DC * D], BF16))
            wv = eca(nc.sbuf_tensor(C.u("l1_wv_s"), [128, DC * D], BF16))
            qgb = eca(nc.sbuf_tensor(C.u("l1_qgb"), [128, DC * 512], BF16))

            def g(e):
                C.wait(e, "LD2")
                for nm, dst in [("l1_wq", wq), ("l1_wk", wk), ("l1_wv", wv)]:
                    C.inc(e.dma_start(
                        out=dst[:].rearrange("p (c d) -> p c d", d=D),
                        in_=C.dram[nm].rearrange("(c p) d -> p c d", p=128)),
                        "LD", 16)
                C.inc(e.memset(vst[:], 1.0), "GP", 1)
            C.g(g)
            t_wload = C.tok("LD")
            t_vmem = C.tok("GP")

            tr_tok = [0] * NTILE
            qk_tok = {}
            vev_tok = [0] * NTILE
            qship_tok = [0] * NGRP
            nq_tok = {}
            for grp in range(NGRP):
                hv_sc, hv_dv = C.cnt["SC"], C.cnt["DV"]

                def s(e, grp=grp, hv_dv=hv_dv,
                      pv=tuple(tr_tok[max(0, grp * 4 - 2): grp * 4])):
                    C.wait(e, "DV", t_stats)
                    for ti in range(4):
                        i = grp * 4 + ti
                        if ti >= 2:
                            C.wait(e, "DV", hv_dv + ti - 1)
                        elif i >= 2:
                            C.wait(e, "DV", pv[ti])
                        C.inc(e.activation(
                            htm2[:, (i % 2) * D:(i % 2 + 1) * D],
                            xs[:, i * D:(i + 1) * D],
                            AF.Identity, scale=ap6[:, i, 4:5],
                            bias=ap6[:, i, 5:6]), "SC", 1)
                C.s(s)
                for ti in range(4):
                    i = grp * 4 + ti
                    ap_t = hv_sc + ti + 1

                    def t(e, i=i, ti=ti, ap_t=ap_t):
                        C.wtok(e, "SC", ap_t)
                        if i >= 2:
                            C.wtok(e, "DV", tr_tok[i - 2])
                        for c in range(8):
                            ins = e.transpose(
                                p_tr[:, (i % 2) * D + c * 128:
                                     (i % 2) * D + (c + 1) * 128],
                                htm2[:, (i % 2) * D + c * 128:
                                     (i % 2) * D + (c + 1) * 128],
                                k["ident"][:])
                        C.inc(ins, "PE", 1)
                    C.t(t)
                    pe_t = C.tok("PE")

                    def v(e, i=i, ti=ti, pe_t=pe_t):
                        C.wtok(e, "PE", pe_t)
                        for c in range(8):
                            ins = e.tensor_copy(
                                h2f[:, c * 512 + ti * 128:
                                    c * 512 + (ti + 1) * 128],
                                p_tr[:, (i % 2) * D + c * 128:
                                     (i % 2) * D + (c + 1) * 128])
                        C.inc(ins, "DV", 1)
                    C.v(v)
                    tr_tok[i] = C.tok("DV")
                    assert tr_tok[i] == hv_dv + ti + 1

                qp_pe, qp_sc = C.cnt["PE"], C.cnt["SC"]

                def s(e, qp_pe=qp_pe):
                    for oc in range(8):
                        qtr = oc % 2
                        C.wait(e, "PE", qp_pe + 2 * oc + 1)
                        C.inc(e.activation(
                            qg_dst(qgb, oc), p_mm[:, qtr * 512:
                                                  (qtr + 1) * 512],
                            AF.Identity), "SC", 1)
                C.s(s)
                for oc in range(8):
                    for wsb, isq in ((wq, True), (wk, False)):
                        qtr = (oc % 2) + (0 if isq else 2)

                        def t(e, oc=oc, grp=grp, qtr=qtr, wsb=wsb, isq=isq):
                            if oc == 0 and isq:
                                C.wtok(e, "DV", tr_tok[grp * 4 + 3])
                                C.wait(e, "LD", t_wload)
                                if grp > 0:
                                    C.wtok(e, "ST", qship_tok[grp - 1])
                                    C.wtok(e, "DV", nq_tok[(grp - 1, 7)][1])
                            prev = qk_tok.get((grp, oc - 2, isq)) or (
                                qk_tok.get((grp - 1, 6 + oc, isq))
                                if oc < 2 and grp > 0 else None)
                            if prev:
                                nm, v_ = prev
                                C.wtok(e, nm, v_)
                            for dc in range(8):
                                ins = e.matmul(
                                    p_mm[:, qtr * 512:(qtr + 1) * 512],
                                    wsb[:, dc * D + oc * 128:
                                        dc * D + (oc + 1) * 128],
                                    h2f[:, dc * 512:(dc + 1) * 512],
                                    start=(dc == 0), stop=(dc == 7))
                            C.inc(ins, "PE", 1)
                        C.t(t)
                        pe_t = C.tok("PE")

                        if isq:
                            assert pe_t == qp_pe + 2 * oc + 1
                            qk_tok[(grp, oc, True)] = ("SC", qp_sc + oc + 1)
                        else:
                            def v(e, oc=oc, grp=grp, qtr=qtr, pe_t=pe_t):
                                C.wtok(e, "PE", pe_t)
                                C.inc(e.tensor_scalar(
                                    kp[:, oc * S + grp * 512:
                                       oc * S + (grp + 1) * 512],
                                    p_mm[:, qtr * 512:(qtr + 1) * 512],
                                    0.125, None, op0=ALU.mult), "DV", 1)
                            C.v(v)
                            qk_tok[(grp, oc, False)] = ("DV", C.tok("DV"))

                vp_pe, vp_sc = C.cnt["PE"], C.cnt["SC"]

                def s(e, vp_pe=vp_pe, grp=grp):
                    C.wait(e, "GP", t_vmem)
                    for ti in range(4):
                        i = grp * 4 + ti
                        C.wait(e, "PE", vp_pe + ti + 1)
                        dst = vst[:, i * H * 65:(i + 1) * H * 65].rearrange(
                            "p (h e) -> p h e", e=65)[:, :, 0:64]
                        srcp = p_v[:].rearrange("p (h e) -> p h e", e=64)
                        C.inc(e.activation(dst, srcp, AF.Identity), "SC", 1)
                C.s(s)
                for ti in range(4):
                    i = grp * 4 + ti

                    def t(e, i=i, ti=ti, grp=grp):
                        if ti == 0:
                            nm, v_ = qk_tok[(grp, 7, False)]
                            C.wtok(e, nm, v_)
                            nm, v_ = qk_tok[(grp, 7, True)]
                            C.wtok(e, nm, v_)
                        if i >= 1:
                            C.wtok(e, "SC", vev_tok[i - 1])
                        for dc in range(8):
                            for hh in range(2):
                                ins = e.matmul(
                                    p_v[:, hh * 512:(hh + 1) * 512],
                                    h2f[:, dc * 512 + ti * 128:
                                        dc * 512 + (ti + 1) * 128],
                                    wv[:, dc * D + hh * 512:
                                        dc * D + (hh + 1) * 512],
                                    start=(dc == 0), stop=(dc == 7))
                        C.inc(ins, "PE", 1)
                    C.t(t)
                    assert C.cnt["PE"] == vp_pe + ti + 1
                    vev_tok[i] = vp_sc + ti + 1

                qn_pe, qn_sc, qn_dv = C.cnt["PE"], C.cnt["SC"], C.cnt["DV"]

                def s(e, qn_pe=qn_pe):
                    for pc in range(8):
                        C.wait(e, "PE", qn_pe + pc)
                        C.inc(e.activation(
                            sqs[:],
                            qgb[:, pc * 512:(pc + 1) * 512],
                            AF.Square), "SC", 1)
                C.s(s)
                assert C.cnt["SC"] == qn_sc + 8

                for pc in range(8):
                    def t(e, pc=pc, qn_sc=qn_sc, qn_dv=qn_dv):
                        C.wait(e, "SC", qn_sc + pc + 1)
                        C.wait(e, "DV", qn_dv + pc)
                        C.inc(e.matmul(p_mm[0:1, 0:512], k["ones_col"][:],
                                       sqs[:],
                                       start=True, stop=True), "PE", 1)
                    C.t(t)

                    def v(e, pc=pc, grp=grp, qn_pe=qn_pe):
                        C.wait(e, "PE", qn_pe + pc + 1)
                        C.inc(e.tensor_reduce(
                            nrm[:, pc * 4 + grp: pc * 4 + grp + 1],
                            p_mm[0:1, 0:512], axis=AX.X, op=ALU.max), "DV", 1)
                    C.v(v)
                    nq_tok[(grp, pc)] = (qn_pe + pc + 1, C.tok("DV"))
                assert C.cnt["PE"] == qn_pe + 8

                def y(e, grp=grp):
                    C.wtok(e, "SC", qk_tok[(grp, 7, True)][1])
                    C.inc(e.dma_start(out=qscr[grp], in_=qgb[:]), "ST", 16)
                C.y(y)
                qship_tok[grp] = C.tok("ST")

            kn_pe, kn_sc, kn_dv = C.cnt["PE"], C.cnt["SC"], C.cnt["DV"]

            def s(e, kn_pe=kn_pe):
                for j in range(32):
                    C.wait(e, "PE", kn_pe + j)
                    pc, g4 = divmod(j, 4)
                    C.inc(e.activation(
                        sqs[:],
                        kp[:, pc * S + g4 * 512: pc * S + (g4 + 1) * 512],
                        AF.Square), "SC", 1)
            C.s(s)
            assert C.cnt["SC"] == kn_sc + 32

            for j in range(32):
                pc, g4 = divmod(j, 4)

                def t(e, j=j, kn_sc=kn_sc, kn_dv=kn_dv):
                    C.wait(e, "SC", kn_sc + j + 1)
                    C.wait(e, "DV", kn_dv + j)
                    C.inc(e.matmul(p_mm[0:1, 0:512], k["ones_col"][:],
                                   sqs[:],
                                   start=True, stop=True), "PE", 1)
                C.t(t)

                def v(e, j=j, pc=pc, g4=g4, kn_pe=kn_pe):
                    C.wait(e, "PE", kn_pe + j + 1)
                    C.inc(e.tensor_reduce(
                        nrm[:, 40 + pc * 4 + g4: 40 + pc * 4 + g4 + 1],
                        p_mm[0:1, 0:512], axis=AX.X, op=ALU.max), "DV", 1)
                C.v(v)
            assert C.cnt["PE"] == kn_pe + 32

            def v(e):
                C.wait(e, "DV")
                e.tensor_reduce(nrm[:, 32:33], nrm[:, 0:32], axis=AX.X,
                                op=ALU.max)
                C.fence(e, e.tensor_reduce(nrm[:, 72:73], nrm[:, 40:72],
                                           axis=AX.X, op=ALU.max), "DV")
                C.inc(e.tensor_tensor(nrm[:, 33:34], nrm[:, 32:33],
                                      nrm[:, 72:73], op=ALU.mult), "DV", 1)
            C.v(v)

            def s(e):
                C.wait(e, "DV")
                C.fence(e, e.activation(nrm[:, 34:35], nrm[:, 33:34],
                                        AF.Ln), "SC")
                C.inc(e.activation(mrow[:], nrm[:, 34:35], AF.Exp, scale=0.5),
                      "SC", 1)
            C.s(s)
            sq_t = C.tok("SC")

            def t(e, sq_t=sq_t):
                C.wtok(e, "SC", sq_t)
                C.inc(e.matmul(p_mm[:, 0:1], k["ones_row"][:], mrow[:],
                               start=True, stop=True), "PE", 1)
            C.t(t)
            mm_t = C.tok("PE")

            def v(e, mm_t=mm_t):
                C.wtok(e, "PE", mm_t)
                C.inc(e.tensor_scalar(negm[:], p_mm[:, 0:1], -1.0, None,
                                      op0=ALU.mult), "DV", 1)
            C.v(v)
        t_negm = C.tok("DV")

        with ExitStack() as pb:
            ecb = pb.enter_context
            wp = ecb(nc.sbuf_tensor(C.u("l1_wp_s"), [128, DC * D], BF16))
            qgrp = ecb(nc.sbuf_tensor(C.u("l1_qgrp"), [128, DC * 512], BF16))
            pt = ecb(nc.sbuf_tensor(C.u("l1_pt"), [128, 2 * 1024], BF16))
            au = ecb(nc.sbuf_tensor(C.u("l1_au"), [65, 2 * 512], BF16))
            an = ecb(nc.sbuf_tensor(C.u("l1_an"), [128, 4 * D], F32))
            ag = ecb(nc.sbuf_tensor(C.u("l1_ag"), [128, 2 * D], BF16))
            af = ecb(nc.sbuf_tensor(C.u("l1_af"), [128, 2 * D], BF16))
            p_sp = ecb(nc.psum_tensor(C.u("l1_p_s"), [128, 2 * 1024], F32))
            p_o = ecb(nc.psum_tensor(C.u("l1_p_o"), [65, 2 * 512], F32))
            p_at = ecb(nc.psum_tensor(C.u("l1_p_at"), [128, 2 * 1024], BF16))
            p_w = p_at[:].bitcast(F32)[:, 0:512]

            def g(e):
                C.wait(e, "PE")
                C.wait(e, "SC")
                C.wait(e, "DV")
                C.wait(e, "ST")
                C.inc(e.dma_start(
                    out=wp[:].rearrange("p (c d) -> p c d", d=D),
                    in_=C.dram["l1_wp"].rearrange("(c p) d -> p c d", p=128)),
                    "LD", 16)
            C.g(g)
            t_wp = C.tok("LD")

            qld_tok = [0] * NGRP
            exp_tok = {}
            pv_tok = {}
            auev_tok = {}
            anrm_tok = {}
            agl_tok = [0] * 4
            aft_tok = [0] * 4
            wpo_tok = [0] * NTILE
            for grp in range(NGRP):
                def y(e, grp=grp):
                    C.wait(e, "PE")
                    if grp == 0:
                        C.wait(e, "SC")
                        C.wait(e, "DV")
                        C.wait(e, "ST")
                    C.inc(e.dma_start(out=qgrp[:], in_=qscr[grp]), "LD2", 16)
                C.y(y)
                qld_tok[grp] = C.tok("LD2")

                for pc in range(8):
                    sc_base = C.cnt["SC"]
                    for tc in range(NTILE):
                        exp_tok[(grp, pc, tc)] = sc_base + tc + 1
                    st_pe_l = [0] * NTILE

                    def mk_st(tc, pc=pc, grp=grp):
                        buf = tc % 2

                        def t(e, grp=grp, pc=pc, tc=tc, buf=buf):
                            if pc == 0 and tc == 0:
                                C.wait(e, "DV", t_negm)
                                C.wait(e, "LD2", qld_tok[grp])
                            prev = exp_tok.get((grp, pc, tc - 2)) or (
                                exp_tok.get((grp, pc - 1, NTILE - 2))
                                if tc < 2 else None)
                            if prev:
                                C.wtok(e, "SC", prev)
                            e.matmul(
                                p_sp[:, buf * 1024: buf * 1024 + 512],
                                kp[0:64, pc * S + tc * 128:
                                   pc * S + (tc + 1) * 128],
                                qgrp[0:64, pc * 512:(pc + 1) * 512],
                                start=True, stop=True, tile_position=(0, 0))
                            C.inc(e.matmul(
                                p_sp[:, buf * 1024 + 512: buf * 1024 + 1024],
                                kp[64:128, pc * S + tc * 128:
                                   pc * S + (tc + 1) * 128],
                                qgrp[64:128, pc * 512:(pc + 1) * 512],
                                start=True, stop=True,
                                tile_position=(64, 0)), "PE", 1)
                        C.t(t)
                        st_pe_l[tc] = C.tok("PE")

                    def mk_pv(tc, pc=pc, grp=grp):
                        buf = tc % 2

                        def t2(e, pc=pc, tc=tc, buf=buf, grp=grp):
                            C.wtok(e, "SC", exp_tok[(grp, pc, tc)])
                            if tc == 0:
                                prev = auev_tok.get((grp, pc - 1)) or (
                                    auev_tok.get((grp - 1, 7)) if grp > 0
                                    else None)
                                if prev:
                                    C.wtok(e, prev[0], prev[1])
                            e.matmul(p_o[0:65, 0:512],
                                     vst[:, tc * H * 65 + (2 * pc) * 65:
                                         tc * H * 65 + (2 * pc) * 65 + 65],
                                     pt[:, buf * 1024: buf * 1024 + 512],
                                     start=(tc == 0), stop=(tc == NTILE - 1),
                                     skip_group_check=True)
                            C.inc(e.matmul(
                                p_o[0:65, 512:1024],
                                vst[:, tc * H * 65 + (2 * pc + 1) * 65:
                                    tc * H * 65 + (2 * pc + 1) * 65 + 65],
                                pt[:, buf * 1024 + 512: buf * 1024 + 1024],
                                start=(tc == 0), stop=(tc == NTILE - 1),
                                skip_group_check=True), "PE", 1)
                        C.t(t2)

                    mk_st(0)
                    for tc in range(1, NTILE):
                        mk_st(tc)
                        mk_pv(tc - 1)
                    mk_pv(NTILE - 1)
                    pv_tok[(grp, pc)] = C.tok("PE")

                    def s(e, st_pe_l=tuple(st_pe_l), sc_base=sc_base):
                        for tc in range(NTILE):
                            buf = tc % 2
                            C.wtok(e, "PE", st_pe_l[tc])
                            ins = e.activation(pt[:, buf * 1024:
                                                  (buf + 1) * 1024],
                                               p_sp[:, buf * 1024:
                                                    (buf + 1) * 1024],
                                               AF.Exp, bias=negm[:, 0:1])
                            C.inc(ins, "SC", 1)
                            assert C.cnt["SC"] == sc_base + tc + 1
                    C.s(s)

                    def vv(e, pc=pc, grp=grp):
                        C.wtok(e, "PE", pv_tok[(grp, pc)])
                        prev = anrm_tok.get((grp, pc - 1)) or (
                            anrm_tok.get((grp - 1, 7)) if grp > 0 else None)
                        if prev:
                            C.wtok(e, "DV", prev)
                        e.tensor_copy(au[0:65, 0:512], p_o[0:65, 0:512])
                        C.inc(e.tensor_copy(au[0:65, 512:1024],
                                            p_o[0:65, 512:1024]), "DV", 1)
                    C.v(vv)
                    auev_tok[(grp, pc)] = ("DV", C.tok("DV"))

                    for hh_i in range(2):
                        h = 2 * pc + hh_i
                        for sub in range(4):
                            ab = (hh_i * 4 + sub) % 2

                            def t(e, hh_i=hh_i, sub=sub, ab=ab, grp=grp,
                                  pc=pc):
                                nm_, v_ = auev_tok[(grp, pc)]
                                C.wtok(e, nm_, v_)
                                prev = anrm_tok.get((grp, pc, hh_i, sub - 2)) \
                                    or (anrm_tok.get((grp, pc, hh_i - 1,
                                                      sub + 2))
                                        if sub < 2 else None)
                                if prev:
                                    C.wtok(e, "DV", prev)
                                else:
                                    C.wait(e, "DV")
                                C.inc(e.transpose(
                                    p_at[:, ab * 1024: ab * 1024 + 65],
                                    au[0:65, hh_i * 512 + sub * 128:
                                       hh_i * 512 + (sub + 1) * 128],
                                    k["ident65"][:]), "PE", 1)
                            C.t(t)
                            at_t = C.tok("PE")

                            def v(e, sub=sub, h=h, ab=ab, at_t=at_t,
                                  hh_i=hh_i, grp=grp, pc=pc):
                                pr = anrm_tok.get((grp, pc, hh_i, sub - 1)) \
                                    or (anrm_tok.get((grp, pc, hh_i - 1, 3))
                                        if sub == 0 else None)
                                if pr:
                                    C.wtok(e, "DV", pr)
                                C.wtok(e, "PE", at_t)
                                C.fence(e, e.reciprocal(
                                    k["st8"][:, 7:8],
                                    p_at[:, ab * 1024 + 64: ab * 1024 + 65]),
                                    "DV")
                                C.inc(e.tensor_scalar(
                                    an[:, sub * D + h * E:
                                       sub * D + (h + 1) * E],
                                    p_at[:, ab * 1024: ab * 1024 + 64],
                                    k["st8"][:, 7:8], None, op0=ALU.mult),
                                    "DV", 1)
                            C.v(v)
                            anrm_tok[(grp, pc, hh_i, sub)] = C.tok("DV")
                        anrm_tok[(grp, pc)] = C.tok("DV")

                for sub in range(4):
                    i = grp * 4 + sub
                    sb = sub % 2

                    def s(e, sub=sub, sb=sb, grp=grp):
                        C.wtok(e, "DV", anrm_tok[(grp, 7)])
                        if sub >= 2:
                            C.wtok(e, "DV", aft_tok[sub - 2])
                        C.inc(e.activation(ag[:, sb * D:(sb + 1) * D],
                                           an[:, sub * D:(sub + 1) * D],
                                           AF.Gelu), "SC", 1)
                    C.s(s)
                    agl_tok[sub] = C.tok("SC")

                    def t(e, sub=sub, sb=sb):
                        C.wtok(e, "SC", agl_tok[sub])
                        if sub >= 2:
                            C.wtok(e, "DV", aft_tok[sub - 2])
                        for c in range(8):
                            ins = e.transpose(
                                p_at_tr(p_sp, sb, c),
                                ag[:, sb * D + c * 128: sb * D + (c + 1) * 128],
                                k["ident"][:])
                        C.inc(ins, "PE", 1)
                    C.t(t)
                    trf_t = C.tok("PE")

                    def v(e, sub=sub, sb=sb, trf_t=trf_t):
                        C.wtok(e, "PE", trf_t)
                        for c in range(8):
                            ins = e.tensor_copy(
                                af[:, sb * D + c * 128: sb * D + (c + 1) * 128],
                                p_at_tr(p_sp, sb, c))
                        C.inc(ins, "DV", 1)
                    C.v(v)
                    aft_tok[sub] = C.tok("DV")

                    def s2(e, i=i):
                        C.wait(e, "SC")
                        C.wait(e, "DV")
                        C.inc(e.activation(k["scr_b"][:],
                                           xs[:, i * D:(i + 1) * D],
                                           AF.Identity,
                                           scale=ap6[:, i, 4:5],
                                           bias=ap6[:, i, 5:6]), "SC", 1)
                    C.s(s2)
                    h2r_t = C.tok("SC")

                    for hh in range(2):
                        def t2(e, sub=sub, sb=sb, i=i, hh=hh):
                            C.wtok(e, "DV", aft_tok[sub])
                            C.wait(e, "LD", t_wp)
                            if i > 0 or hh > 0:
                                C.wait(e, "DV")
                            for fc in range(8):
                                ins = e.matmul(
                                    p_w,
                                    af[:, sb * D + fc * 128:
                                       sb * D + (fc + 1) * 128],
                                    wp[:, fc * D + hh * 512:
                                       fc * D + (hh + 1) * 512],
                                    start=(fc == 0), stop=(fc == 7))
                            C.inc(ins, "PE", 1)
                        C.t(t2)
                        wp_pe = C.tok("PE")

                        def v2(e, hh=hh, h2r_t=h2r_t, wp_pe=wp_pe):
                            C.wtok(e, "PE", wp_pe)
                            C.wtok(e, "SC", h2r_t)
                            C.inc(e.tensor_tensor(
                                k["scr_a"][:, hh * 512:(hh + 1) * 512], p_w,
                                k["scr_b"][:, hh * 512:(hh + 1) * 512],
                                op=ALU.add), "DV", 1)
                        C.v(v2)

                    emit_ln_post(C, k["scr_a"][:], xs[:, i * D:(i + 1) * D])
                    wpo_tok[i] = C.tok("SC")


def qg_dst(qgb, oc):
    return qgb[:, oc * 512:(oc + 1) * 512]


def p_at_tr(p_sp, sb, c):
    return p_sp[:].bitcast(BF16)[:, sb * 1024 + c * 128:
                                 sb * 1024 + (c + 1) * 128]


def emit_pre_stats(C, lnp):
    k = C.k
    xs = C.xs
    ap6 = lnp[:].rearrange("p (i c) -> p i c", c=6)

    def v(e):
        C.wait(e, "LD2")
        C.wait(e, "DV")
        for i in range(NTILE):
            ins = e.tensor_reduce(lnp[:, 6 * i: 6 * i + 1],
                                  xs[:, i * D:(i + 1) * D], axis=AX.X,
                                  op=ALU.add)
        C.inc(ins, "DV", 1)
    C.v(v)
    tsum = C.tok("DV")

    def s(e, tsum=tsum):
        C.wait(e, "LD2")
        C.wait(e, "SC")
        C.wait(e, "DV", tsum)
        for i in range(NTILE):
            ins = e.activation(k["scr_a" if i % 2 else "scr_b"][:],
                               xs[:, i * D:(i + 1) * D],
                               AF.Square, accum_out=lnp[:, 6 * i + 1:
                                                        6 * i + 2])
            if i < NTILE - 1:
                C.fence(e, ins, "SC")
        C.inc(ins, "SC", 1)
    C.s(s)
    tsq = C.tok("SC")

    def v2(e, tsq=tsq):
        C.wait(e, "SC", tsq)
        C.fence(e, e.tensor_scalar(ap6[:, :, 2:3], ap6[:, :, 0:1], 1.0 / D,
                                   None, op0=ALU.mult), "DV")
        C.fence(e, e.tensor_scalar(ap6[:, :, 3:4], ap6[:, :, 1:2], 1.0 / D,
                                   None, op0=ALU.mult), "DV")
        C.fence(e, e.scalar_tensor_tensor(ap6[:, :, 4:5], ap6[:, :, 2:3],
                                          -1.0, ap6[:, :, 2:3], op0=ALU.mult,
                                          op1=ALU.mult), "DV")
        C.inc(e.tensor_tensor(ap6[:, :, 3:4], ap6[:, :, 3:4], ap6[:, :, 4:5],
                              op=ALU.add), "DV", 1)
    C.v(v2)
    t3 = C.tok("DV")

    def s2(e, t3=t3):
        C.wait(e, "DV", t3)
        C.fence(e, e.activation(ap6[:, :, 4:5], ap6[:, :, 3:4], AF.Ln,
                                bias=k["eps_v"][:, 0:1]), "SC")
        C.inc(e.activation(ap6[:, :, 4:5], ap6[:, :, 4:5], AF.Exp,
                           scale=-0.5), "SC", 1)
    C.s(s2)
    t4 = C.tok("SC")

    def v3(e, t4=t4):
        C.wait(e, "SC", t4)
        C.inc(e.scalar_tensor_tensor(ap6[:, :, 5:6], ap6[:, :, 2:3], -1.0,
                                     ap6[:, :, 4:5], op0=ALU.mult,
                                     op1=ALU.mult), "DV", 1)
    C.v(v3)


def emit_l1f(C, SCR):
    nc = C.nc
    k = C.k
    xs = C.xs
    if getattr(C, "f1_prep_pending", False):
        C.f1_prep_pending = False
        emit_weight_prep(C, SCR, [("f1_wg", HID, D), ("f1_wu", HID, D),
                                  ("f1_wd", D, HID)])
    HH2 = HID // 2
    with ExitStack() as ph:
        ec = ph.enter_context
        wgh = ec(nc.sbuf_tensor(C.u("f1_wgh"), [128, DC * HH2], BF16))
        wuh = ec(nc.sbuf_tensor(C.u("f1_wuh"), [128, DC * HH2], BF16))
        wdb = ec(nc.sbuf_tensor(C.u("f1_wdb"), [128, 2 * 4 * 512], BF16))
        h3f = ec(nc.sbuf_tensor(C.u("f1_h3f"), [128, DC * 512], BF16))
        hh = ec(nc.sbuf_tensor(C.u("f1_hh"), [128, HC * 512], BF16))
        sg = ec(nc.sbuf_tensor(C.u("f1_sg"), [128, 2 * 512], BF16))
        od = ec(nc.sbuf_tensor(C.u("f1_od"), [128, 4 * D], F32))
        lnp = ec(nc.sbuf_tensor(C.u("f1_lnp"), [128, 6 * NTILE], F32))
        htm2 = ec(nc.sbuf_tensor(C.u("f1_htm2"), [128, 2 * D], BF16))
        p_big = ec(nc.psum_tensor(C.u("f1_p_big"), [128, 4 * 512], F32))
        p_g = ec(nc.psum_tensor(C.u("f1_p_g"), [128, 2 * 512], F32))
        p_u = ec(nc.psum_tensor(C.u("f1_p_u"), [128, 2 * 512], F32))
        p_trv = p_big[:].bitcast(BF16)

        emit_pre_stats(C, lnp)
        ap6 = lnp[:].rearrange("p (i c) -> p i c", c=6)
        t_stats = C.tok("DV")

        tr_tok = [0] * NTILE
        dn_tok = {}
        odv_tok = {}
        lnq_tok = {}
        hhev_tok = {}
        wl_tok = {}
        gu_tok = {}
        for grp in range(NGRP):
            for ti in range(4):
                i = grp * 4 + ti

                def s(e, i=i):
                    C.wait(e, "DV", t_stats)
                    if i >= 2:
                        C.wtok(e, "DV", tr_tok[i - 2])
                    C.inc(e.activation(htm2[:, (i % 2) * D:(i % 2 + 1) * D],
                                       xs[:, i * D:(i + 1) * D], AF.Identity,
                                       scale=ap6[:, i, 4:5],
                                       bias=ap6[:, i, 5:6]), "SC", 1)
                C.s(s)
                ap_t = C.tok("SC")

                def t(e, i=i, ti=ti, ap_t=ap_t, grp=grp):
                    C.wtok(e, "SC", ap_t)
                    if i >= 2:
                        C.wtok(e, "DV", tr_tok[i - 2])
                    if grp > 0 and ti < 2:
                        C.wtok(e, "SC", odv_tok[(grp - 1, 3)])
                    for c in range(8):
                        ins = e.transpose(
                            p_trv[:, (i % 2) * 1024 + c * 128:
                                  (i % 2) * 1024 + (c + 1) * 128],
                            htm2[:, (i % 2) * D + c * 128:
                                 (i % 2) * D + (c + 1) * 128],
                            k["ident"][:])
                    C.inc(ins, "PE", 1)
                C.t(t)
                pe_t = C.tok("PE")

                def v(e, i=i, ti=ti, pe_t=pe_t):
                    C.wtok(e, "PE", pe_t)
                    for c in range(8):
                        ins = e.tensor_copy(
                            h3f[:, c * 512 + ti * 128: c * 512 + (ti + 1) * 128],
                            p_trv[:, (i % 2) * 1024 + c * 128:
                                  (i % 2) * 1024 + (c + 1) * 128])
                    C.inc(ins, "DV", 1)
                C.v(v)
                tr_tok[i] = C.tok("DV")

            for half in range(2):
                def y(e, grp=grp, half=half):
                    prev = gu_tok.get((grp, half * HHALF_G - 1)) if half else \
                        (gu_tok.get((grp - 1, HC - 1)) if grp > 0 else None)
                    if prev:
                        C.wtok(e, "PE", prev)
                    for dc in range(8):
                        C.inc(e.dma_start(
                            out=wgh[:, dc * HH2:(dc + 1) * HH2],
                            in_=C.scr["f1_wg_t"][dc * 128:(dc + 1) * 128,
                                                 half * HH2:(half + 1) * HH2]),
                            "WL", 16)
                        C.inc(e.dma_start(
                            out=wuh[:, dc * HH2:(dc + 1) * HH2],
                            in_=C.scr["f1_wu_t"][dc * 128:(dc + 1) * 128,
                                                 half * HH2:(half + 1) * HH2]),
                            "WL", 16)
                C.y(y)
                wl_tok[(grp, half)] = C.tok("WL")

                sc_base = C.cnt["SC"]
                sl_toks = {}
                gu_l = [0] * (HC // 2)
                for hc_l in range(HC // 2):
                    hc = half * (HC // 2) + hc_l
                    sl_toks[hc] = sc_base + hc_l + 1
                for hc_l in range(HC // 2):
                    hc = half * (HC // 2) + hc_l
                    pb = hc % 2

                    def t(e, grp=grp, half=half, hc_l=hc_l, hc=hc, pb=pb):
                        if hc_l == 0:
                            C.wait(e, "WL", wl_tok[(grp, half)])
                            C.wtok(e, "DV", tr_tok[grp * 4 + 3])
                        prev = hhev_tok.get((grp, hc - 2)) or (
                            hhev_tok.get((grp - 1, HC - 2 + hc))
                            if grp > 0 and hc < 2 else None)
                        if prev:
                            C.wtok(e, "DV", prev)
                        for dc in range(8):
                            e.matmul(p_g[:, pb * 512:(pb + 1) * 512],
                                     wgh[:, dc * HH2 + hc_l * 128:
                                         dc * HH2 + (hc_l + 1) * 128],
                                     h3f[:, dc * 512:(dc + 1) * 512],
                                     start=(dc == 0), stop=(dc == 7))
                        for dc in range(8):
                            ins = e.matmul(
                                p_u[:, pb * 512:(pb + 1) * 512],
                                wuh[:, dc * HH2 + hc_l * 128:
                                    dc * HH2 + (hc_l + 1) * 128],
                                h3f[:, dc * 512:(dc + 1) * 512],
                                start=(dc == 0), stop=(dc == 7))
                        C.inc(ins, "PE", 1)
                    C.t(t)
                    gu_tok[(grp, hc)] = C.tok("PE")
                    gu_l[hc_l] = C.tok("PE")

                    def v(e, hc=hc, pb=pb, sl_t=sl_toks[hc]):
                        C.wtok(e, "SC", sl_t)
                        C.inc(e.tensor_tensor(hh[:, hc * 512:(hc + 1) * 512],
                                              sg[:, pb * 512:(pb + 1) * 512],
                                              p_u[:, pb * 512:(pb + 1) * 512],
                                              op=ALU.mult), "DV", 1)
                    C.v(v)
                    hhev_tok[(grp, hc)] = C.tok("DV")

                def s(e, gu_l=tuple(gu_l), half=half, sc_base=sc_base):
                    for hc_l in range(HC // 2):
                        hc = half * (HC // 2) + hc_l
                        pb = hc % 2
                        C.wtok(e, "PE", gu_l[hc_l])
                        C.inc(e.activation(sg[:, pb * 512:(pb + 1) * 512],
                                           p_g[:, pb * 512:(pb + 1) * 512],
                                           AF.Silu, scale=1.0 / 127), "SC", 1)
                        assert C.cnt["SC"] == sc_base + hc_l + 1
                C.s(s)

            for dhalf in range(2):
                for hb in range(HC // 4):
                    buf = hb % 2

                    def y(e, grp=grp, dhalf=dhalf, hb=hb, buf=buf):
                        prev = dn_tok.get((grp, dhalf, hb - 2)) or (
                            dn_tok.get((grp, dhalf - 1, HC // 4 - 2 + hb))
                            if hb < 2 and dhalf > 0 else
                            (dn_tok.get((grp - 1, 1, HC // 4 - 2 + hb))
                             if hb < 2 and dhalf == 0 and grp > 0 else None))
                        if prev:
                            C.wtok(e, "PE", prev)
                        C.inc(e.dma_start(
                            out=wdb[:, buf * 2048:(buf + 1) * 2048].rearrange(
                                "p (c d) -> p c d", d=512),
                            in_=C.scr["f1_wd_t"][hb * 512:(hb + 1) * 512,
                                                 dhalf * 512:(dhalf + 1) * 512
                                                 ].rearrange(
                                "(c p) d -> p c d", p=128)),
                            "WD%d" % buf, 16)
                    C.y(y)
                    wd_t = C.tok("WD%d" % buf)

                    for hc_l in range(4):
                        hc = hb * 4 + hc_l

                        def t(e, grp=grp, dhalf=dhalf, hc=hc, hc_l=hc_l,
                              buf=buf, wd_t=wd_t):
                            if hc_l == 0:
                                C.wait(e, "WD%d" % buf, wd_t)
                            if hc == 0:
                                C.wtok(e, "DV", hhev_tok[(grp, HC - 1)])
                                if dhalf == 0:
                                    C.wtok(e, "DV", tr_tok[grp * 4 + 3])
                                else:
                                    C.wtok(e, "SC", odv_tok[(grp, 3, 0)])
                            for ti in range(4):
                                ins = e.matmul(
                                    p_big[:, ti * 512:(ti + 1) * 512],
                                    hh[:, hc * 512 + ti * 128:
                                       hc * 512 + (ti + 1) * 128],
                                    wdb[:, buf * 2048 + hc_l * 512:
                                        buf * 2048 + (hc_l + 1) * 512],
                                    start=(hc == 0), stop=(hc == HC - 1),
                                    skip_group_check=True)
                            C.inc(ins, "PE", 1)
                        C.t(t)
                    dn_tok[(grp, dhalf, hb)] = C.tok("PE")

                for ti in range(4):
                    def s(e, grp=grp, ti=ti, dhalf=dhalf):
                        C.wtok(e, "PE", dn_tok[(grp, dhalf, HC // 4 - 1)])
                        if grp > 0 and dhalf == 0:
                            C.wtok(e, "DV", lnq_tok[(grp - 1, ti)])
                        C.inc(e.activation(
                            od[:, ti * D + dhalf * 512: ti * D + (dhalf + 1) * 512],
                            p_big[:, ti * 512:(ti + 1) * 512], AF.Identity,
                            scale=1.0 / (127.0 * 127.0)), "SC", 1)
                    C.s(s)
                    odv_tok[(grp, ti, dhalf)] = C.tok("SC")
                odv_tok[(grp, 3)] = C.tok("SC")

            for ti in range(4):
                i = grp * 4 + ti

                def dummy(e, grp=grp, ti=ti):
                    C.wtok(e, "SC", odv_tok[(grp, ti, 1)])
                    C.inc(e.engine_nop(), "DV", 1)
                C.v(dummy)
                emit_ln_post(C, od[:, ti * D:(ti + 1) * D],
                             xs[:, i * D:(i + 1) * D])
                lnq_tok[(grp, ti)] = C.tok("DV")


HHALF_G = HC // 2


def build_program(phases):
    nc = bass.Bass("TRN2", target_bir_lowering=False)

    xin = nc.dram_tensor("x", [NSEQ, S, D], F32, kind="ExternalInput").ap()
    out = nc.dram_tensor("out", [NSEQ, S, D], F32, kind="ExternalOutput").ap()
    dbg = nc.dram_tensor("dbg", [NSEQ, 4, 128, D], F32, kind="ExternalOutput").ap()
    W = {}
    for nm, shp, dt in [
        ("l0_wq", [D, D], F32), ("l0_wk", [D, D], F32), ("l0_wv", [D, D], F32),
        ("l0_bq", [D], F32), ("l0_bk", [D], F32), ("l0_bv", [D], F32),
        ("f0_wr", [D, D], F32), ("f0_wk", [D, D], F32), ("f0_wv", [D, D], F32),
        ("f0_mur", [D], F32), ("f0_muk", [D], F32),
        ("l1_wq", [D, D], F32), ("l1_wk", [D, D], F32), ("l1_wv", [D, D], F32),
        ("l1_wp", [D, D], F32),
        ("f1_wg", [HID, D], I32), ("f1_wu", [HID, D], I32),
        ("f1_wd", [D, HID], I32),
    ]:
        W[nm] = nc.dram_tensor(nm, shp, dt, kind="ExternalInput").ap()

    need_f0 = "l0f" in phases
    need_f1 = "l1f" in phases
    SCR = {}
    if "l1a" in phases:
        SCR["qscr"] = nc.dram_tensor("qscr", [NGRP, 128, DC * 512], BF16,
                                     kind="Internal").ap()
    if need_f0:
        for nm in ("f0_wr", "f0_wk", "f0_wv"):
            SCR[nm + "_s1"] = nc.dram_tensor("s1_" + nm, [D, D], BF16,
                                             kind="Internal").ap()
            SCR[nm + "_t"] = nc.dram_tensor("t_" + nm, [D, D], BF16,
                                            kind="Internal").ap()
    if need_f1:
        for nm, R, Cc in (("f1_wg", HID, D), ("f1_wu", HID, D),
                          ("f1_wd", D, HID)):
            SCR[nm + "_s1"] = nc.dram_tensor("s1_" + nm, [R, Cc], BF16,
                                             kind="Internal").ap()
            SCR[nm + "_t"] = nc.dram_tensor("t_" + nm, [Cc, R], BF16,
                                            kind="Internal").ap()

    with ExitStack() as stack:
        ec = stack.enter_context
        xs = ec(nc.sbuf_tensor("xs", [128, NTILE * D], F32))
        ident = ec(nc.sbuf_tensor("ident", [128, 128], BF16))
        ident65 = ec(nc.sbuf_tensor("ident65", [65, 65], BF16))
        ones_col = ec(nc.sbuf_tensor("ones_col", [128, 1], BF16))
        ones_row = ec(nc.sbuf_tensor("ones_row", [1, 128], BF16))
        cmask = ec(nc.sbuf_tensor("cmask", [128, 128], F32))
        cnt16 = ec(nc.sbuf_tensor("cnt16", [128, NTILE], F32))
        eps_v = ec(nc.sbuf_tensor("eps_v", [128, 1], F32))
        st8 = ec(nc.sbuf_tensor("st8", [128, 8], F32))
        scr_a = ec(nc.sbuf_tensor("scr_a", [128, D], F32))
        scr_b = ec(nc.sbuf_tensor("scr_b", [128, D], F32))

        block = ec(nc.Block())
        C = Ctx(nc, block, stack)
        C.dram = {"x": xin, "out": out, "dbg": dbg, **W}
        C.scr = SCR
        C.xs = xs
        C.k = dict(ident=ident, ident65=ident65, ones_col=ones_col,
                   ones_row=ones_row, cmask=cmask, cnt16=cnt16, eps_v=eps_v,
                   st8=st8, scr_a=scr_a, scr_b=scr_b)

        emit_consts(C)
        C.f0_prep_pending = need_f0
        C.f1_prep_pending = need_f1

        for seq in range(NSEQ):
            C.seq = seq
            emit_load_x(C, seq)
            C.barrier()
            if "l0a" in phases:
                emit_l0a(C)
                C.barrier()
            if "l0f" in phases:
                emit_l0f(C, SCR)
                C.barrier()
            if "l1a" in phases:
                emit_l1a(C)
                C.barrier()
            if "l1f" in phases:
                emit_l1f(C, SCR)
                C.barrier()
            emit_store_x(C, seq)
            C.barrier()
    return nc


def in_map_for_core(inputs, core):
    x = np.ascontiguousarray(inputs["L0_attn_X"][2 * core: 2 * core + 2])
    m = {"x": x}
    ren = {"l0_wq": "L0_attn_Wq", "l0_wk": "L0_attn_Wk", "l0_wv": "L0_attn_Wv",
           "l0_bq": "L0_attn_bq", "l0_bk": "L0_attn_bk", "l0_bv": "L0_attn_bv",
           "f0_wr": "L0_ffn_W_r", "f0_wk": "L0_ffn_W_k", "f0_wv": "L0_ffn_W_v",
           "f0_mur": "L0_ffn_mu_r", "f0_muk": "L0_ffn_mu_k",
           "l1_wq": "L1_attn_Wq", "l1_wk": "L1_attn_Wk", "l1_wv": "L1_attn_Wv",
           "l1_wp": "L1_attn_Wpost",
           "f1_wg": "L1_ffn_w_gate", "f1_wu": "L1_ffn_w_up",
           "f1_wd": "L1_ffn_w_down"}
    for a, b in ren.items():
        m[a] = np.ascontiguousarray(inputs[b])
    return m


def kernel(**inputs):
    nc = build_program(("l0a", "l0f", "l1a", "l1f"))
    in_maps = [in_map_for_core(inputs, c) for c in range(8)]
    res = run_bass_kernel_spmd(nc, in_maps, core_ids=list(range(8)))
    return np.concatenate([np.asarray(r["out"]) for r in res.results], axis=0)


# version 24
# speedup vs baseline: 1.1977x; 1.0100x over previous
import numpy as np
from contextlib import ExitStack

import concourse.bass as bass
import concourse.mybir as mybir
from concourse.bass_utils import run_bass_kernel_spmd

F32 = mybir.dt.float32
BF16 = mybir.dt.bfloat16
I32 = mybir.dt.int32
AF = mybir.ActivationFunctionType
ALU = mybir.AluOpType
AX = mybir.AxisListType

NTILE = 16
NGRP = 4
D = 1024
DC = 8
S = 2048
HID = 3072
HC = 24
NSEQ = 2
EPS = 1e-5
H, E = 16, 64

_SECT = {"tensor": "PE", "scalar": "SC", "vector": "DV", "gpsimd": "GP",
         "sync": "SY"}


class Ctx:
    def __init__(self, nc, block, stack):
        self.nc = nc
        self.block = block
        self.stack = stack
        self.sem = {}
        self.cnt = {}
        for name in ("LD", "LD2", "ST", "PE", "SC", "DV", "GP", "SY",
                     "WD0", "WD1", "WL", "WP", "WP2"):
            self.sem[name] = stack.enter_context(nc.semaphore(name))
            self.cnt[name] = 0
        self.bar_s = stack.enter_context(nc.sbuf_tensor("bar_s", [1, 8], F32))

    def t(self, f):
        self.block.tensor(f)

    def s(self, f):
        self.block.scalar(f)

    def v(self, f):
        self.block.vector(f)

    def g(self, f):
        self.block.gpsimd(f)

    def y(self, f):
        self.block.sync(f)

    def u(self, name):
        self._uid = getattr(self, "_uid", 0) + 1
        return f"{name}__{self._uid}"

    def inc(self, instr, name, n=1):
        instr.then_inc(self.sem[name], n)
        self.cnt[name] += n
        return self.cnt[name]

    def wait(self, eng, name, val=None):
        eng.wait_ge(self.sem[name], self.cnt[name] if val is None else val)

    def fence(self, e, instr, name):
        self.inc(instr, name, 1)
        e.wait_ge(self.sem[name], self.cnt[name])

    def tok(self, name):
        return self.cnt[name]

    def wtok(self, e, name, val):
        if val > 0:
            e.wait_ge(self.sem[name], val)

    def barrier(self):
        def mk_inc(engname):
            def f(e):
                if engname == "scalar":
                    ins = e.memzero(self.bar_s[:])
                elif engname in ("vector", "gpsimd"):
                    ins = e.engine_nop()
                else:
                    ins = e.nop()
                self.inc(ins, _SECT[engname], 1)
            return f
        for e in ("scalar", "vector", "gpsimd", "sync"):
            getattr(self, e[0] if e != "sync" else "y")(mk_inc(e))
        snap = dict(self.cnt)

        def mk_wait():
            def f(e):
                for nm, v in snap.items():
                    if v > 0:
                        e.wait_ge(self.sem[nm], v)
            return f
        for e in ("tensor", "scalar", "vector", "gpsimd", "sync"):
            getattr(self, e[0] if e != "sync" else "y")(mk_wait())


def emit_consts(C):
    k = C.k

    def g(e):
        C.fence(e, e.memset(k["ident"][:], 0.0), "GP")
        e.affine_select(out=k["ident"][:], in_=k["ident"][:],
                        compare_op=ALU.not_equal, fill=1.0,
                        base=0, pattern=[[-1, 128]], channel_multiplier=1)
        C.fence(e, e.memset(k["ident65"][:], 0.0), "GP")
        e.affine_select(out=k["ident65"][:], in_=k["ident65"][:],
                        compare_op=ALU.not_equal, fill=1.0,
                        base=0, pattern=[[-1, 65]], channel_multiplier=1)
        e.memset(k["ones_col"][:], 1.0)
        e.memset(k["ones_row"][:], 1.0)
        C.fence(e, e.memset(k["cmask"][:], 0.0), "GP")
        e.affine_select(out=k["cmask"][:], in_=k["cmask"][:],
                        compare_op=ALU.is_ge, fill=-1.0e9,
                        base=0, pattern=[[-1, 128]], channel_multiplier=1)
        e.iota(k["cnt16"][:], [[-128, NTILE]], base=2047, channel_multiplier=-1,
               allow_small_or_imprecise_dtypes=True)
        e.memset(k["eps_v"][:], EPS)
        C.inc(e.engine_nop(), "GP", 1)
    C.g(g)


def emit_weight_prep(C, SCR, jobs, stgw=None):
    if not jobs:
        return
    nc = C.nc
    stgw = stgw or 12 * D
    stgtw = max(R for _, R, _ in jobs)
    pstack = ExitStack()
    stg = pstack.enter_context(nc.sbuf_tensor(C.u("wprep"),
                                              [128, max(stgw, stgtw)], BF16))
    stgT = stg

    for nm, R, Cc in jobs:
        src = C.dram[nm]
        scr1 = SCR[nm + "_s1"]
        scr2 = SCR[nm + "_t"]
        nch = R // 128

        hchunks = max(1, stgw // Cc)
        for c0 in range(0, nch, hchunks):
            c1 = min(c0 + hchunks, nch)

            def g(e, src=src, c0=c0, c1=c1, Cc=Cc):
                C.wait(e, "WP2")
                C.inc(e.dma_start(
                    out=stg[:, 0:(c1 - c0) * Cc].rearrange(
                        "p (c d) -> p c d", d=Cc),
                    in_=src.rearrange("(c p) d -> p c d",
                                      p=128)[:, c0:c1]),
                    "WP", 16)
            C.g(g)

            def y1(e, scr1=scr1, c0=c0, c1=c1, Cc=Cc):
                C.wait(e, "WP")
                C.inc(e.dma_start(
                    out=scr1.rearrange("(c p) d -> p c d", p=128)[:, c0:c1],
                    in_=stg[:, 0:(c1 - c0) * Cc].rearrange(
                        "p (c d) -> p c d", d=Cc)),
                    "WP2", 16)
            C.y(y1)

        for cc in range(Cc // 128):
            def y2(e, scr1=scr1, scr2=scr2, cc=cc, R=R):
                C.wait(e, "WP2")
                C.inc(e.dma_start(out=stgT[:, 0:R],
                                  in_=scr1[:, cc * 128:(cc + 1) * 128],
                                  transpose=True), "WP2", 16)
                C.wait(e, "WP2")
                C.inc(e.dma_start(out=scr2[cc * 128:(cc + 1) * 128, :],
                                  in_=stgT[:, 0:R]), "WP2", 16)
            C.y(y2)
    pstack.close()


def emit_load_x(C, seq):
    def y(e):
        C.inc(e.dma_start(out=C.xs[:].rearrange("p (i d) -> p i d", d=D),
                          in_=C.dram["x"][seq].rearrange("(i p) d -> p i d",
                                                         p=128)),
              "LD2", 16)
    C.y(y)


def emit_store_x(C, seq):
    def y(e):
        C.inc(e.dma_start(
            out=C.dram["out"][seq].rearrange("(i p) d -> p i d", p=128),
            in_=C.xs[:].rearrange("p (i d) -> p i d", d=D)), "ST", 16)
    C.y(y)


def emit_transpose_1024(C, src_fn, psum_tr, dst_fn, nchunk=8):
    k = C.k

    def t(e):
        C.wait(e, "DV")
        C.wait(e, "SC")
        for c in range(nchunk):
            ins = e.transpose(psum_tr[:, c * 128:(c + 1) * 128], src_fn(c),
                              k["ident"][:])
        C.inc(ins, "PE", 1)
    C.t(t)

    def v(e):
        C.wait(e, "PE")
        for c in range(nchunk):
            ins = e.tensor_copy(dst_fn(c), psum_tr[:, c * 128:(c + 1) * 128])
        C.inc(ins, "DV", 1)
    C.v(v)


L0A_STAGE = 5

def emit_l0a(C):
    nc = C.nc
    k = C.k
    xs = C.xs
    with ExitStack() as ph:
        ec = ph.enter_context
        wq = ec(nc.sbuf_tensor(C.u("l0_wq_s"), [128, DC * D], BF16))
        wk = ec(nc.sbuf_tensor(C.u("l0_wk_s"), [128, DC * D], BF16))
        wv = ec(nc.sbuf_tensor(C.u("l0_wv_s"), [128, DC * D], BF16))
        bq = ec(nc.sbuf_tensor(C.u("l0_bq_s"), [128, 8], F32))
        bk = ec(nc.sbuf_tensor(C.u("l0_bk_s"), [128, 8], F32))
        bvr_bf = ec(nc.sbuf_tensor(C.u("l0_bv_row_bf"), [1, D], BF16))
        bvb = ec(nc.sbuf_tensor(C.u("l0_bv_bc"), [128, D], BF16))
        h0 = ec(nc.sbuf_tensor(C.u("h0_fm"), [128, DC * S], BF16))
        kf = ec(nc.sbuf_tensor(C.u("k_fm"), [128, DC * S], BF16))
        qg = ec(nc.sbuf_tensor(C.u("q_fm_g"), [128, DC * 512], BF16))
        e_scr = ec(nc.sbuf_tensor(C.u("e_scr"), [128, S], BF16))
        sm = ec(nc.sbuf_tensor(C.u("l0_sm"), [128, 8], F32))
        lnp = ec(nc.sbuf_tensor(C.u("l0_lnp"), [128, 6 * NTILE], F32))
        htm2 = ec(nc.sbuf_tensor(C.u("l0_htm2"), [128, 2 * D], BF16))

        def g(e):
            C.wait(e, "LD2")
            for nm, dst in (("l0_wq", wq), ("l0_wk", wk), ("l0_wv", wv)):
                C.inc(e.dma_start(out=dst[:].rearrange("p (c d) -> p c d", d=D),
                                  in_=C.dram[nm].rearrange(
                    "(c p) d -> p c d", p=128)), "LD", 16)
            for c in range(8):
                C.inc(e.dma_start(out=bq[:, c:c + 1],
                                  in_=C.dram["l0_bq"][c * 128:(c + 1) * 128][:, None]),
                      "LD", 16)
                C.inc(e.dma_start(out=bk[:, c:c + 1],
                                  in_=C.dram["l0_bk"][c * 128:(c + 1) * 128][:, None]),
                      "LD", 16)
            C.inc(e.dma_start(out=bvr_bf[:], in_=C.dram["l0_bv"][None, :]),
                  "LD", 16)
        C.g(g)
        t_loads = C.tok("LD")

        if getattr(C, "f0_prep_pending", False):
            C.f0_prep_pending = False
            emit_weight_prep(C, C.scr, [("f0_wr", D, D), ("f0_wk", D, D),
                                        ("f0_wv", D, D)], stgw=D)
        emit_pre_stats(C, lnp)
        ap6 = lnp[:].rearrange("p (i c) -> p i c", c=6)
        t_stats = C.tok("DV")

        p_tr = ec(nc.psum_tensor(C.u("p_tr"), [128, 2 * D], BF16))
        p_mm = ec(nc.psum_tensor(C.u("p_mm"), [128, 2 * 512], F32))
        bc_toks = []
        for half in range(2):
            def t0(e, half=half):
                C.wait(e, "LD", t_loads)
                if half == 0:
                    C.wait(e, "DV", )
                C.inc(e.matmul(p_mm[:, half * 512:(half + 1) * 512],
                               k["ones_row"][:],
                               bvr_bf[:, half * 512:(half + 1) * 512],
                               start=True, stop=True), "PE", 1)
            C.t(t0)
            tok_pe = C.tok("PE")

            def v0(e, half=half, tok_pe=tok_pe):
                C.wtok(e, "PE", tok_pe)
                C.inc(e.tensor_copy(bvb[:, half * 512:(half + 1) * 512],
                                    p_mm[:, half * 512:(half + 1) * 512]),
                      "DV", 1)
            C.v(v0)
        t_bvb = C.tok("DV")

        ap_tok = [0] * NTILE
        tr_tok = [0] * NTILE
        kproj_tok = [0] * NGRP
        kev_tok = {}
        for grp in range(NGRP):
            hv_sc, hv_dv = C.cnt["SC"], C.cnt["DV"]

            def s(e, grp=grp, hv_dv=hv_dv,
                  pv=tuple(tr_tok[max(0, grp * 4 - 2): grp * 4])):
                C.wait(e, "DV", t_stats)
                for ti in range(4):
                    i = grp * 4 + ti
                    if ti >= 2:
                        C.wait(e, "DV", hv_dv + ti - 1)
                    elif i >= 2:
                        C.wait(e, "DV", pv[ti])
                    C.inc(e.activation(htm2[:, (i % 2) * D:(i % 2 + 1) * D],
                                       xs[:, i * D:(i + 1) * D], AF.Identity,
                                       scale=ap6[:, i, 4:5],
                                       bias=ap6[:, i, 5:6]), "SC", 1)
            C.s(s)
            for ti in range(4):
                ap_tok[grp * 4 + ti] = hv_sc + ti + 1
            for ti in range(4):
                i = grp * 4 + ti

                def t(e, i=i):
                    C.wtok(e, "SC", ap_tok[i])
                    if i >= 2:
                        C.wtok(e, "DV", tr_tok[i - 2])
                    for c in range(8):
                        ins = e.transpose(
                            p_tr[:, (i % 2) * D + c * 128:
                                 (i % 2) * D + (c + 1) * 128],
                            htm2[:, (i % 2) * D + c * 128:
                                 (i % 2) * D + (c + 1) * 128],
                            k["ident"][:])
                    C.inc(ins, "PE", 1)
                C.t(t)
                tok_pe = C.tok("PE")

                def v(e, i=i, tok_pe=tok_pe):
                    C.wtok(e, "PE", tok_pe)
                    for c in range(8):
                        ins = e.tensor_copy(
                            h0[:, c * S + i * 128: c * S + (i + 1) * 128],
                            p_tr[:, (i % 2) * D + c * 128:
                                 (i % 2) * D + (c + 1) * 128])
                    C.inc(ins, "DV", 1)
                C.v(v)
                tr_tok[i] = C.tok("DV")
                assert tr_tok[i] == hv_dv + ti + 1

            for oc in range(8):
                half = oc % 2

                def t(e, oc=oc, grp=grp, half=half):
                    if oc == 0:
                        C.wtok(e, "DV", tr_tok[grp * 4 + 3])
                        if grp == 0:
                            C.wait(e, "DV", t_bvb)
                    if (grp, oc) in kev_tok:
                        pass
                    prev = kev_tok.get((grp, oc - 2)) or (
                        kev_tok.get((grp - 1, 6 + oc)) if oc < 2 else None)
                    if prev:
                        C.wtok(e, "DV", prev)
                    for dc in range(8):
                        ins = e.matmul(
                            p_mm[:, half * 512:(half + 1) * 512],
                            wk[:, dc * D + oc * 128: dc * D + (oc + 1) * 128],
                            h0[:, dc * S + grp * 512: dc * S + (grp + 1) * 512],
                            start=(dc == 0), stop=(dc == 7))
                    C.inc(ins, "PE", 1)
                C.t(t)
                tok_pe = C.tok("PE")

                def v(e, oc=oc, grp=grp, half=half, tok_pe=tok_pe):
                    C.wtok(e, "PE", tok_pe)
                    C.inc(e.tensor_scalar(
                        kf[:, oc * S + grp * 512: oc * S + (grp + 1) * 512],
                        p_mm[:, half * 512:(half + 1) * 512], bk[:, oc:oc + 1],
                        None, op0=ALU.add), "DV", 1)
                C.v(v)
                kev_tok[(grp, oc)] = C.tok("DV")
            kproj_tok[grp] = C.tok("DV")

        p_s = ec(nc.psum_tensor(C.u("p_s"), [128, S], F32))
        exp_tok = [0] * (NTILE + 1)
        oev_tok = [0] * NTILE
        q_tok = [0] * NGRP
        qev_tok = {}
        for i in range(NTILE):
            grp, ti = divmod(i, 4)
            if ti == 0:
                for oc in range(8):
                    def t(e, oc=oc, grp=grp, i=i):
                        if oc == 0:
                            C.wtok(e, "DV", kproj_tok[NGRP - 1])
                            if i > 0:
                                C.wtok(e, "DV", oev_tok[i - 1])
                        prev = qev_tok.get((grp, oc - 1))
                        if prev:
                            C.wtok(e, "DV", prev)
                        for dc in range(8):
                            ins = e.matmul(
                                p_mm[:, 0:512],
                                wq[:, dc * D + oc * 128: dc * D + (oc + 1) * 128],
                                h0[:, dc * S + grp * 512:
                                   dc * S + (grp + 1) * 512],
                                start=(dc == 0), stop=(dc == 7))
                        C.inc(ins, "PE", 1)
                    C.t(t)
                    tok_pe = C.tok("PE")

                    def vv(e, oc=oc, tok_pe=tok_pe):
                        C.wtok(e, "PE", tok_pe)
                        C.inc(e.tensor_scalar(qg[:, oc * 512:(oc + 1) * 512],
                                              p_mm[:, 0:512], bq[:, oc:oc + 1],
                                              None, op0=ALU.add), "DV", 1)
                    C.v(vv)
                    qev_tok[(grp, oc)] = C.tok("DV")
                q_tok[grp] = C.tok("DV")

            ncols = (i + 1) * 128

            def t(e, ti=ti, ncols=ncols, grp=grp, i=i):
                C.wtok(e, "DV", q_tok[grp])
                C.wtok(e, "SC", exp_tok[i - 1] if i > 0 else 0)
                if i > 0:
                    C.wtok(e, "DV", oev_tok[i - 1] if False else 0)
                for dc in range(8):
                    for blk in range((ncols + 511) // 512):
                        n = min(512, ncols - blk * 512)
                        ins = e.matmul(
                            p_s[:, blk * 512: blk * 512 + n],
                            qg[:, dc * 512 + ti * 128: dc * 512 + (ti + 1) * 128],
                            kf[:, dc * S + blk * 512: dc * S + blk * 512 + n],
                            start=(dc == 0), stop=(dc == 7))
                C.inc(ins, "PE", 1)
            C.t(t)
            sc_pe = C.tok("PE")

            def t2(e, i=i):
                if i > 0:
                    C.wtok(e, "DV", oev_tok[i - 1])
                else:
                    C.wait(e, "DV", t_bvb)
                    C.wtok(e, "DV", kproj_tok[NGRP - 1])
                C.wtok(e, "DV", qev_tok.get((i // 4, 7), 0))
                for dc in range(8):
                    for half in range(2):
                        ins = e.matmul(
                            p_mm[:, half * 512:(half + 1) * 512],
                            h0[:, dc * S + i * 128: dc * S + (i + 1) * 128],
                            wv[:, dc * D + half * 512: dc * D + (half + 1) * 512],
                            start=(dc == 0), stop=(dc == 7))
                C.inc(ins, "PE", 1)
            C.t(t2)
            v_pe = C.tok("PE")

            def v(e, ncols=ncols, sc_pe=sc_pe):
                C.wtok(e, "PE", sc_pe)
                C.fence(e, e.tensor_tensor(p_s[:, ncols - 128: ncols],
                                           p_s[:, ncols - 128: ncols],
                                           k["cmask"][:], op=ALU.add), "DV")
                C.fence(e, e.tensor_reduce(sm[:, 0:1], p_s[:, 0:ncols],
                                           axis=AX.X, op=ALU.max), "DV")
                C.fence(e, e.tensor_scalar(sm[:, 1:2], sm[:, 0:1], 1.0 / 32,
                                           None, op0=ALU.mult), "DV")
                C.fence(e, e.tensor_scalar(sm[:, 1:2], sm[:, 1:2], 0.0, None,
                                           op0=ALU.max), "DV")
                C.inc(e.tensor_scalar(sm[:, 2:3], sm[:, 1:2], -1.0, None,
                                      op0=ALU.mult), "DV", 1)
            C.v(v)
            mx_tok = C.tok("DV")

            def s(e, ncols=ncols, mx_tok=mx_tok):
                C.wtok(e, "DV", mx_tok)
                e.activation(e_scr[:, 0:ncols], p_s[:, 0:ncols], AF.Exp,
                             scale=1.0 / 32, bias=sm[:, 2:3],
                             accum_out=sm[:, 3:4])
                C.inc(e.activation(sm[:, 4:5], sm[:, 2:3], AF.Exp), "SC", 1)
            C.s(s)
            exp_tok[i] = C.tok("SC")

            def v(e, i=i, v_pe=v_pe):
                C.wtok(e, "SC", exp_tok[i])
                C.fence(e, e.tensor_tensor(sm[:, 5:6], k["cnt16"][:, i:i + 1],
                                           sm[:, 4:5], op=ALU.mult), "DV")
                C.fence(e, e.tensor_tensor(sm[:, 5:6], sm[:, 5:6], sm[:, 3:4],
                                           op=ALU.add), "DV")
                C.fence(e, e.reciprocal(sm[:, 6:7], sm[:, 5:6]), "DV")
                C.fence(e, e.tensor_tensor(sm[:, 6:7], sm[:, 6:7], sm[:, 5:6],
                                           op=ALU.mult), "DV")
                C.wtok(e, "PE", v_pe)
                C.fence(e, e.tensor_tensor(k["scr_a"][:], p_mm[:], bvb[:],
                                           op=ALU.add), "DV")
                C.inc(e.tensor_scalar(k["scr_a"][:], k["scr_a"][:], sm[:, 6:7], None,
                                      op0=ALU.mult), "DV", 1)
            C.v(v)
            oev_tok[i] = C.tok("DV")

            emit_ln_post(C, k["scr_a"][:], xs[:, i * D:(i + 1) * D])


def emit_ln_post(C, o_ap, xs_ap):
    k = C.k
    st = k["st8"]

    def v1(e):
        C.wait(e, "DV")
        C.wait(e, "SC")
        C.fence(e, e.tensor_reduce(st[:, 0:1], o_ap, axis=AX.X, op=ALU.add),
                "DV")
        C.fence(e, e.tensor_scalar(st[:, 2:3], st[:, 0:1], 1.0 / D, None,
                                   op0=ALU.mult), "DV")
        C.inc(e.scalar_tensor_tensor(st[:, 4:5], st[:, 2:3], -1.0,
                                     st[:, 2:3], op0=ALU.mult,
                                     op1=ALU.mult), "DV", 1)
    C.v(v1)
    t3 = C.tok("DV")
    t3c = t3 + 2

    def s1(e, t3=t3, t3c=t3c):
        C.wait(e, "DV", t3)
        C.inc(e.activation(k["scr_b"][:], o_ap, AF.Square,
                           accum_out=st[:, 1:2]), "SC", 1)
        C.wait(e, "DV", t3c)
        C.fence(e, e.activation(st[:, 4:5], st[:, 3:4], AF.Ln,
                                bias=k["eps_v"][:, 0:1]), "SC")
        C.inc(e.activation(st[:, 5:6], st[:, 4:5], AF.Exp, scale=-0.5),
              "SC", 1)
    C.s(s1)
    t3b = C.tok("SC") - 2
    t4 = C.tok("SC")

    def v2(e, t3b=t3b):
        C.wait(e, "SC", t3b)
        C.fence(e, e.tensor_scalar(st[:, 3:4], st[:, 1:2], 1.0 / D, None,
                                   op0=ALU.mult), "DV")
        C.inc(e.tensor_tensor(st[:, 3:4], st[:, 3:4], st[:, 4:5], op=ALU.add),
              "DV", 1)
    C.v(v2)

    def v3(e, t4=t4):
        C.wait(e, "SC", t4)
        C.fence(e, e.scalar_tensor_tensor(st[:, 6:7], st[:, 2:3], -1.0,
                                          st[:, 5:6], op0=ALU.mult,
                                          op1=ALU.mult), "DV")
        C.fence(e, e.tensor_scalar(k["scr_b"][:], o_ap, st[:, 5:6],
                                   st[:, 6:7], op0=ALU.mult, op1=ALU.add),
                "DV")
        C.inc(e.tensor_tensor(xs_ap, xs_ap, k["scr_b"][:], op=ALU.add),
              "DV", 1)
    C.v(v3)


def emit_l0f(C, SCR):
    nc = C.nc
    k = C.k
    xs = C.xs
    with ExitStack() as ph:
        ec = ph.enter_context
        wrT = ec(nc.sbuf_tensor(C.u("f0_wrT"), [128, DC * D], BF16))
        wkT = ec(nc.sbuf_tensor(C.u("f0_wkT"), [128, DC * D], BF16))
        wvT = ec(nc.sbuf_tensor(C.u("f0_wvT"), [128, DC * D], BF16))
        mur = ec(nc.sbuf_tensor(C.u("f0_mur_s"), [128, 8], F32))
        muk = ec(nc.sbuf_tensor(C.u("f0_muk_s"), [128, 8], F32))
        murc = ec(nc.sbuf_tensor(C.u("f0_murc"), [128, 8], F32))
        mukc = ec(nc.sbuf_tensor(C.u("f0_mukc"), [128, 8], F32))
        hext = ec(nc.sbuf_tensor(C.u("f0_hext"), [128, 8 * 513], BF16))
        xr = ec(nc.sbuf_tensor(C.u("f0_xr"), [128, 8 * 512], BF16))
        xk = ec(nc.sbuf_tensor(C.u("f0_xk"), [128, 8 * 512], BF16))
        kact = ec(nc.sbuf_tensor(C.u("f0_kact"), [128, 8 * 512], BF16))
        tmp_bf = ec(nc.sbuf_tensor(C.u("f0_tmp"), [128, 512], BF16))
        krel = ec(nc.sbuf_tensor(C.u("f0_krel"), [128, 2 * 512], BF16))
        rtm = ec(nc.sbuf_tensor(C.u("f0_rtm"), [128, D], F32))
        lnp = ec(nc.sbuf_tensor(C.u("f0_lnp"), [128, 6 * NTILE], F32))
        htm2 = ec(nc.sbuf_tensor(C.u("f0_htm2"), [128, 2 * D], BF16))
        p_tr = ec(nc.psum_tensor(C.u("f0_p_tr"), [128, 2 * D], BF16))
        p_k = ec(nc.psum_tensor(C.u("f0_p_k"), [128, 2 * 512], F32))
        p_r = ec(nc.psum_tensor(C.u("f0_p_r"), [128, D], F32))
        p_kv = ec(nc.psum_tensor(C.u("f0_p_kv"), [128, D], F32))

        def y(e):
            C.wait(e, "LD2")
            for nm, dst in (("f0_wr_t", wrT), ("f0_wk_t", wkT),
                            ("f0_wv_t", wvT)):
                C.inc(e.dma_start(out=dst[:].rearrange("p (c d) -> p c d", d=D),
                                  in_=C.scr[nm].rearrange(
                    "(c p) d -> p c d", p=128)), "LD2", 16)
        C.y(y)
        t_wl = C.tok("LD2")

        def g(e):
            C.wait(e, "LD2")
            for c in range(8):
                C.inc(e.dma_start(out=mur[:, c:c + 1],
                                  in_=C.dram["f0_mur"][c * 128:(c + 1) * 128][:, None]),
                      "LD", 16)
                C.inc(e.dma_start(out=muk[:, c:c + 1],
                                  in_=C.dram["f0_muk"][c * 128:(c + 1) * 128][:, None]),
                      "LD", 16)
            for c in range(8):
                ins = e.memset(hext[:, c * 513: c * 513 + 1], 0.0)
            C.inc(ins, "GP", 1)
        C.g(g)
        t_mu = C.tok("LD")
        t_hx0 = C.tok("GP")

        if getattr(C, "f1_prep_pending", False):
            C.f1_prep_pending = False
            emit_weight_prep(C, SCR, [("f1_wg", HID, D), ("f1_wu", HID, D),
                                      ("f1_wd", D, HID)])

        def v0(e):
            C.wait(e, "LD", t_mu)
            C.fence(e, e.tensor_scalar(murc[:], mur[:], -1.0, 1.0,
                                       op0=ALU.mult, op1=ALU.add), "DV")
            C.inc(e.tensor_scalar(mukc[:], muk[:], -1.0, 1.0, op0=ALU.mult,
                                  op1=ALU.add), "DV", 1)
        C.v(v0)
        t_muc = C.tok("DV")

        emit_pre_stats(C, lnp)
        ap6 = lnp[:].rearrange("p (i c) -> p i c", c=6)
        t_stats = C.tok("DV")

        tr_tok = [0] * NTILE
        mix_tok = [0] * NGRP
        carry_tok = [0] * NGRP
        kact_tok = {}
        kev_sc = {}
        o_tok = [0] * NTILE
        rr_tok = [0] * NTILE
        kvp_tok = [0] * NTILE
        for grp in range(NGRP):
            if grp > 0:
                def v(e, grp=grp):
                    C.wtok(e, "DV", mix_tok[grp - 1])
                    for c in range(8):
                        ins = e.tensor_copy(hext[:, c * 513: c * 513 + 1],
                                            hext[:, c * 513 + 512:
                                                 c * 513 + 513])
                    C.inc(ins, "DV", 1)
                C.v(v)
                carry_tok[grp] = C.tok("DV")

            hv_sc, hv_dv = C.cnt["SC"], C.cnt["DV"]

            def s(e, grp=grp, hv_dv=hv_dv,
                  pv=tuple(tr_tok[max(0, grp * 4 - 2): grp * 4])):
                C.wait(e, "DV", t_stats)
                for ti in range(4):
                    i = grp * 4 + ti
                    if ti >= 2:
                        C.wait(e, "DV", hv_dv + ti - 1)
                    elif i >= 2:
                        C.wait(e, "DV", pv[ti])
                    C.inc(e.activation(htm2[:, (i % 2) * D:(i % 2 + 1) * D],
                                       xs[:, i * D:(i + 1) * D], AF.Identity,
                                       scale=ap6[:, i, 4:5],
                                       bias=ap6[:, i, 5:6]), "SC", 1)
            C.s(s)
            for ti in range(4):
                i = grp * 4 + ti
                ap_t = hv_sc + ti + 1

                def t(e, i=i, ap_t=ap_t, grp=grp, ti=ti):
                    C.wtok(e, "SC", ap_t)
                    if i >= 2:
                        C.wtok(e, "DV", tr_tok[i - 2])
                    for c in range(8):
                        ins = e.transpose(
                            p_tr[:, (i % 2) * D + c * 128:
                                 (i % 2) * D + (c + 1) * 128],
                            htm2[:, (i % 2) * D + c * 128:
                                 (i % 2) * D + (c + 1) * 128],
                            k["ident"][:])
                    C.inc(ins, "PE", 1)
                C.t(t)
                pe_t = C.tok("PE")

                def v(e, i=i, ti=ti, grp=grp, pe_t=pe_t):
                    C.wtok(e, "PE", pe_t)
                    if ti == 0 and grp > 0:
                        C.wtok(e, "DV", carry_tok[grp])
                    if grp == 0 and ti == 0:
                        C.wait(e, "GP", t_hx0)
                    for c in range(8):
                        ins = e.tensor_copy(
                            hext[:, c * 513 + 1 + ti * 128:
                                 c * 513 + 1 + (ti + 1) * 128],
                            p_tr[:, (i % 2) * D + c * 128:
                                 (i % 2) * D + (c + 1) * 128])
                    C.inc(ins, "DV", 1)
                C.v(v)
                tr_tok[i] = C.tok("DV")

            def v(e, grp=grp):
                C.wtok(e, "DV", tr_tok[grp * 4 + 3])
                C.wait(e, "DV", t_muc)
                if grp > 0:
                    C.wtok(e, "PE", kvp_tok[grp * 4 - 1])
                for c in range(8):
                    h1 = hext[:, c * 513 + 1: c * 513 + 513]
                    h1p = hext[:, c * 513: c * 513 + 512]
                    C.fence(e, e.tensor_scalar(tmp_bf[:], h1p, murc[:, c:c + 1],
                                               None, op0=ALU.mult), "DV")
                    C.fence(e, e.scalar_tensor_tensor(
                        xr[:, c * 512:(c + 1) * 512], h1, mur[:, c:c + 1],
                        tmp_bf[:], op0=ALU.mult, op1=ALU.add), "DV")
                    C.fence(e, e.tensor_scalar(tmp_bf[:], h1p, mukc[:, c:c + 1],
                                               None, op0=ALU.mult), "DV")
                    ins = e.scalar_tensor_tensor(xk[:, c * 512:(c + 1) * 512],
                                                 h1, muk[:, c:c + 1], tmp_bf[:],
                                                 op0=ALU.mult, op1=ALU.add)
                    if c < 7:
                        C.fence(e, ins, "DV")
                C.inc(ins, "DV", 1)
            C.v(v)
            mix_tok[grp] = C.tok("DV")

            ka_pe, ka_sc = C.cnt["PE"], C.cnt["SC"]

            def s(e, ka_pe=ka_pe):
                for oc in range(8):
                    pb = oc % 2
                    C.wait(e, "PE", ka_pe + oc + 1)
                    C.inc(e.activation(krel[:, pb * 512:(pb + 1) * 512],
                                       p_k[:, pb * 512:(pb + 1) * 512],
                                       AF.Relu), "SC", 1)
            C.s(s)
            for oc in range(8):
                pb = oc % 2

                def t(e, oc=oc, grp=grp, pb=pb):
                    if oc == 0:
                        C.wtok(e, "DV", mix_tok[grp])
                        C.wait(e, "LD2", t_wl)
                    prev = kact_tok.get((grp, oc - 2)) or (
                        kact_tok.get((grp - 1, 6 + oc))
                        if oc < 2 and grp > 0 else None)
                    if prev:
                        C.wtok(e, "DV", prev)
                    for dc in range(8):
                        ins = e.matmul(
                            p_k[:, pb * 512:(pb + 1) * 512],
                            wkT[:, dc * D + oc * 128: dc * D + (oc + 1) * 128],
                            xk[:, dc * 512:(dc + 1) * 512],
                            start=(dc == 0), stop=(dc == 7))
                    C.inc(ins, "PE", 1)
                C.t(t)
                assert C.cnt["PE"] == ka_pe + oc + 1

                def v(e, oc=oc, pb=pb, sc_t=ka_sc + oc + 1):
                    C.wait(e, "SC", sc_t)
                    C.inc(e.tensor_tensor(kact[:, oc * 512:(oc + 1) * 512],
                                          krel[:, pb * 512:(pb + 1) * 512],
                                          krel[:, pb * 512:(pb + 1) * 512],
                                          op=ALU.mult), "DV", 1)
                C.v(v)
                kact_tok[(grp, oc)] = C.tok("DV")

            for ti in range(4):
                i = grp * 4 + ti

                def t(e, ti=ti, grp=grp, i=i):
                    if ti == 0:
                        C.wtok(e, "DV", mix_tok[grp])
                    if i > 0:
                        C.wtok(e, "SC", rr_tok[i - 1])
                    for dc in range(8):
                        for half in range(2):
                            ins = e.matmul(
                                p_r[:, half * 512:(half + 1) * 512],
                                xr[:, dc * 512 + ti * 128:
                                   dc * 512 + (ti + 1) * 128],
                                wrT[:, dc * D + half * 512:
                                    dc * D + (half + 1) * 512],
                                start=(dc == 0), stop=(dc == 7))
                    C.inc(ins, "PE", 1)
                C.t(t)
                pe_r = C.tok("PE")

                def s(e, pe_r=pe_r, i=i):
                    C.wtok(e, "PE", pe_r)
                    if i > 0:
                        C.wtok(e, "DV", o_tok[i - 1])
                    C.inc(e.activation(rtm[:], p_r[:], AF.Sigmoid), "SC", 1)
                C.s(s)
                rr_tok[i] = C.tok("SC")

                def t2(e, ti=ti, grp=grp, i=i):
                    C.wtok(e, "DV", kact_tok[(grp, 7)])
                    if i > 0:
                        C.wtok(e, "DV", o_tok[i - 1])
                    for fc in range(8):
                        for half in range(2):
                            ins = e.matmul(
                                p_kv[:, half * 512:(half + 1) * 512],
                                kact[:, fc * 512 + ti * 128:
                                     fc * 512 + (ti + 1) * 128],
                                wvT[:, fc * D + half * 512:
                                    fc * D + (half + 1) * 512],
                                start=(fc == 0), stop=(fc == 7))
                    C.inc(ins, "PE", 1)
                C.t(t2)
                kvp_tok[i] = C.tok("PE")

                def v(e, i=i):
                    C.wtok(e, "PE", kvp_tok[i])
                    C.wtok(e, "SC", rr_tok[i])
                    C.wait(e, "DV")
                    C.inc(e.tensor_tensor(k["scr_a"][:], p_kv[:], rtm[:],
                                          op=ALU.mult), "DV", 1)
                C.v(v)
                o_tok[i] = C.tok("DV")

                emit_ln_post(C, k["scr_a"][:], xs[:, i * D:(i + 1) * D])


def emit_l1a(C):
    nc = C.nc
    k = C.k
    xs = C.xs
    qscr = C.scr["qscr"]
    with ExitStack() as ph:
        ec = ph.enter_context
        kp = ec(nc.sbuf_tensor(C.u("l1_k"), [128, DC * S], BF16))
        vst = ec(nc.sbuf_tensor(C.u("l1_v"), [128, NTILE * H * 65], BF16))
        h2f = ec(nc.sbuf_tensor(C.u("l1_h2f"), [128, DC * 512], BF16))
        lnp = ec(nc.sbuf_tensor(C.u("l1_lnp"), [128, 6 * NTILE], F32))
        htm2 = ec(nc.sbuf_tensor(C.u("l1_htm2"), [128, 2 * D], BF16))
        sqs = ec(nc.sbuf_tensor(C.u("l1_sq"), [128, 512], BF16))
        nrm = ec(nc.sbuf_tensor(C.u("l1_nrm"), [1, 80], F32))
        negm = ec(nc.sbuf_tensor(C.u("l1_negm"), [128, 1], F32))
        mrow = ec(nc.sbuf_tensor(C.u("l1_mrow"), [1, 1], BF16))
        ap6 = None

        emit_pre_stats(C, lnp)
        ap6 = lnp[:].rearrange("p (i c) -> p i c", c=6)
        t_stats = C.tok("DV")

        with ExitStack() as pa:
            eca = pa.enter_context
            p_tr = eca(nc.psum_tensor(C.u("l1_p_tr"), [128, 2 * D], BF16))
            p_mm = eca(nc.psum_tensor(C.u("l1_p_mm"), [128, 4 * 512], F32))
            p_v = eca(nc.psum_tensor(C.u("l1_p_v"), [128, D], F32))
            wq = eca(nc.sbuf_tensor(C.u("l1_wq_s"), [128, DC * D], BF16))
            wk = eca(nc.sbuf_tensor(C.u("l1_wk_s"), [128, DC * D], BF16))
            wv = eca(nc.sbuf_tensor(C.u("l1_wv_s"), [128, DC * D], BF16))
            qgb = eca(nc.sbuf_tensor(C.u("l1_qgb"), [128, DC * 512], BF16))

            def g(e):
                C.wait(e, "LD2")
                for nm, dst in [("l1_wq", wq), ("l1_wk", wk), ("l1_wv", wv)]:
                    C.inc(e.dma_start(
                        out=dst[:].rearrange("p (c d) -> p c d", d=D),
                        in_=C.dram[nm].rearrange("(c p) d -> p c d", p=128)),
                        "LD", 16)
                C.inc(e.memset(vst[:], 1.0), "GP", 1)
            C.g(g)
            t_wload = C.tok("LD")
            t_vmem = C.tok("GP")

            tr_tok = [0] * NTILE
            qk_tok = {}
            vev_tok = [0] * NTILE
            qship_tok = [0] * NGRP
            nq_tok = {}
            for grp in range(NGRP):
                hv_sc, hv_dv = C.cnt["SC"], C.cnt["DV"]

                def s(e, grp=grp, hv_dv=hv_dv,
                      pv=tuple(tr_tok[max(0, grp * 4 - 2): grp * 4])):
                    C.wait(e, "DV", t_stats)
                    for ti in range(4):
                        i = grp * 4 + ti
                        if ti >= 2:
                            C.wait(e, "DV", hv_dv + ti - 1)
                        elif i >= 2:
                            C.wait(e, "DV", pv[ti])
                        C.inc(e.activation(
                            htm2[:, (i % 2) * D:(i % 2 + 1) * D],
                            xs[:, i * D:(i + 1) * D],
                            AF.Identity, scale=ap6[:, i, 4:5],
                            bias=ap6[:, i, 5:6]), "SC", 1)
                C.s(s)
                for ti in range(4):
                    i = grp * 4 + ti
                    ap_t = hv_sc + ti + 1

                    def t(e, i=i, ti=ti, ap_t=ap_t):
                        C.wtok(e, "SC", ap_t)
                        if i >= 2:
                            C.wtok(e, "DV", tr_tok[i - 2])
                        for c in range(8):
                            ins = e.transpose(
                                p_tr[:, (i % 2) * D + c * 128:
                                     (i % 2) * D + (c + 1) * 128],
                                htm2[:, (i % 2) * D + c * 128:
                                     (i % 2) * D + (c + 1) * 128],
                                k["ident"][:])
                        C.inc(ins, "PE", 1)
                    C.t(t)
                    pe_t = C.tok("PE")

                    def v(e, i=i, ti=ti, pe_t=pe_t):
                        C.wtok(e, "PE", pe_t)
                        for c in range(8):
                            ins = e.tensor_copy(
                                h2f[:, c * 512 + ti * 128:
                                    c * 512 + (ti + 1) * 128],
                                p_tr[:, (i % 2) * D + c * 128:
                                     (i % 2) * D + (c + 1) * 128])
                        C.inc(ins, "DV", 1)
                    C.v(v)
                    tr_tok[i] = C.tok("DV")
                    assert tr_tok[i] == hv_dv + ti + 1

                qp_pe, qp_sc = C.cnt["PE"], C.cnt["SC"]

                def s(e, qp_pe=qp_pe):
                    for oc in range(8):
                        qtr = oc % 2
                        C.wait(e, "PE", qp_pe + 2 * oc + 1)
                        C.inc(e.activation(
                            qg_dst(qgb, oc), p_mm[:, qtr * 512:
                                                  (qtr + 1) * 512],
                            AF.Identity), "SC", 1)
                C.s(s)
                for oc in range(8):
                    for wsb, isq in ((wq, True), (wk, False)):
                        qtr = (oc % 2) + (0 if isq else 2)

                        def t(e, oc=oc, grp=grp, qtr=qtr, wsb=wsb, isq=isq):
                            if oc == 0 and isq:
                                C.wtok(e, "DV", tr_tok[grp * 4 + 3])
                                C.wait(e, "LD", t_wload)
                                if grp > 0:
                                    C.wtok(e, "ST", qship_tok[grp - 1])
                                    C.wtok(e, "DV", nq_tok[(grp - 1, 7)][1])
                            prev = qk_tok.get((grp, oc - 2, isq)) or (
                                qk_tok.get((grp - 1, 6 + oc, isq))
                                if oc < 2 and grp > 0 else None)
                            if prev:
                                nm, v_ = prev
                                C.wtok(e, nm, v_)
                            for dc in range(8):
                                ins = e.matmul(
                                    p_mm[:, qtr * 512:(qtr + 1) * 512],
                                    wsb[:, dc * D + oc * 128:
                                        dc * D + (oc + 1) * 128],
                                    h2f[:, dc * 512:(dc + 1) * 512],
                                    start=(dc == 0), stop=(dc == 7))
                            C.inc(ins, "PE", 1)
                        C.t(t)
                        pe_t = C.tok("PE")

                        if isq:
                            assert pe_t == qp_pe + 2 * oc + 1
                            qk_tok[(grp, oc, True)] = ("SC", qp_sc + oc + 1)
                        else:
                            def v(e, oc=oc, grp=grp, qtr=qtr, pe_t=pe_t):
                                C.wtok(e, "PE", pe_t)
                                C.inc(e.tensor_scalar(
                                    kp[:, oc * S + grp * 512:
                                       oc * S + (grp + 1) * 512],
                                    p_mm[:, qtr * 512:(qtr + 1) * 512],
                                    0.125, None, op0=ALU.mult), "DV", 1)
                            C.v(v)
                            qk_tok[(grp, oc, False)] = ("DV", C.tok("DV"))

                vp_pe, vp_sc = C.cnt["PE"], C.cnt["SC"]

                def s(e, vp_pe=vp_pe, grp=grp):
                    C.wait(e, "GP", t_vmem)
                    for ti in range(4):
                        i = grp * 4 + ti
                        C.wait(e, "PE", vp_pe + ti + 1)
                        dst = vst[:, i * H * 65:(i + 1) * H * 65].rearrange(
                            "p (h e) -> p h e", e=65)[:, :, 0:64]
                        srcp = p_v[:].rearrange("p (h e) -> p h e", e=64)
                        C.inc(e.activation(dst, srcp, AF.Identity), "SC", 1)
                C.s(s)
                for ti in range(4):
                    i = grp * 4 + ti

                    def t(e, i=i, ti=ti, grp=grp):
                        if ti == 0:
                            nm, v_ = qk_tok[(grp, 7, False)]
                            C.wtok(e, nm, v_)
                            nm, v_ = qk_tok[(grp, 7, True)]
                            C.wtok(e, nm, v_)
                        if i >= 1:
                            C.wtok(e, "SC", vev_tok[i - 1])
                        for dc in range(8):
                            for hh in range(2):
                                ins = e.matmul(
                                    p_v[:, hh * 512:(hh + 1) * 512],
                                    h2f[:, dc * 512 + ti * 128:
                                        dc * 512 + (ti + 1) * 128],
                                    wv[:, dc * D + hh * 512:
                                        dc * D + (hh + 1) * 512],
                                    start=(dc == 0), stop=(dc == 7))
                        C.inc(ins, "PE", 1)
                    C.t(t)
                    assert C.cnt["PE"] == vp_pe + ti + 1
                    vev_tok[i] = vp_sc + ti + 1

                qn_pe, qn_sc, qn_dv = C.cnt["PE"], C.cnt["SC"], C.cnt["DV"]

                def s(e, qn_pe=qn_pe):
                    for pc in range(8):
                        C.wait(e, "PE", qn_pe + pc)
                        C.inc(e.activation(
                            sqs[:],
                            qgb[:, pc * 512:(pc + 1) * 512],
                            AF.Square), "SC", 1)
                C.s(s)
                assert C.cnt["SC"] == qn_sc + 8

                for pc in range(8):
                    def t(e, pc=pc, qn_sc=qn_sc, qn_dv=qn_dv):
                        C.wait(e, "SC", qn_sc + pc + 1)
                        C.wait(e, "DV", qn_dv + pc)
                        C.inc(e.matmul(p_mm[0:1, 0:512], k["ones_col"][:],
                                       sqs[:],
                                       start=True, stop=True), "PE", 1)
                    C.t(t)

                    def v(e, pc=pc, grp=grp, qn_pe=qn_pe):
                        C.wait(e, "PE", qn_pe + pc + 1)
                        C.inc(e.tensor_reduce(
                            nrm[:, pc * 4 + grp: pc * 4 + grp + 1],
                            p_mm[0:1, 0:512], axis=AX.X, op=ALU.max), "DV", 1)
                    C.v(v)
                    nq_tok[(grp, pc)] = (qn_pe + pc + 1, C.tok("DV"))
                assert C.cnt["PE"] == qn_pe + 8

                def y(e, grp=grp):
                    C.wtok(e, "SC", qk_tok[(grp, 7, True)][1])
                    C.inc(e.dma_start(out=qscr[grp], in_=qgb[:]), "ST", 16)
                C.y(y)
                qship_tok[grp] = C.tok("ST")

            kn_pe, kn_sc, kn_dv = C.cnt["PE"], C.cnt["SC"], C.cnt["DV"]

            def s(e, kn_pe=kn_pe):
                for j in range(32):
                    C.wait(e, "PE", kn_pe + j)
                    pc, g4 = divmod(j, 4)
                    C.inc(e.activation(
                        sqs[:],
                        kp[:, pc * S + g4 * 512: pc * S + (g4 + 1) * 512],
                        AF.Square), "SC", 1)
            C.s(s)
            assert C.cnt["SC"] == kn_sc + 32

            for j in range(32):
                pc, g4 = divmod(j, 4)

                def t(e, j=j, kn_sc=kn_sc, kn_dv=kn_dv):
                    C.wait(e, "SC", kn_sc + j + 1)
                    C.wait(e, "DV", kn_dv + j)
                    C.inc(e.matmul(p_mm[0:1, 0:512], k["ones_col"][:],
                                   sqs[:],
                                   start=True, stop=True), "PE", 1)
                C.t(t)

                def v(e, j=j, pc=pc, g4=g4, kn_pe=kn_pe):
                    C.wait(e, "PE", kn_pe + j + 1)
                    C.inc(e.tensor_reduce(
                        nrm[:, 40 + pc * 4 + g4: 40 + pc * 4 + g4 + 1],
                        p_mm[0:1, 0:512], axis=AX.X, op=ALU.max), "DV", 1)
                C.v(v)
            assert C.cnt["PE"] == kn_pe + 32

            def v(e):
                C.wait(e, "DV")
                e.tensor_reduce(nrm[:, 32:33], nrm[:, 0:32], axis=AX.X,
                                op=ALU.max)
                C.fence(e, e.tensor_reduce(nrm[:, 72:73], nrm[:, 40:72],
                                           axis=AX.X, op=ALU.max), "DV")
                C.inc(e.tensor_tensor(nrm[:, 33:34], nrm[:, 32:33],
                                      nrm[:, 72:73], op=ALU.mult), "DV", 1)
            C.v(v)

            def s(e):
                C.wait(e, "DV")
                C.fence(e, e.activation(nrm[:, 34:35], nrm[:, 33:34],
                                        AF.Ln), "SC")
                C.inc(e.activation(mrow[:], nrm[:, 34:35], AF.Exp, scale=0.5),
                      "SC", 1)
            C.s(s)
            sq_t = C.tok("SC")

            def t(e, sq_t=sq_t):
                C.wtok(e, "SC", sq_t)
                C.inc(e.matmul(p_mm[:, 0:1], k["ones_row"][:], mrow[:],
                               start=True, stop=True), "PE", 1)
            C.t(t)
            mm_t = C.tok("PE")

            def v(e, mm_t=mm_t):
                C.wtok(e, "PE", mm_t)
                C.inc(e.tensor_scalar(negm[:], p_mm[:, 0:1], -1.0, None,
                                      op0=ALU.mult), "DV", 1)
            C.v(v)
        t_negm = C.tok("DV")

        with ExitStack() as pb:
            ecb = pb.enter_context
            wp = ecb(nc.sbuf_tensor(C.u("l1_wp_s"), [128, DC * D], BF16))
            qgrp = ecb(nc.sbuf_tensor(C.u("l1_qgrp"), [128, DC * 512], BF16))
            pt = ecb(nc.sbuf_tensor(C.u("l1_pt"), [128, 2 * 1024], BF16))
            au = ecb(nc.sbuf_tensor(C.u("l1_au"), [65, 2 * 512], BF16))
            an = ecb(nc.sbuf_tensor(C.u("l1_an"), [128, 4 * D], F32))
            ag = ecb(nc.sbuf_tensor(C.u("l1_ag"), [128, 2 * D], BF16))
            af = ecb(nc.sbuf_tensor(C.u("l1_af"), [128, 2 * D], BF16))
            p_sp = ecb(nc.psum_tensor(C.u("l1_p_s"), [128, 2 * 1024], F32))
            p_o = ecb(nc.psum_tensor(C.u("l1_p_o"), [65, 2 * 512], F32))
            p_at = ecb(nc.psum_tensor(C.u("l1_p_at"), [128, 2 * 1024], BF16))
            p_w = p_at[:].bitcast(F32)[:, 0:512]

            def g(e):
                C.wait(e, "PE")
                C.wait(e, "SC")
                C.wait(e, "DV")
                C.wait(e, "ST")
                C.inc(e.dma_start(
                    out=wp[:].rearrange("p (c d) -> p c d", d=D),
                    in_=C.dram["l1_wp"].rearrange("(c p) d -> p c d", p=128)),
                    "LD", 16)
            C.g(g)
            t_wp = C.tok("LD")

            qld_tok = [0] * NGRP
            exp_tok = {}
            pv_tok = {}
            auev_tok = {}
            anrm_tok = {}
            agl_tok = [0] * 4
            aft_tok = [0] * 4
            wpo_tok = [0] * NTILE
            for grp in range(NGRP):
                def y(e, grp=grp):
                    C.wait(e, "PE")
                    if grp == 0:
                        C.wait(e, "SC")
                        C.wait(e, "DV")
                        C.wait(e, "ST")
                    C.inc(e.dma_start(out=qgrp[:], in_=qscr[grp]), "LD2", 16)
                C.y(y)
                qld_tok[grp] = C.tok("LD2")

                eg_sc0 = C.cnt["SC"]
                st_pe_all = []
                for pc in range(8):
                    sc_base = eg_sc0 + NTILE * pc
                    for tc in range(NTILE):
                        exp_tok[(grp, pc, tc)] = sc_base + tc + 1
                    st_pe_l = [0] * NTILE

                    def mk_st(tc, pc=pc, grp=grp):
                        buf = tc % 2

                        def t(e, grp=grp, pc=pc, tc=tc, buf=buf):
                            if pc == 0 and tc == 0:
                                C.wait(e, "DV", t_negm)
                                C.wait(e, "LD2", qld_tok[grp])
                            prev = exp_tok.get((grp, pc, tc - 2)) or (
                                exp_tok.get((grp, pc - 1, NTILE - 2))
                                if tc < 2 else None)
                            if prev:
                                C.wtok(e, "SC", prev)
                            e.matmul(
                                p_sp[:, buf * 1024: buf * 1024 + 512],
                                kp[0:64, pc * S + tc * 128:
                                   pc * S + (tc + 1) * 128],
                                qgrp[0:64, pc * 512:(pc + 1) * 512],
                                start=True, stop=True, tile_position=(0, 0))
                            C.inc(e.matmul(
                                p_sp[:, buf * 1024 + 512: buf * 1024 + 1024],
                                kp[64:128, pc * S + tc * 128:
                                   pc * S + (tc + 1) * 128],
                                qgrp[64:128, pc * 512:(pc + 1) * 512],
                                start=True, stop=True,
                                tile_position=(64, 0)), "PE", 1)
                        C.t(t)
                        st_pe_l[tc] = C.tok("PE")

                    def mk_pv(tc, pc=pc, grp=grp):
                        buf = tc % 2

                        def t2(e, pc=pc, tc=tc, buf=buf, grp=grp):
                            C.wtok(e, "SC", exp_tok[(grp, pc, tc)])
                            if tc == 0:
                                prev = auev_tok.get((grp, pc - 1)) or (
                                    auev_tok.get((grp - 1, 7)) if grp > 0
                                    else None)
                                if prev:
                                    C.wtok(e, prev[0], prev[1])
                            e.matmul(p_o[0:65, 0:512],
                                     vst[:, tc * H * 65 + (2 * pc) * 65:
                                         tc * H * 65 + (2 * pc) * 65 + 65],
                                     pt[:, buf * 1024: buf * 1024 + 512],
                                     start=(tc == 0), stop=(tc == NTILE - 1),
                                     skip_group_check=True)
                            C.inc(e.matmul(
                                p_o[0:65, 512:1024],
                                vst[:, tc * H * 65 + (2 * pc + 1) * 65:
                                    tc * H * 65 + (2 * pc + 1) * 65 + 65],
                                pt[:, buf * 1024 + 512: buf * 1024 + 1024],
                                start=(tc == 0), stop=(tc == NTILE - 1),
                                skip_group_check=True), "PE", 1)
                        C.t(t2)

                    mk_st(0)
                    for tc in range(1, NTILE):
                        mk_st(tc)
                        mk_pv(tc - 1)
                    mk_pv(NTILE - 1)
                    pv_tok[(grp, pc)] = C.tok("PE")
                    st_pe_all.append(tuple(st_pe_l))

                    def vv(e, pc=pc, grp=grp):
                        C.wtok(e, "PE", pv_tok[(grp, pc)])
                        prev = anrm_tok.get((grp, pc - 1)) or (
                            anrm_tok.get((grp - 1, 7)) if grp > 0 else None)
                        if prev:
                            C.wtok(e, "DV", prev)
                        e.tensor_copy(au[0:65, 0:512], p_o[0:65, 0:512])
                        C.inc(e.tensor_copy(au[0:65, 512:1024],
                                            p_o[0:65, 512:1024]), "DV", 1)
                    C.v(vv)
                    auev_tok[(grp, pc)] = ("DV", C.tok("DV"))

                    for hh_i in range(2):
                        h = 2 * pc + hh_i
                        for sub in range(4):
                            ab = (hh_i * 4 + sub) % 2

                            def t(e, hh_i=hh_i, sub=sub, ab=ab, grp=grp,
                                  pc=pc):
                                nm_, v_ = auev_tok[(grp, pc)]
                                C.wtok(e, nm_, v_)
                                prev = anrm_tok.get((grp, pc, hh_i, sub - 2)) \
                                    or (anrm_tok.get((grp, pc, hh_i - 1,
                                                      sub + 2))
                                        if sub < 2 else None)
                                if prev:
                                    C.wtok(e, "DV", prev)
                                else:
                                    C.wait(e, "DV")
                                C.inc(e.transpose(
                                    p_at[:, ab * 1024: ab * 1024 + 65],
                                    au[0:65, hh_i * 512 + sub * 128:
                                       hh_i * 512 + (sub + 1) * 128],
                                    k["ident65"][:]), "PE", 1)
                            C.t(t)
                            at_t = C.tok("PE")

                            def v(e, sub=sub, h=h, ab=ab, at_t=at_t,
                                  hh_i=hh_i, grp=grp, pc=pc):
                                pr = anrm_tok.get((grp, pc, hh_i, sub - 1)) \
                                    or (anrm_tok.get((grp, pc, hh_i - 1, 3))
                                        if sub == 0 else None)
                                if pr:
                                    C.wtok(e, "DV", pr)
                                C.wtok(e, "PE", at_t)
                                C.fence(e, e.reciprocal(
                                    k["st8"][:, 7:8],
                                    p_at[:, ab * 1024 + 64: ab * 1024 + 65]),
                                    "DV")
                                C.inc(e.tensor_scalar(
                                    an[:, sub * D + h * E:
                                       sub * D + (h + 1) * E],
                                    p_at[:, ab * 1024: ab * 1024 + 64],
                                    k["st8"][:, 7:8], None, op0=ALU.mult),
                                    "DV", 1)
                            C.v(v)
                            anrm_tok[(grp, pc, hh_i, sub)] = C.tok("DV")
                        anrm_tok[(grp, pc)] = C.tok("DV")

                assert C.cnt["SC"] == eg_sc0
                def s(e, st_pe_all=tuple(st_pe_all)):
                    for pc in range(8):
                        for tc in range(NTILE):
                            buf = tc % 2
                            C.wait(e, "PE", st_pe_all[pc][tc])
                            C.inc(e.activation(pt[:, buf * 1024:
                                                  (buf + 1) * 1024],
                                               p_sp[:, buf * 1024:
                                                    (buf + 1) * 1024],
                                               AF.Exp, bias=negm[:, 0:1]),
                                  "SC", 1)
                C.s(s)
                assert C.cnt["SC"] == eg_sc0 + 8 * NTILE

                for sub in range(4):
                    i = grp * 4 + sub
                    sb = sub % 2

                    def s(e, sub=sub, sb=sb, grp=grp, i=i,
                          dv_now=C.cnt["DV"]):
                        C.wtok(e, "DV", anrm_tok[(grp, 7)])
                        if sub >= 2:
                            C.wtok(e, "DV", aft_tok[sub - 2])
                        C.inc(e.activation(ag[:, sb * D:(sb + 1) * D],
                                           an[:, sub * D:(sub + 1) * D],
                                           AF.Gelu), "SC", 1)
                        C.wait(e, "DV", dv_now)
                        C.inc(e.activation(k["scr_b"][:],
                                           xs[:, i * D:(i + 1) * D],
                                           AF.Identity,
                                           scale=ap6[:, i, 4:5],
                                           bias=ap6[:, i, 5:6]), "SC", 1)
                    C.s(s)
                    agl_tok[sub] = C.tok("SC") - 1
                    h2r_t = C.tok("SC")

                    def t(e, sub=sub, sb=sb):
                        C.wtok(e, "SC", agl_tok[sub])
                        if sub >= 2:
                            C.wtok(e, "DV", aft_tok[sub - 2])
                        for c in range(8):
                            ins = e.transpose(
                                p_at_tr(p_sp, sb, c),
                                ag[:, sb * D + c * 128: sb * D + (c + 1) * 128],
                                k["ident"][:])
                        C.inc(ins, "PE", 1)
                    C.t(t)
                    trf_t = C.tok("PE")

                    def v(e, sub=sub, sb=sb, trf_t=trf_t):
                        C.wtok(e, "PE", trf_t)
                        for c in range(8):
                            ins = e.tensor_copy(
                                af[:, sb * D + c * 128: sb * D + (c + 1) * 128],
                                p_at_tr(p_sp, sb, c))
                        C.inc(ins, "DV", 1)
                    C.v(v)
                    aft_tok[sub] = C.tok("DV")


                    for hh in range(2):
                        def t2(e, sub=sub, sb=sb, i=i, hh=hh):
                            C.wtok(e, "DV", aft_tok[sub])
                            C.wait(e, "LD", t_wp)
                            if i > 0 or hh > 0:
                                C.wait(e, "DV")
                            for fc in range(8):
                                ins = e.matmul(
                                    p_w,
                                    af[:, sb * D + fc * 128:
                                       sb * D + (fc + 1) * 128],
                                    wp[:, fc * D + hh * 512:
                                       fc * D + (hh + 1) * 512],
                                    start=(fc == 0), stop=(fc == 7))
                            C.inc(ins, "PE", 1)
                        C.t(t2)
                        wp_pe = C.tok("PE")

                        def v2(e, hh=hh, h2r_t=h2r_t, wp_pe=wp_pe):
                            C.wtok(e, "PE", wp_pe)
                            C.wtok(e, "SC", h2r_t)
                            C.inc(e.tensor_tensor(
                                k["scr_a"][:, hh * 512:(hh + 1) * 512], p_w,
                                k["scr_b"][:, hh * 512:(hh + 1) * 512],
                                op=ALU.add), "DV", 1)
                        C.v(v2)

                    emit_ln_post(C, k["scr_a"][:], xs[:, i * D:(i + 1) * D])
                    wpo_tok[i] = C.tok("SC")


def qg_dst(qgb, oc):
    return qgb[:, oc * 512:(oc + 1) * 512]


def p_at_tr(p_sp, sb, c):
    return p_sp[:].bitcast(BF16)[:, sb * 1024 + c * 128:
                                 sb * 1024 + (c + 1) * 128]


def emit_pre_stats(C, lnp):
    k = C.k
    xs = C.xs
    ap6 = lnp[:].rearrange("p (i c) -> p i c", c=6)

    def v(e):
        C.wait(e, "LD2")
        C.wait(e, "DV")
        for i in range(NTILE):
            ins = e.tensor_reduce(lnp[:, 6 * i: 6 * i + 1],
                                  xs[:, i * D:(i + 1) * D], axis=AX.X,
                                  op=ALU.add)
        C.inc(ins, "DV", 1)
    C.v(v)
    tsum = C.tok("DV")

    def s(e, tsum=tsum):
        C.wait(e, "LD2")
        C.wait(e, "SC")
        C.wait(e, "DV", tsum)
        for i in range(NTILE):
            ins = e.activation(k["scr_a" if i % 2 else "scr_b"][:],
                               xs[:, i * D:(i + 1) * D],
                               AF.Square, accum_out=lnp[:, 6 * i + 1:
                                                        6 * i + 2])
            if i < NTILE - 1:
                C.fence(e, ins, "SC")
        C.inc(ins, "SC", 1)
    C.s(s)
    tsq = C.tok("SC")

    def v2(e, tsq=tsq):
        C.wait(e, "SC", tsq)
        C.fence(e, e.tensor_scalar(ap6[:, :, 2:3], ap6[:, :, 0:1], 1.0 / D,
                                   None, op0=ALU.mult), "DV")
        C.fence(e, e.tensor_scalar(ap6[:, :, 3:4], ap6[:, :, 1:2], 1.0 / D,
                                   None, op0=ALU.mult), "DV")
        C.fence(e, e.scalar_tensor_tensor(ap6[:, :, 4:5], ap6[:, :, 2:3],
                                          -1.0, ap6[:, :, 2:3], op0=ALU.mult,
                                          op1=ALU.mult), "DV")
        C.inc(e.tensor_tensor(ap6[:, :, 3:4], ap6[:, :, 3:4], ap6[:, :, 4:5],
                              op=ALU.add), "DV", 1)
    C.v(v2)
    t3 = C.tok("DV")

    def s2(e, t3=t3):
        C.wait(e, "DV", t3)
        C.fence(e, e.activation(ap6[:, :, 4:5], ap6[:, :, 3:4], AF.Ln,
                                bias=k["eps_v"][:, 0:1]), "SC")
        C.inc(e.activation(ap6[:, :, 4:5], ap6[:, :, 4:5], AF.Exp,
                           scale=-0.5), "SC", 1)
    C.s(s2)
    t4 = C.tok("SC")

    def v3(e, t4=t4):
        C.wait(e, "SC", t4)
        C.inc(e.scalar_tensor_tensor(ap6[:, :, 5:6], ap6[:, :, 2:3], -1.0,
                                     ap6[:, :, 4:5], op0=ALU.mult,
                                     op1=ALU.mult), "DV", 1)
    C.v(v3)


def emit_l1f(C, SCR):
    nc = C.nc
    k = C.k
    xs = C.xs
    if getattr(C, "f1_prep_pending", False):
        C.f1_prep_pending = False
        emit_weight_prep(C, SCR, [("f1_wg", HID, D), ("f1_wu", HID, D),
                                  ("f1_wd", D, HID)])
    HH2 = HID // 2
    with ExitStack() as ph:
        ec = ph.enter_context
        wgh = ec(nc.sbuf_tensor(C.u("f1_wgh"), [128, DC * HH2], BF16))
        wuh = ec(nc.sbuf_tensor(C.u("f1_wuh"), [128, DC * HH2], BF16))
        wdb = ec(nc.sbuf_tensor(C.u("f1_wdb"), [128, 2 * 4 * 512], BF16))
        h3f = ec(nc.sbuf_tensor(C.u("f1_h3f"), [128, DC * 512], BF16))
        hh = ec(nc.sbuf_tensor(C.u("f1_hh"), [128, HC * 512], BF16))
        sg = ec(nc.sbuf_tensor(C.u("f1_sg"), [128, 2 * 512], BF16))
        od = ec(nc.sbuf_tensor(C.u("f1_od"), [128, 4 * D], F32))
        lnp = ec(nc.sbuf_tensor(C.u("f1_lnp"), [128, 6 * NTILE], F32))
        htm2 = ec(nc.sbuf_tensor(C.u("f1_htm2"), [128, 2 * D], BF16))
        p_big = ec(nc.psum_tensor(C.u("f1_p_big"), [128, 4 * 512], F32))
        p_g = ec(nc.psum_tensor(C.u("f1_p_g"), [128, 2 * 512], F32))
        p_u = ec(nc.psum_tensor(C.u("f1_p_u"), [128, 2 * 512], F32))
        p_trv = p_big[:].bitcast(BF16)

        emit_pre_stats(C, lnp)
        ap6 = lnp[:].rearrange("p (i c) -> p i c", c=6)
        t_stats = C.tok("DV")

        tr_tok = [0] * NTILE
        dn_tok = {}
        odv_tok = {}
        lnq_tok = {}
        hhev_tok = {}
        wl_tok = {}
        gu_tok = {}
        for grp in range(NGRP):
            for ti in range(4):
                i = grp * 4 + ti

                def s(e, i=i):
                    C.wait(e, "DV", t_stats)
                    if i >= 2:
                        C.wtok(e, "DV", tr_tok[i - 2])
                    C.inc(e.activation(htm2[:, (i % 2) * D:(i % 2 + 1) * D],
                                       xs[:, i * D:(i + 1) * D], AF.Identity,
                                       scale=ap6[:, i, 4:5],
                                       bias=ap6[:, i, 5:6]), "SC", 1)
                C.s(s)
                ap_t = C.tok("SC")

                def t(e, i=i, ti=ti, ap_t=ap_t, grp=grp):
                    C.wtok(e, "SC", ap_t)
                    if i >= 2:
                        C.wtok(e, "DV", tr_tok[i - 2])
                    if grp > 0 and ti < 2:
                        C.wtok(e, "SC", odv_tok[(grp - 1, 3)])
                    for c in range(8):
                        ins = e.transpose(
                            p_trv[:, (i % 2) * 1024 + c * 128:
                                  (i % 2) * 1024 + (c + 1) * 128],
                            htm2[:, (i % 2) * D + c * 128:
                                 (i % 2) * D + (c + 1) * 128],
                            k["ident"][:])
                    C.inc(ins, "PE", 1)
                C.t(t)
                pe_t = C.tok("PE")

                def v(e, i=i, ti=ti, pe_t=pe_t):
                    C.wtok(e, "PE", pe_t)
                    for c in range(8):
                        ins = e.tensor_copy(
                            h3f[:, c * 512 + ti * 128: c * 512 + (ti + 1) * 128],
                            p_trv[:, (i % 2) * 1024 + c * 128:
                                  (i % 2) * 1024 + (c + 1) * 128])
                    C.inc(ins, "DV", 1)
                C.v(v)
                tr_tok[i] = C.tok("DV")

            for half in range(2):
                def y(e, grp=grp, half=half):
                    prev = gu_tok.get((grp, half * HHALF_G - 1)) if half else \
                        (gu_tok.get((grp - 1, HC - 1)) if grp > 0 else None)
                    if prev:
                        C.wtok(e, "PE", prev)
                    for dc in range(8):
                        C.inc(e.dma_start(
                            out=wgh[:, dc * HH2:(dc + 1) * HH2],
                            in_=C.scr["f1_wg_t"][dc * 128:(dc + 1) * 128,
                                                 half * HH2:(half + 1) * HH2]),
                            "WL", 16)
                        C.inc(e.dma_start(
                            out=wuh[:, dc * HH2:(dc + 1) * HH2],
                            in_=C.scr["f1_wu_t"][dc * 128:(dc + 1) * 128,
                                                 half * HH2:(half + 1) * HH2]),
                            "WL", 16)
                C.y(y)
                wl_tok[(grp, half)] = C.tok("WL")

                sc_base = C.cnt["SC"]
                sl_toks = {}
                gu_l = [0] * (HC // 2)
                for hc_l in range(HC // 2):
                    hc = half * (HC // 2) + hc_l
                    sl_toks[hc] = sc_base + hc_l + 1
                for hc_l in range(HC // 2):
                    hc = half * (HC // 2) + hc_l
                    pb = hc % 2

                    def t(e, grp=grp, half=half, hc_l=hc_l, hc=hc, pb=pb):
                        if hc_l == 0:
                            C.wait(e, "WL", wl_tok[(grp, half)])
                            C.wtok(e, "DV", tr_tok[grp * 4 + 3])
                        prev = hhev_tok.get((grp, hc - 2)) or (
                            hhev_tok.get((grp - 1, HC - 2 + hc))
                            if grp > 0 and hc < 2 else None)
                        if prev:
                            C.wtok(e, "DV", prev)
                        for dc in range(8):
                            e.matmul(p_g[:, pb * 512:(pb + 1) * 512],
                                     wgh[:, dc * HH2 + hc_l * 128:
                                         dc * HH2 + (hc_l + 1) * 128],
                                     h3f[:, dc * 512:(dc + 1) * 512],
                                     start=(dc == 0), stop=(dc == 7))
                        for dc in range(8):
                            ins = e.matmul(
                                p_u[:, pb * 512:(pb + 1) * 512],
                                wuh[:, dc * HH2 + hc_l * 128:
                                    dc * HH2 + (hc_l + 1) * 128],
                                h3f[:, dc * 512:(dc + 1) * 512],
                                start=(dc == 0), stop=(dc == 7))
                        C.inc(ins, "PE", 1)
                    C.t(t)
                    gu_tok[(grp, hc)] = C.tok("PE")
                    gu_l[hc_l] = C.tok("PE")

                    def v(e, hc=hc, pb=pb, sl_t=sl_toks[hc]):
                        C.wtok(e, "SC", sl_t)
                        C.inc(e.tensor_tensor(hh[:, hc * 512:(hc + 1) * 512],
                                              sg[:, pb * 512:(pb + 1) * 512],
                                              p_u[:, pb * 512:(pb + 1) * 512],
                                              op=ALU.mult), "DV", 1)
                    C.v(v)
                    hhev_tok[(grp, hc)] = C.tok("DV")

                def s(e, gu_l=tuple(gu_l), half=half, sc_base=sc_base):
                    for hc_l in range(HC // 2):
                        hc = half * (HC // 2) + hc_l
                        pb = hc % 2
                        C.wtok(e, "PE", gu_l[hc_l])
                        C.inc(e.activation(sg[:, pb * 512:(pb + 1) * 512],
                                           p_g[:, pb * 512:(pb + 1) * 512],
                                           AF.Silu, scale=1.0 / 127), "SC", 1)
                        assert C.cnt["SC"] == sc_base + hc_l + 1
                C.s(s)

            for dhalf in range(2):
                for hb in range(HC // 4):
                    buf = hb % 2

                    def y(e, grp=grp, dhalf=dhalf, hb=hb, buf=buf):
                        prev = dn_tok.get((grp, dhalf, hb - 2)) or (
                            dn_tok.get((grp, dhalf - 1, HC // 4 - 2 + hb))
                            if hb < 2 and dhalf > 0 else
                            (dn_tok.get((grp - 1, 1, HC // 4 - 2 + hb))
                             if hb < 2 and dhalf == 0 and grp > 0 else None))
                        if prev:
                            C.wtok(e, "PE", prev)
                        C.inc(e.dma_start(
                            out=wdb[:, buf * 2048:(buf + 1) * 2048].rearrange(
                                "p (c d) -> p c d", d=512),
                            in_=C.scr["f1_wd_t"][hb * 512:(hb + 1) * 512,
                                                 dhalf * 512:(dhalf + 1) * 512
                                                 ].rearrange(
                                "(c p) d -> p c d", p=128)),
                            "WD%d" % buf, 16)
                    C.y(y)
                    wd_t = C.tok("WD%d" % buf)

                    for hc_l in range(4):
                        hc = hb * 4 + hc_l

                        def t(e, grp=grp, dhalf=dhalf, hc=hc, hc_l=hc_l,
                              buf=buf, wd_t=wd_t):
                            if hc_l == 0:
                                C.wait(e, "WD%d" % buf, wd_t)
                            if hc == 0:
                                C.wtok(e, "DV", hhev_tok[(grp, HC - 1)])
                                if dhalf == 0:
                                    C.wtok(e, "DV", tr_tok[grp * 4 + 3])
                                else:
                                    C.wtok(e, "SC", odv_tok[(grp, 3, 0)])
                            for ti in range(4):
                                ins = e.matmul(
                                    p_big[:, ti * 512:(ti + 1) * 512],
                                    hh[:, hc * 512 + ti * 128:
                                       hc * 512 + (ti + 1) * 128],
                                    wdb[:, buf * 2048 + hc_l * 512:
                                        buf * 2048 + (hc_l + 1) * 512],
                                    start=(hc == 0), stop=(hc == HC - 1),
                                    skip_group_check=True)
                            C.inc(ins, "PE", 1)
                        C.t(t)
                    dn_tok[(grp, dhalf, hb)] = C.tok("PE")

                for ti in range(4):
                    def s(e, grp=grp, ti=ti, dhalf=dhalf):
                        C.wtok(e, "PE", dn_tok[(grp, dhalf, HC // 4 - 1)])
                        if grp > 0 and dhalf == 0:
                            C.wtok(e, "DV", lnq_tok[(grp - 1, ti)])
                        C.inc(e.activation(
                            od[:, ti * D + dhalf * 512: ti * D + (dhalf + 1) * 512],
                            p_big[:, ti * 512:(ti + 1) * 512], AF.Identity,
                            scale=1.0 / (127.0 * 127.0)), "SC", 1)
                    C.s(s)
                    odv_tok[(grp, ti, dhalf)] = C.tok("SC")
                odv_tok[(grp, 3)] = C.tok("SC")

            for ti in range(4):
                i = grp * 4 + ti

                def dummy(e, grp=grp, ti=ti):
                    C.wtok(e, "SC", odv_tok[(grp, ti, 1)])
                    C.inc(e.engine_nop(), "DV", 1)
                C.v(dummy)
                emit_ln_post(C, od[:, ti * D:(ti + 1) * D],
                             xs[:, i * D:(i + 1) * D])
                lnq_tok[(grp, ti)] = C.tok("DV")


HHALF_G = HC // 2


def build_program(phases):
    nc = bass.Bass("TRN2", target_bir_lowering=False)

    xin = nc.dram_tensor("x", [NSEQ, S, D], F32, kind="ExternalInput").ap()
    out = nc.dram_tensor("out", [NSEQ, S, D], F32, kind="ExternalOutput").ap()
    dbg = nc.dram_tensor("dbg", [NSEQ, 4, 128, D], F32, kind="ExternalOutput").ap()
    W = {}
    for nm, shp, dt in [
        ("l0_wq", [D, D], F32), ("l0_wk", [D, D], F32), ("l0_wv", [D, D], F32),
        ("l0_bq", [D], F32), ("l0_bk", [D], F32), ("l0_bv", [D], F32),
        ("f0_wr", [D, D], F32), ("f0_wk", [D, D], F32), ("f0_wv", [D, D], F32),
        ("f0_mur", [D], F32), ("f0_muk", [D], F32),
        ("l1_wq", [D, D], F32), ("l1_wk", [D, D], F32), ("l1_wv", [D, D], F32),
        ("l1_wp", [D, D], F32),
        ("f1_wg", [HID, D], I32), ("f1_wu", [HID, D], I32),
        ("f1_wd", [D, HID], I32),
    ]:
        W[nm] = nc.dram_tensor(nm, shp, dt, kind="ExternalInput").ap()

    need_f0 = "l0f" in phases
    need_f1 = "l1f" in phases
    SCR = {}
    if "l1a" in phases:
        SCR["qscr"] = nc.dram_tensor("qscr", [NGRP, 128, DC * 512], BF16,
                                     kind="Internal").ap()
    if need_f0:
        for nm in ("f0_wr", "f0_wk", "f0_wv"):
            SCR[nm + "_s1"] = nc.dram_tensor("s1_" + nm, [D, D], BF16,
                                             kind="Internal").ap()
            SCR[nm + "_t"] = nc.dram_tensor("t_" + nm, [D, D], BF16,
                                            kind="Internal").ap()
    if need_f1:
        for nm, R, Cc in (("f1_wg", HID, D), ("f1_wu", HID, D),
                          ("f1_wd", D, HID)):
            SCR[nm + "_s1"] = nc.dram_tensor("s1_" + nm, [R, Cc], BF16,
                                             kind="Internal").ap()
            SCR[nm + "_t"] = nc.dram_tensor("t_" + nm, [Cc, R], BF16,
                                            kind="Internal").ap()

    with ExitStack() as stack:
        ec = stack.enter_context
        xs = ec(nc.sbuf_tensor("xs", [128, NTILE * D], F32))
        ident = ec(nc.sbuf_tensor("ident", [128, 128], BF16))
        ident65 = ec(nc.sbuf_tensor("ident65", [65, 65], BF16))
        ones_col = ec(nc.sbuf_tensor("ones_col", [128, 1], BF16))
        ones_row = ec(nc.sbuf_tensor("ones_row", [1, 128], BF16))
        cmask = ec(nc.sbuf_tensor("cmask", [128, 128], F32))
        cnt16 = ec(nc.sbuf_tensor("cnt16", [128, NTILE], F32))
        eps_v = ec(nc.sbuf_tensor("eps_v", [128, 1], F32))
        st8 = ec(nc.sbuf_tensor("st8", [128, 8], F32))
        scr_a = ec(nc.sbuf_tensor("scr_a", [128, D], F32))
        scr_b = ec(nc.sbuf_tensor("scr_b", [128, D], F32))

        block = ec(nc.Block())
        C = Ctx(nc, block, stack)
        C.dram = {"x": xin, "out": out, "dbg": dbg, **W}
        C.scr = SCR
        C.xs = xs
        C.k = dict(ident=ident, ident65=ident65, ones_col=ones_col,
                   ones_row=ones_row, cmask=cmask, cnt16=cnt16, eps_v=eps_v,
                   st8=st8, scr_a=scr_a, scr_b=scr_b)

        emit_consts(C)
        C.f0_prep_pending = need_f0
        C.f1_prep_pending = need_f1

        for seq in range(NSEQ):
            C.seq = seq
            emit_load_x(C, seq)
            C.barrier()
            if "l0a" in phases:
                emit_l0a(C)
                C.barrier()
            if "l0f" in phases:
                emit_l0f(C, SCR)
                C.barrier()
            if "l1a" in phases:
                emit_l1a(C)
                C.barrier()
            if "l1f" in phases:
                emit_l1f(C, SCR)
                C.barrier()
            emit_store_x(C, seq)
            C.barrier()
    return nc


def in_map_for_core(inputs, core):
    x = np.ascontiguousarray(inputs["L0_attn_X"][2 * core: 2 * core + 2])
    m = {"x": x}
    ren = {"l0_wq": "L0_attn_Wq", "l0_wk": "L0_attn_Wk", "l0_wv": "L0_attn_Wv",
           "l0_bq": "L0_attn_bq", "l0_bk": "L0_attn_bk", "l0_bv": "L0_attn_bv",
           "f0_wr": "L0_ffn_W_r", "f0_wk": "L0_ffn_W_k", "f0_wv": "L0_ffn_W_v",
           "f0_mur": "L0_ffn_mu_r", "f0_muk": "L0_ffn_mu_k",
           "l1_wq": "L1_attn_Wq", "l1_wk": "L1_attn_Wk", "l1_wv": "L1_attn_Wv",
           "l1_wp": "L1_attn_Wpost",
           "f1_wg": "L1_ffn_w_gate", "f1_wu": "L1_ffn_w_up",
           "f1_wd": "L1_ffn_w_down"}
    for a, b in ren.items():
        m[a] = np.ascontiguousarray(inputs[b])
    return m


def kernel(**inputs):
    nc = build_program(("l0a", "l0f", "l1a", "l1f"))
    in_maps = [in_map_for_core(inputs, c) for c in range(8)]
    res = run_bass_kernel_spmd(nc, in_maps, core_ids=list(range(8)))
    return np.concatenate([np.asarray(r["out"]) for r in res.results], axis=0)


# version 25
# speedup vs baseline: 1.1999x; 1.0018x over previous
import numpy as np
from contextlib import ExitStack

import concourse.bass as bass
import concourse.mybir as mybir
from concourse.bass_utils import run_bass_kernel_spmd

F32 = mybir.dt.float32
BF16 = mybir.dt.bfloat16
I32 = mybir.dt.int32
AF = mybir.ActivationFunctionType
ALU = mybir.AluOpType
AX = mybir.AxisListType

NTILE = 16
NGRP = 4
D = 1024
DC = 8
S = 2048
HID = 3072
HC = 24
NSEQ = 2
EPS = 1e-5
H, E = 16, 64

_SECT = {"tensor": "PE", "scalar": "SC", "vector": "DV", "gpsimd": "GP",
         "sync": "SY"}


class Ctx:
    def __init__(self, nc, block, stack):
        self.nc = nc
        self.block = block
        self.stack = stack
        self.sem = {}
        self.cnt = {}
        for name in ("LD", "LD2", "ST", "PE", "SC", "DV", "GP", "SY",
                     "WD0", "WD1", "WL", "WP", "WP2"):
            self.sem[name] = stack.enter_context(nc.semaphore(name))
            self.cnt[name] = 0
        self.bar_s = stack.enter_context(nc.sbuf_tensor("bar_s", [1, 8], F32))

    def t(self, f):
        self.block.tensor(f)

    def s(self, f):
        self.block.scalar(f)

    def v(self, f):
        self.block.vector(f)

    def g(self, f):
        self.block.gpsimd(f)

    def y(self, f):
        self.block.sync(f)

    def u(self, name):
        self._uid = getattr(self, "_uid", 0) + 1
        return f"{name}__{self._uid}"

    def inc(self, instr, name, n=1):
        instr.then_inc(self.sem[name], n)
        self.cnt[name] += n
        return self.cnt[name]

    def wait(self, eng, name, val=None):
        eng.wait_ge(self.sem[name], self.cnt[name] if val is None else val)

    def fence(self, e, instr, name):
        self.inc(instr, name, 1)
        e.wait_ge(self.sem[name], self.cnt[name])

    def tok(self, name):
        return self.cnt[name]

    def wtok(self, e, name, val):
        if val > 0:
            e.wait_ge(self.sem[name], val)

    def barrier(self):
        def mk_inc(engname):
            def f(e):
                if engname == "scalar":
                    ins = e.memzero(self.bar_s[:])
                elif engname in ("vector", "gpsimd"):
                    ins = e.engine_nop()
                else:
                    ins = e.nop()
                self.inc(ins, _SECT[engname], 1)
            return f
        for e in ("scalar", "vector", "gpsimd", "sync"):
            getattr(self, e[0] if e != "sync" else "y")(mk_inc(e))
        snap = dict(self.cnt)

        def mk_wait():
            def f(e):
                for nm, v in snap.items():
                    if v > 0:
                        e.wait_ge(self.sem[nm], v)
            return f
        for e in ("tensor", "scalar", "vector", "gpsimd", "sync"):
            getattr(self, e[0] if e != "sync" else "y")(mk_wait())


def emit_consts(C):
    k = C.k

    def g(e):
        C.fence(e, e.memset(k["ident"][:], 0.0), "GP")
        e.affine_select(out=k["ident"][:], in_=k["ident"][:],
                        compare_op=ALU.not_equal, fill=1.0,
                        base=0, pattern=[[-1, 128]], channel_multiplier=1)
        C.fence(e, e.memset(k["ident65"][:], 0.0), "GP")
        e.affine_select(out=k["ident65"][:], in_=k["ident65"][:],
                        compare_op=ALU.not_equal, fill=1.0,
                        base=0, pattern=[[-1, 65]], channel_multiplier=1)
        e.memset(k["ones_col"][:], 1.0)
        e.memset(k["ones_row"][:], 1.0)
        C.fence(e, e.memset(k["cmask"][:], 0.0), "GP")
        e.affine_select(out=k["cmask"][:], in_=k["cmask"][:],
                        compare_op=ALU.is_ge, fill=-1.0e9,
                        base=0, pattern=[[-1, 128]], channel_multiplier=1)
        e.iota(k["cnt16"][:], [[-128, NTILE]], base=2047, channel_multiplier=-1,
               allow_small_or_imprecise_dtypes=True)
        e.memset(k["eps_v"][:], EPS)
        C.inc(e.engine_nop(), "GP", 1)
    C.g(g)


def emit_weight_prep(C, SCR, jobs, stgw=None):
    if not jobs:
        return
    nc = C.nc
    stgw = stgw or 12 * D
    stgtw = max(R for _, R, _ in jobs)
    pstack = ExitStack()
    stg = pstack.enter_context(nc.sbuf_tensor(C.u("wprep"),
                                              [128, max(stgw, stgtw)], BF16))
    stgT = stg

    for nm, R, Cc in jobs:
        src = C.dram[nm]
        scr1 = SCR[nm + "_s1"]
        scr2 = SCR[nm + "_t"]
        nch = R // 128

        hchunks = max(1, stgw // Cc)
        for c0 in range(0, nch, hchunks):
            c1 = min(c0 + hchunks, nch)

            def g(e, src=src, c0=c0, c1=c1, Cc=Cc):
                C.wait(e, "WP2")
                C.inc(e.dma_start(
                    out=stg[:, 0:(c1 - c0) * Cc].rearrange(
                        "p (c d) -> p c d", d=Cc),
                    in_=src.rearrange("(c p) d -> p c d",
                                      p=128)[:, c0:c1]),
                    "WP", 16)
            C.g(g)

            def y1(e, scr1=scr1, c0=c0, c1=c1, Cc=Cc):
                C.wait(e, "WP")
                C.inc(e.dma_start(
                    out=scr1.rearrange("(c p) d -> p c d", p=128)[:, c0:c1],
                    in_=stg[:, 0:(c1 - c0) * Cc].rearrange(
                        "p (c d) -> p c d", d=Cc)),
                    "WP2", 16)
            C.y(y1)

        for cc in range(Cc // 128):
            def y2(e, scr1=scr1, scr2=scr2, cc=cc, R=R):
                C.wait(e, "WP2")
                C.inc(e.dma_start(out=stgT[:, 0:R],
                                  in_=scr1[:, cc * 128:(cc + 1) * 128],
                                  transpose=True), "WP2", 16)
                C.wait(e, "WP2")
                C.inc(e.dma_start(out=scr2[cc * 128:(cc + 1) * 128, :],
                                  in_=stgT[:, 0:R]), "WP2", 16)
            C.y(y2)
    pstack.close()


def emit_load_x(C, seq):
    def y(e):
        C.inc(e.dma_start(out=C.xs[:].rearrange("p (i d) -> p i d", d=D),
                          in_=C.dram["x"][seq].rearrange("(i p) d -> p i d",
                                                         p=128)),
              "LD2", 16)
    C.y(y)


def emit_store_x(C, seq):
    def y(e):
        C.inc(e.dma_start(
            out=C.dram["out"][seq].rearrange("(i p) d -> p i d", p=128),
            in_=C.xs[:].rearrange("p (i d) -> p i d", d=D)), "ST", 16)
    C.y(y)


def emit_transpose_1024(C, src_fn, psum_tr, dst_fn, nchunk=8):
    k = C.k

    def t(e):
        C.wait(e, "DV")
        C.wait(e, "SC")
        for c in range(nchunk):
            ins = e.transpose(psum_tr[:, c * 128:(c + 1) * 128], src_fn(c),
                              k["ident"][:])
        C.inc(ins, "PE", 1)
    C.t(t)

    def v(e):
        C.wait(e, "PE")
        for c in range(nchunk):
            ins = e.tensor_copy(dst_fn(c), psum_tr[:, c * 128:(c + 1) * 128])
        C.inc(ins, "DV", 1)
    C.v(v)


L0A_STAGE = 5

def emit_l0a(C):
    nc = C.nc
    k = C.k
    xs = C.xs
    with ExitStack() as ph:
        ec = ph.enter_context
        wq = ec(nc.sbuf_tensor(C.u("l0_wq_s"), [128, DC * D], BF16))
        wk = ec(nc.sbuf_tensor(C.u("l0_wk_s"), [128, DC * D], BF16))
        wv = ec(nc.sbuf_tensor(C.u("l0_wv_s"), [128, DC * D], BF16))
        bq = ec(nc.sbuf_tensor(C.u("l0_bq_s"), [128, 8], F32))
        bk = ec(nc.sbuf_tensor(C.u("l0_bk_s"), [128, 8], F32))
        bvr_bf = ec(nc.sbuf_tensor(C.u("l0_bv_row_bf"), [1, D], BF16))
        bvb = ec(nc.sbuf_tensor(C.u("l0_bv_bc"), [128, D], BF16))
        h0 = ec(nc.sbuf_tensor(C.u("h0_fm"), [128, DC * S], BF16))
        kf = ec(nc.sbuf_tensor(C.u("k_fm"), [128, DC * S], BF16))
        qg = ec(nc.sbuf_tensor(C.u("q_fm_g"), [128, DC * 512], BF16))
        e_scr = ec(nc.sbuf_tensor(C.u("e_scr"), [128, S], BF16))
        sm = ec(nc.sbuf_tensor(C.u("l0_sm"), [128, 8], F32))
        lnp = ec(nc.sbuf_tensor(C.u("l0_lnp"), [128, 6 * NTILE], F32))
        htm2 = ec(nc.sbuf_tensor(C.u("l0_htm2"), [128, 2 * D], BF16))

        def g(e):
            C.wait(e, "LD2")
            for nm, dst in (("l0_wq", wq), ("l0_wk", wk), ("l0_wv", wv)):
                C.inc(e.dma_start(out=dst[:].rearrange("p (c d) -> p c d", d=D),
                                  in_=C.dram[nm].rearrange(
                    "(c p) d -> p c d", p=128)), "LD", 16)
            for c in range(8):
                C.inc(e.dma_start(out=bq[:, c:c + 1],
                                  in_=C.dram["l0_bq"][c * 128:(c + 1) * 128][:, None]),
                      "LD", 16)
                C.inc(e.dma_start(out=bk[:, c:c + 1],
                                  in_=C.dram["l0_bk"][c * 128:(c + 1) * 128][:, None]),
                      "LD", 16)
            C.inc(e.dma_start(out=bvr_bf[:], in_=C.dram["l0_bv"][None, :]),
                  "LD", 16)
        C.g(g)
        t_loads = C.tok("LD")

        if getattr(C, "f0_prep_pending", False):
            C.f0_prep_pending = False
            emit_weight_prep(C, C.scr, [("f0_wr", D, D), ("f0_wk", D, D),
                                        ("f0_wv", D, D)], stgw=D)
        emit_pre_stats(C, lnp)
        ap6 = lnp[:].rearrange("p (i c) -> p i c", c=6)
        t_stats = C.tok("DV")

        p_tr = ec(nc.psum_tensor(C.u("p_tr"), [128, 2 * D], BF16))
        p_mm = ec(nc.psum_tensor(C.u("p_mm"), [128, 2 * 512], F32))
        bc_toks = []
        for half in range(2):
            def t0(e, half=half):
                C.wait(e, "LD", t_loads)
                if half == 0:
                    C.wait(e, "DV", )
                C.inc(e.matmul(p_mm[:, half * 512:(half + 1) * 512],
                               k["ones_row"][:],
                               bvr_bf[:, half * 512:(half + 1) * 512],
                               start=True, stop=True), "PE", 1)
            C.t(t0)
            tok_pe = C.tok("PE")

            def v0(e, half=half, tok_pe=tok_pe):
                C.wtok(e, "PE", tok_pe)
                C.inc(e.tensor_copy(bvb[:, half * 512:(half + 1) * 512],
                                    p_mm[:, half * 512:(half + 1) * 512]),
                      "DV", 1)
            C.v(v0)
        t_bvb = C.tok("DV")

        ap_tok = [0] * NTILE
        tr_tok = [0] * NTILE
        kproj_tok = [0] * NGRP
        kev_tok = {}
        for grp in range(NGRP):
            hv_sc, hv_dv = C.cnt["SC"], C.cnt["DV"]

            def s(e, grp=grp, hv_dv=hv_dv,
                  pv=tuple(tr_tok[max(0, grp * 4 - 2): grp * 4])):
                C.wait(e, "DV", t_stats)
                for ti in range(4):
                    i = grp * 4 + ti
                    if ti >= 2:
                        C.wait(e, "DV", hv_dv + ti - 1)
                    elif i >= 2:
                        C.wait(e, "DV", pv[ti])
                    C.inc(e.activation(htm2[:, (i % 2) * D:(i % 2 + 1) * D],
                                       xs[:, i * D:(i + 1) * D], AF.Identity,
                                       scale=ap6[:, i, 4:5],
                                       bias=ap6[:, i, 5:6]), "SC", 1)
            C.s(s)
            for ti in range(4):
                ap_tok[grp * 4 + ti] = hv_sc + ti + 1
            for ti in range(4):
                i = grp * 4 + ti

                def t(e, i=i):
                    C.wtok(e, "SC", ap_tok[i])
                    if i >= 2:
                        C.wtok(e, "DV", tr_tok[i - 2])
                    for c in range(8):
                        ins = e.transpose(
                            p_tr[:, (i % 2) * D + c * 128:
                                 (i % 2) * D + (c + 1) * 128],
                            htm2[:, (i % 2) * D + c * 128:
                                 (i % 2) * D + (c + 1) * 128],
                            k["ident"][:])
                    C.inc(ins, "PE", 1)
                C.t(t)
                tok_pe = C.tok("PE")

                def v(e, i=i, tok_pe=tok_pe):
                    C.wtok(e, "PE", tok_pe)
                    for c in range(8):
                        ins = e.tensor_copy(
                            h0[:, c * S + i * 128: c * S + (i + 1) * 128],
                            p_tr[:, (i % 2) * D + c * 128:
                                 (i % 2) * D + (c + 1) * 128])
                    C.inc(ins, "DV", 1)
                C.v(v)
                tr_tok[i] = C.tok("DV")
                assert tr_tok[i] == hv_dv + ti + 1

            for oc in range(8):
                half = oc % 2

                def t(e, oc=oc, grp=grp, half=half):
                    if oc == 0:
                        C.wtok(e, "DV", tr_tok[grp * 4 + 3])
                        if grp == 0:
                            C.wait(e, "DV", t_bvb)
                    if (grp, oc) in kev_tok:
                        pass
                    prev = kev_tok.get((grp, oc - 2)) or (
                        kev_tok.get((grp - 1, 6 + oc)) if oc < 2 else None)
                    if prev:
                        C.wtok(e, "DV", prev)
                    for dc in range(8):
                        ins = e.matmul(
                            p_mm[:, half * 512:(half + 1) * 512],
                            wk[:, dc * D + oc * 128: dc * D + (oc + 1) * 128],
                            h0[:, dc * S + grp * 512: dc * S + (grp + 1) * 512],
                            start=(dc == 0), stop=(dc == 7))
                    C.inc(ins, "PE", 1)
                C.t(t)
                tok_pe = C.tok("PE")

                def v(e, oc=oc, grp=grp, half=half, tok_pe=tok_pe):
                    C.wtok(e, "PE", tok_pe)
                    C.inc(e.tensor_scalar(
                        kf[:, oc * S + grp * 512: oc * S + (grp + 1) * 512],
                        p_mm[:, half * 512:(half + 1) * 512], bk[:, oc:oc + 1],
                        None, op0=ALU.add), "DV", 1)
                C.v(v)
                kev_tok[(grp, oc)] = C.tok("DV")
            kproj_tok[grp] = C.tok("DV")

        p_s = ec(nc.psum_tensor(C.u("p_s"), [128, S], F32))
        exp_tok = [0] * (NTILE + 1)
        oev_tok = [0] * NTILE
        q_tok = [0] * NGRP
        qev_tok = {}
        for i in range(NTILE):
            grp, ti = divmod(i, 4)
            if ti == 0:
                for oc in range(8):
                    def t(e, oc=oc, grp=grp, i=i):
                        if oc == 0:
                            C.wtok(e, "DV", kproj_tok[NGRP - 1])
                            if i > 0:
                                C.wtok(e, "DV", oev_tok[i - 1])
                        prev = qev_tok.get((grp, oc - 1))
                        if prev:
                            C.wtok(e, "DV", prev)
                        for dc in range(8):
                            ins = e.matmul(
                                p_mm[:, 0:512],
                                wq[:, dc * D + oc * 128: dc * D + (oc + 1) * 128],
                                h0[:, dc * S + grp * 512:
                                   dc * S + (grp + 1) * 512],
                                start=(dc == 0), stop=(dc == 7))
                        C.inc(ins, "PE", 1)
                    C.t(t)
                    tok_pe = C.tok("PE")

                    def vv(e, oc=oc, tok_pe=tok_pe):
                        C.wtok(e, "PE", tok_pe)
                        C.inc(e.tensor_scalar(qg[:, oc * 512:(oc + 1) * 512],
                                              p_mm[:, 0:512], bq[:, oc:oc + 1],
                                              None, op0=ALU.add), "DV", 1)
                    C.v(vv)
                    qev_tok[(grp, oc)] = C.tok("DV")
                q_tok[grp] = C.tok("DV")

            ncols = (i + 1) * 128

            def t(e, ti=ti, ncols=ncols, grp=grp, i=i):
                C.wtok(e, "DV", q_tok[grp])
                C.wtok(e, "SC", exp_tok[i - 1] if i > 0 else 0)
                if i > 0:
                    C.wtok(e, "DV", oev_tok[i - 1] if False else 0)
                for dc in range(8):
                    for blk in range((ncols + 511) // 512):
                        n = min(512, ncols - blk * 512)
                        ins = e.matmul(
                            p_s[:, blk * 512: blk * 512 + n],
                            qg[:, dc * 512 + ti * 128: dc * 512 + (ti + 1) * 128],
                            kf[:, dc * S + blk * 512: dc * S + blk * 512 + n],
                            start=(dc == 0), stop=(dc == 7))
                C.inc(ins, "PE", 1)
            C.t(t)
            sc_pe = C.tok("PE")

            def t2(e, i=i):
                if i > 0:
                    C.wtok(e, "DV", oev_tok[i - 1])
                else:
                    C.wait(e, "DV", t_bvb)
                    C.wtok(e, "DV", kproj_tok[NGRP - 1])
                C.wtok(e, "DV", qev_tok.get((i // 4, 7), 0))
                for dc in range(8):
                    for half in range(2):
                        ins = e.matmul(
                            p_mm[:, half * 512:(half + 1) * 512],
                            h0[:, dc * S + i * 128: dc * S + (i + 1) * 128],
                            wv[:, dc * D + half * 512: dc * D + (half + 1) * 512],
                            start=(dc == 0), stop=(dc == 7))
                C.inc(ins, "PE", 1)
            C.t(t2)
            v_pe = C.tok("PE")

            def v(e, ncols=ncols, sc_pe=sc_pe):
                C.wtok(e, "PE", sc_pe)
                C.fence(e, e.tensor_tensor(p_s[:, ncols - 128: ncols],
                                           p_s[:, ncols - 128: ncols],
                                           k["cmask"][:], op=ALU.add), "DV")
                C.fence(e, e.tensor_reduce(sm[:, 0:1], p_s[:, 0:ncols],
                                           axis=AX.X, op=ALU.max), "DV")
                C.fence(e, e.tensor_scalar(sm[:, 1:2], sm[:, 0:1], 1.0 / 32,
                                           None, op0=ALU.mult), "DV")
                C.fence(e, e.tensor_scalar(sm[:, 1:2], sm[:, 1:2], 0.0, None,
                                           op0=ALU.max), "DV")
                C.inc(e.tensor_scalar(sm[:, 2:3], sm[:, 1:2], -1.0, None,
                                      op0=ALU.mult), "DV", 1)
            C.v(v)
            mx_tok = C.tok("DV")

            def s(e, ncols=ncols, mx_tok=mx_tok):
                C.wtok(e, "DV", mx_tok)
                e.activation(e_scr[:, 0:ncols], p_s[:, 0:ncols], AF.Exp,
                             scale=1.0 / 32, bias=sm[:, 2:3],
                             accum_out=sm[:, 3:4])
                C.inc(e.activation(sm[:, 4:5], sm[:, 2:3], AF.Exp), "SC", 1)
            C.s(s)
            exp_tok[i] = C.tok("SC")

            def v(e, i=i, v_pe=v_pe):
                C.wtok(e, "SC", exp_tok[i])
                C.fence(e, e.tensor_tensor(sm[:, 5:6], k["cnt16"][:, i:i + 1],
                                           sm[:, 4:5], op=ALU.mult), "DV")
                C.fence(e, e.tensor_tensor(sm[:, 5:6], sm[:, 5:6], sm[:, 3:4],
                                           op=ALU.add), "DV")
                C.fence(e, e.reciprocal(sm[:, 6:7], sm[:, 5:6]), "DV")
                C.fence(e, e.tensor_tensor(sm[:, 6:7], sm[:, 6:7], sm[:, 5:6],
                                           op=ALU.mult), "DV")
                C.wtok(e, "PE", v_pe)
                C.fence(e, e.tensor_tensor(k["scr_a"][:], p_mm[:], bvb[:],
                                           op=ALU.add), "DV")
                C.inc(e.tensor_scalar(k["scr_a"][:], k["scr_a"][:], sm[:, 6:7], None,
                                      op0=ALU.mult), "DV", 1)
            C.v(v)
            oev_tok[i] = C.tok("DV")

            emit_ln_post(C, k["scr_a"][:], xs[:, i * D:(i + 1) * D])


def emit_ln_post(C, o_ap, xs_ap):
    k = C.k
    st = k["st8"]

    def v1(e):
        C.wait(e, "DV")
        C.wait(e, "SC")
        C.fence(e, e.tensor_reduce(st[:, 0:1], o_ap, axis=AX.X, op=ALU.add),
                "DV")
        C.fence(e, e.tensor_scalar(st[:, 2:3], st[:, 0:1], 1.0 / D, None,
                                   op0=ALU.mult), "DV")
        C.inc(e.scalar_tensor_tensor(st[:, 4:5], st[:, 2:3], -1.0,
                                     st[:, 2:3], op0=ALU.mult,
                                     op1=ALU.mult), "DV", 1)
    C.v(v1)
    t3 = C.tok("DV")
    t3c = t3 + 2

    def s1(e, t3=t3, t3c=t3c):
        C.wait(e, "DV", t3)
        C.inc(e.activation(k["scr_b"][:], o_ap, AF.Square,
                           accum_out=st[:, 1:2]), "SC", 1)
        C.wait(e, "DV", t3c)
        C.fence(e, e.activation(st[:, 4:5], st[:, 3:4], AF.Ln,
                                bias=k["eps_v"][:, 0:1]), "SC")
        C.inc(e.activation(st[:, 5:6], st[:, 4:5], AF.Exp, scale=-0.5),
              "SC", 1)
    C.s(s1)
    t3b = C.tok("SC") - 2
    t4 = C.tok("SC")

    def v2(e, t3b=t3b):
        C.wait(e, "SC", t3b)
        C.fence(e, e.tensor_scalar(st[:, 3:4], st[:, 1:2], 1.0 / D, None,
                                   op0=ALU.mult), "DV")
        C.inc(e.tensor_tensor(st[:, 3:4], st[:, 3:4], st[:, 4:5], op=ALU.add),
              "DV", 1)
    C.v(v2)

    def v3(e, t4=t4):
        C.wait(e, "SC", t4)
        C.fence(e, e.scalar_tensor_tensor(st[:, 6:7], st[:, 2:3], -1.0,
                                          st[:, 5:6], op0=ALU.mult,
                                          op1=ALU.mult), "DV")
        C.fence(e, e.tensor_scalar(k["scr_b"][:], o_ap, st[:, 5:6],
                                   st[:, 6:7], op0=ALU.mult, op1=ALU.add),
                "DV")
        C.inc(e.tensor_tensor(xs_ap, xs_ap, k["scr_b"][:], op=ALU.add),
              "DV", 1)
    C.v(v3)


def emit_l0f(C, SCR):
    nc = C.nc
    k = C.k
    xs = C.xs
    with ExitStack() as ph:
        ec = ph.enter_context
        wrT = ec(nc.sbuf_tensor(C.u("f0_wrT"), [128, DC * D], BF16))
        wkT = ec(nc.sbuf_tensor(C.u("f0_wkT"), [128, DC * D], BF16))
        wvT = ec(nc.sbuf_tensor(C.u("f0_wvT"), [128, DC * D], BF16))
        mur = ec(nc.sbuf_tensor(C.u("f0_mur_s"), [128, 8], F32))
        muk = ec(nc.sbuf_tensor(C.u("f0_muk_s"), [128, 8], F32))
        murc = ec(nc.sbuf_tensor(C.u("f0_murc"), [128, 8], F32))
        mukc = ec(nc.sbuf_tensor(C.u("f0_mukc"), [128, 8], F32))
        hext = ec(nc.sbuf_tensor(C.u("f0_hext"), [128, 8 * 513], BF16))
        xr = ec(nc.sbuf_tensor(C.u("f0_xr"), [128, 8 * 512], BF16))
        xk = ec(nc.sbuf_tensor(C.u("f0_xk"), [128, 8 * 512], BF16))
        kact = ec(nc.sbuf_tensor(C.u("f0_kact"), [128, 8 * 512], BF16))
        tmp_bf = ec(nc.sbuf_tensor(C.u("f0_tmp"), [128, 512], BF16))
        krel = ec(nc.sbuf_tensor(C.u("f0_krel"), [128, 2 * 512], BF16))
        rtm = ec(nc.sbuf_tensor(C.u("f0_rtm"), [128, D], F32))
        lnp = ec(nc.sbuf_tensor(C.u("f0_lnp"), [128, 6 * NTILE], F32))
        htm2 = ec(nc.sbuf_tensor(C.u("f0_htm2"), [128, 2 * D], BF16))
        p_tr = ec(nc.psum_tensor(C.u("f0_p_tr"), [128, 2 * D], BF16))
        p_k = ec(nc.psum_tensor(C.u("f0_p_k"), [128, 2 * 512], F32))
        p_r = ec(nc.psum_tensor(C.u("f0_p_r"), [128, D], F32))
        p_kv = ec(nc.psum_tensor(C.u("f0_p_kv"), [128, D], F32))

        def y(e):
            C.wait(e, "LD2")
            for nm, dst in (("f0_wr_t", wrT), ("f0_wk_t", wkT),
                            ("f0_wv_t", wvT)):
                C.inc(e.dma_start(out=dst[:].rearrange("p (c d) -> p c d", d=D),
                                  in_=C.scr[nm].rearrange(
                    "(c p) d -> p c d", p=128)), "LD2", 16)
        C.y(y)
        t_wl = C.tok("LD2")

        def g(e):
            C.wait(e, "LD2")
            for c in range(8):
                C.inc(e.dma_start(out=mur[:, c:c + 1],
                                  in_=C.dram["f0_mur"][c * 128:(c + 1) * 128][:, None]),
                      "LD", 16)
                C.inc(e.dma_start(out=muk[:, c:c + 1],
                                  in_=C.dram["f0_muk"][c * 128:(c + 1) * 128][:, None]),
                      "LD", 16)
            for c in range(8):
                ins = e.memset(hext[:, c * 513: c * 513 + 1], 0.0)
            C.inc(ins, "GP", 1)
        C.g(g)
        t_mu = C.tok("LD")
        t_hx0 = C.tok("GP")

        if getattr(C, "f1_prep_pending", False):
            C.f1_prep_pending = False
            emit_weight_prep(C, SCR, [("f1_wg", HID, D), ("f1_wu", HID, D),
                                      ("f1_wd", D, HID)])

        def v0(e):
            C.wait(e, "LD", t_mu)
            C.fence(e, e.tensor_scalar(murc[:], mur[:], -1.0, 1.0,
                                       op0=ALU.mult, op1=ALU.add), "DV")
            C.inc(e.tensor_scalar(mukc[:], muk[:], -1.0, 1.0, op0=ALU.mult,
                                  op1=ALU.add), "DV", 1)
        C.v(v0)
        t_muc = C.tok("DV")

        emit_pre_stats(C, lnp)
        ap6 = lnp[:].rearrange("p (i c) -> p i c", c=6)
        t_stats = C.tok("DV")

        tr_tok = [0] * NTILE
        mix_tok = [0] * NGRP
        carry_tok = [0] * NGRP
        kact_tok = {}
        kev_sc = {}
        o_tok = [0] * NTILE
        rr_tok = [0] * NTILE
        kvp_tok = [0] * NTILE
        for grp in range(NGRP):
            if grp > 0:
                def v(e, grp=grp):
                    C.wtok(e, "DV", mix_tok[grp - 1])
                    for c in range(8):
                        ins = e.tensor_copy(hext[:, c * 513: c * 513 + 1],
                                            hext[:, c * 513 + 512:
                                                 c * 513 + 513])
                    C.inc(ins, "DV", 1)
                C.v(v)
                carry_tok[grp] = C.tok("DV")

            hv_sc, hv_dv = C.cnt["SC"], C.cnt["DV"]

            def s(e, grp=grp, hv_dv=hv_dv,
                  pv=tuple(tr_tok[max(0, grp * 4 - 2): grp * 4])):
                C.wait(e, "DV", t_stats)
                for ti in range(4):
                    i = grp * 4 + ti
                    if ti >= 2:
                        C.wait(e, "DV", hv_dv + ti - 1)
                    elif i >= 2:
                        C.wait(e, "DV", pv[ti])
                    C.inc(e.activation(htm2[:, (i % 2) * D:(i % 2 + 1) * D],
                                       xs[:, i * D:(i + 1) * D], AF.Identity,
                                       scale=ap6[:, i, 4:5],
                                       bias=ap6[:, i, 5:6]), "SC", 1)
            C.s(s)
            for ti in range(4):
                i = grp * 4 + ti
                ap_t = hv_sc + ti + 1

                def t(e, i=i, ap_t=ap_t, grp=grp, ti=ti):
                    C.wtok(e, "SC", ap_t)
                    if i >= 2:
                        C.wtok(e, "DV", tr_tok[i - 2])
                    for c in range(8):
                        ins = e.transpose(
                            p_tr[:, (i % 2) * D + c * 128:
                                 (i % 2) * D + (c + 1) * 128],
                            htm2[:, (i % 2) * D + c * 128:
                                 (i % 2) * D + (c + 1) * 128],
                            k["ident"][:])
                    C.inc(ins, "PE", 1)
                C.t(t)
                pe_t = C.tok("PE")

                def v(e, i=i, ti=ti, grp=grp, pe_t=pe_t):
                    C.wtok(e, "PE", pe_t)
                    if ti == 0 and grp > 0:
                        C.wtok(e, "DV", carry_tok[grp])
                    if grp == 0 and ti == 0:
                        C.wait(e, "GP", t_hx0)
                    for c in range(8):
                        ins = e.tensor_copy(
                            hext[:, c * 513 + 1 + ti * 128:
                                 c * 513 + 1 + (ti + 1) * 128],
                            p_tr[:, (i % 2) * D + c * 128:
                                 (i % 2) * D + (c + 1) * 128])
                    C.inc(ins, "DV", 1)
                C.v(v)
                tr_tok[i] = C.tok("DV")

            def v(e, grp=grp):
                C.wtok(e, "DV", tr_tok[grp * 4 + 3])
                C.wait(e, "DV", t_muc)
                if grp > 0:
                    C.wtok(e, "PE", kvp_tok[grp * 4 - 1])
                for c in range(8):
                    h1 = hext[:, c * 513 + 1: c * 513 + 513]
                    h1p = hext[:, c * 513: c * 513 + 512]
                    C.fence(e, e.tensor_scalar(tmp_bf[:], h1p, murc[:, c:c + 1],
                                               None, op0=ALU.mult), "DV")
                    C.fence(e, e.scalar_tensor_tensor(
                        xr[:, c * 512:(c + 1) * 512], h1, mur[:, c:c + 1],
                        tmp_bf[:], op0=ALU.mult, op1=ALU.add), "DV")
                    C.fence(e, e.tensor_scalar(tmp_bf[:], h1p, mukc[:, c:c + 1],
                                               None, op0=ALU.mult), "DV")
                    ins = e.scalar_tensor_tensor(xk[:, c * 512:(c + 1) * 512],
                                                 h1, muk[:, c:c + 1], tmp_bf[:],
                                                 op0=ALU.mult, op1=ALU.add)
                    if c < 7:
                        C.fence(e, ins, "DV")
                C.inc(ins, "DV", 1)
            C.v(v)
            mix_tok[grp] = C.tok("DV")

            ka_pe, ka_sc = C.cnt["PE"], C.cnt["SC"]

            def s(e, ka_pe=ka_pe):
                for oc in range(8):
                    pb = oc % 2
                    C.wait(e, "PE", ka_pe + oc + 1)
                    C.inc(e.activation(krel[:, pb * 512:(pb + 1) * 512],
                                       p_k[:, pb * 512:(pb + 1) * 512],
                                       AF.Relu), "SC", 1)
            C.s(s)
            for oc in range(8):
                pb = oc % 2

                def t(e, oc=oc, grp=grp, pb=pb):
                    if oc == 0:
                        C.wtok(e, "DV", mix_tok[grp])
                        C.wait(e, "LD2", t_wl)
                    prev = kact_tok.get((grp, oc - 2)) or (
                        kact_tok.get((grp - 1, 6 + oc))
                        if oc < 2 and grp > 0 else None)
                    if prev:
                        C.wtok(e, "DV", prev)
                    for dc in range(8):
                        ins = e.matmul(
                            p_k[:, pb * 512:(pb + 1) * 512],
                            wkT[:, dc * D + oc * 128: dc * D + (oc + 1) * 128],
                            xk[:, dc * 512:(dc + 1) * 512],
                            start=(dc == 0), stop=(dc == 7))
                    C.inc(ins, "PE", 1)
                C.t(t)
                assert C.cnt["PE"] == ka_pe + oc + 1

                def v(e, oc=oc, pb=pb, sc_t=ka_sc + oc + 1):
                    C.wait(e, "SC", sc_t)
                    C.inc(e.tensor_tensor(kact[:, oc * 512:(oc + 1) * 512],
                                          krel[:, pb * 512:(pb + 1) * 512],
                                          krel[:, pb * 512:(pb + 1) * 512],
                                          op=ALU.mult), "DV", 1)
                C.v(v)
                kact_tok[(grp, oc)] = C.tok("DV")

            for ti in range(4):
                i = grp * 4 + ti

                def t(e, ti=ti, grp=grp, i=i):
                    if ti == 0:
                        C.wtok(e, "DV", mix_tok[grp])
                    if i > 0:
                        C.wtok(e, "SC", rr_tok[i - 1])
                    for dc in range(8):
                        for half in range(2):
                            ins = e.matmul(
                                p_r[:, half * 512:(half + 1) * 512],
                                xr[:, dc * 512 + ti * 128:
                                   dc * 512 + (ti + 1) * 128],
                                wrT[:, dc * D + half * 512:
                                    dc * D + (half + 1) * 512],
                                start=(dc == 0), stop=(dc == 7))
                    C.inc(ins, "PE", 1)
                C.t(t)
                pe_r = C.tok("PE")

                def s(e, pe_r=pe_r, i=i):
                    C.wtok(e, "PE", pe_r)
                    if i > 0:
                        C.wtok(e, "DV", o_tok[i - 1])
                    C.inc(e.activation(rtm[:], p_r[:], AF.Sigmoid), "SC", 1)
                C.s(s)
                rr_tok[i] = C.tok("SC")

                def t2(e, ti=ti, grp=grp, i=i):
                    C.wtok(e, "DV", kact_tok[(grp, 7)])
                    if i > 0:
                        C.wtok(e, "DV", o_tok[i - 1])
                    for fc in range(8):
                        for half in range(2):
                            ins = e.matmul(
                                p_kv[:, half * 512:(half + 1) * 512],
                                kact[:, fc * 512 + ti * 128:
                                     fc * 512 + (ti + 1) * 128],
                                wvT[:, fc * D + half * 512:
                                    fc * D + (half + 1) * 512],
                                start=(fc == 0), stop=(fc == 7))
                    C.inc(ins, "PE", 1)
                C.t(t2)
                kvp_tok[i] = C.tok("PE")

                def v(e, i=i):
                    C.wtok(e, "PE", kvp_tok[i])
                    C.wtok(e, "SC", rr_tok[i])
                    C.wait(e, "DV")
                    C.inc(e.tensor_tensor(k["scr_a"][:], p_kv[:], rtm[:],
                                          op=ALU.mult), "DV", 1)
                C.v(v)
                o_tok[i] = C.tok("DV")

                emit_ln_post(C, k["scr_a"][:], xs[:, i * D:(i + 1) * D])


def emit_l1a(C):
    nc = C.nc
    k = C.k
    xs = C.xs
    qscr = C.scr["qscr"]
    with ExitStack() as ph:
        ec = ph.enter_context
        kp = ec(nc.sbuf_tensor(C.u("l1_k"), [128, DC * S], BF16))
        vst = ec(nc.sbuf_tensor(C.u("l1_v"), [128, NTILE * H * 65], BF16))
        h2f = ec(nc.sbuf_tensor(C.u("l1_h2f"), [128, DC * 512], BF16))
        lnp = ec(nc.sbuf_tensor(C.u("l1_lnp"), [128, 6 * NTILE], F32))
        htm2 = ec(nc.sbuf_tensor(C.u("l1_htm2"), [128, 2 * D], BF16))
        sqs = ec(nc.sbuf_tensor(C.u("l1_sq"), [128, 512], BF16))
        nrm = ec(nc.sbuf_tensor(C.u("l1_nrm"), [1, 80], F32))
        negm = ec(nc.sbuf_tensor(C.u("l1_negm"), [128, 1], F32))
        mrow = ec(nc.sbuf_tensor(C.u("l1_mrow"), [1, 1], BF16))
        ap6 = None

        emit_pre_stats(C, lnp)
        ap6 = lnp[:].rearrange("p (i c) -> p i c", c=6)
        t_stats = C.tok("DV")

        with ExitStack() as pa:
            eca = pa.enter_context
            p_tr = eca(nc.psum_tensor(C.u("l1_p_tr"), [128, 2 * D], BF16))
            p_mm = eca(nc.psum_tensor(C.u("l1_p_mm"), [128, 4 * 512], F32))
            p_v = eca(nc.psum_tensor(C.u("l1_p_v"), [128, D], F32))
            wq = eca(nc.sbuf_tensor(C.u("l1_wq_s"), [128, DC * D], BF16))
            wk = eca(nc.sbuf_tensor(C.u("l1_wk_s"), [128, DC * D], BF16))
            wv = eca(nc.sbuf_tensor(C.u("l1_wv_s"), [128, DC * D], BF16))
            qgb = eca(nc.sbuf_tensor(C.u("l1_qgb"), [128, DC * 512], BF16))

            def g(e):
                C.wait(e, "LD2")
                for nm, dst in [("l1_wq", wq), ("l1_wk", wk), ("l1_wv", wv)]:
                    C.inc(e.dma_start(
                        out=dst[:].rearrange("p (c d) -> p c d", d=D),
                        in_=C.dram[nm].rearrange("(c p) d -> p c d", p=128)),
                        "LD", 16)
                C.inc(e.memset(vst[:], 1.0), "GP", 1)
            C.g(g)
            t_wload = C.tok("LD")
            t_vmem = C.tok("GP")

            tr_tok = [0] * NTILE
            qk_tok = {}
            vev_tok = [0] * NTILE
            qship_tok = [0] * NGRP
            nq_tok = {}
            for grp in range(NGRP):
                hv_sc, hv_dv = C.cnt["SC"], C.cnt["DV"]

                def s(e, grp=grp, hv_dv=hv_dv,
                      pv=tuple(tr_tok[max(0, grp * 4 - 2): grp * 4])):
                    C.wait(e, "DV", t_stats)
                    for ti in range(4):
                        i = grp * 4 + ti
                        if ti >= 2:
                            C.wait(e, "DV", hv_dv + ti - 1)
                        elif i >= 2:
                            C.wait(e, "DV", pv[ti])
                        C.inc(e.activation(
                            htm2[:, (i % 2) * D:(i % 2 + 1) * D],
                            xs[:, i * D:(i + 1) * D],
                            AF.Identity, scale=ap6[:, i, 4:5],
                            bias=ap6[:, i, 5:6]), "SC", 1)
                C.s(s)
                for ti in range(4):
                    i = grp * 4 + ti
                    ap_t = hv_sc + ti + 1

                    def t(e, i=i, ti=ti, ap_t=ap_t):
                        C.wtok(e, "SC", ap_t)
                        if i >= 2:
                            C.wtok(e, "DV", tr_tok[i - 2])
                        for c in range(8):
                            ins = e.transpose(
                                p_tr[:, (i % 2) * D + c * 128:
                                     (i % 2) * D + (c + 1) * 128],
                                htm2[:, (i % 2) * D + c * 128:
                                     (i % 2) * D + (c + 1) * 128],
                                k["ident"][:])
                        C.inc(ins, "PE", 1)
                    C.t(t)
                    pe_t = C.tok("PE")

                    def v(e, i=i, ti=ti, pe_t=pe_t):
                        C.wtok(e, "PE", pe_t)
                        for c in range(8):
                            ins = e.tensor_copy(
                                h2f[:, c * 512 + ti * 128:
                                    c * 512 + (ti + 1) * 128],
                                p_tr[:, (i % 2) * D + c * 128:
                                     (i % 2) * D + (c + 1) * 128])
                        C.inc(ins, "DV", 1)
                    C.v(v)
                    tr_tok[i] = C.tok("DV")
                    assert tr_tok[i] == hv_dv + ti + 1

                qp_pe, qp_sc = C.cnt["PE"], C.cnt["SC"]

                def s(e, qp_pe=qp_pe):
                    for oc in range(8):
                        qtr = oc % 2
                        C.wait(e, "PE", qp_pe + 2 * oc + 1)
                        C.inc(e.activation(
                            qg_dst(qgb, oc), p_mm[:, qtr * 512:
                                                  (qtr + 1) * 512],
                            AF.Identity), "SC", 1)
                C.s(s)
                for oc in range(8):
                    for wsb, isq in ((wq, True), (wk, False)):
                        qtr = (oc % 2) + (0 if isq else 2)

                        def t(e, oc=oc, grp=grp, qtr=qtr, wsb=wsb, isq=isq):
                            if oc == 0 and isq:
                                C.wtok(e, "DV", tr_tok[grp * 4 + 3])
                                C.wait(e, "LD", t_wload)
                                if grp > 0:
                                    C.wtok(e, "ST", qship_tok[grp - 1])
                                    C.wtok(e, "DV", nq_tok[(grp - 1, 7)][1])
                            prev = qk_tok.get((grp, oc - 2, isq)) or (
                                qk_tok.get((grp - 1, 6 + oc, isq))
                                if oc < 2 and grp > 0 else None)
                            if prev:
                                nm, v_ = prev
                                C.wtok(e, nm, v_)
                            for dc in range(8):
                                ins = e.matmul(
                                    p_mm[:, qtr * 512:(qtr + 1) * 512],
                                    wsb[:, dc * D + oc * 128:
                                        dc * D + (oc + 1) * 128],
                                    h2f[:, dc * 512:(dc + 1) * 512],
                                    start=(dc == 0), stop=(dc == 7))
                            C.inc(ins, "PE", 1)
                        C.t(t)
                        pe_t = C.tok("PE")

                        if isq:
                            assert pe_t == qp_pe + 2 * oc + 1
                            qk_tok[(grp, oc, True)] = ("SC", qp_sc + oc + 1)
                        else:
                            def v(e, oc=oc, grp=grp, qtr=qtr, pe_t=pe_t):
                                C.wtok(e, "PE", pe_t)
                                C.inc(e.tensor_scalar(
                                    kp[:, oc * S + grp * 512:
                                       oc * S + (grp + 1) * 512],
                                    p_mm[:, qtr * 512:(qtr + 1) * 512],
                                    0.125, None, op0=ALU.mult), "DV", 1)
                            C.v(v)
                            qk_tok[(grp, oc, False)] = ("DV", C.tok("DV"))

                vp_pe, vp_sc = C.cnt["PE"], C.cnt["SC"]

                def s(e, vp_pe=vp_pe, grp=grp):
                    C.wait(e, "GP", t_vmem)
                    for ti in range(4):
                        i = grp * 4 + ti
                        C.wait(e, "PE", vp_pe + ti + 1)
                        dst = vst[:, i * H * 65:(i + 1) * H * 65].rearrange(
                            "p (h e) -> p h e", e=65)[:, :, 0:64]
                        srcp = p_v[:].rearrange("p (h e) -> p h e", e=64)
                        C.inc(e.activation(dst, srcp, AF.Identity), "SC", 1)
                C.s(s)
                for ti in range(4):
                    i = grp * 4 + ti

                    def t(e, i=i, ti=ti, grp=grp):
                        if ti == 0:
                            nm, v_ = qk_tok[(grp, 7, False)]
                            C.wtok(e, nm, v_)
                            nm, v_ = qk_tok[(grp, 7, True)]
                            C.wtok(e, nm, v_)
                        if i >= 1:
                            C.wtok(e, "SC", vev_tok[i - 1])
                        for dc in range(8):
                            for hh in range(2):
                                ins = e.matmul(
                                    p_v[:, hh * 512:(hh + 1) * 512],
                                    h2f[:, dc * 512 + ti * 128:
                                        dc * 512 + (ti + 1) * 128],
                                    wv[:, dc * D + hh * 512:
                                        dc * D + (hh + 1) * 512],
                                    start=(dc == 0), stop=(dc == 7))
                        C.inc(ins, "PE", 1)
                    C.t(t)
                    assert C.cnt["PE"] == vp_pe + ti + 1
                    vev_tok[i] = vp_sc + ti + 1

                qn_pe, qn_sc, qn_dv = C.cnt["PE"], C.cnt["SC"], C.cnt["DV"]

                def s(e, qn_pe=qn_pe):
                    for pc in range(8):
                        C.wait(e, "PE", qn_pe + pc)
                        C.inc(e.activation(
                            sqs[:],
                            qgb[:, pc * 512:(pc + 1) * 512],
                            AF.Square), "SC", 1)
                C.s(s)
                assert C.cnt["SC"] == qn_sc + 8

                for pc in range(8):
                    def t(e, pc=pc, qn_sc=qn_sc, qn_dv=qn_dv):
                        C.wait(e, "SC", qn_sc + pc + 1)
                        C.wait(e, "DV", qn_dv + pc)
                        C.inc(e.matmul(p_mm[0:1, 0:512], k["ones_col"][:],
                                       sqs[:],
                                       start=True, stop=True), "PE", 1)
                    C.t(t)

                    def v(e, pc=pc, grp=grp, qn_pe=qn_pe):
                        C.wait(e, "PE", qn_pe + pc + 1)
                        C.inc(e.tensor_reduce(
                            nrm[:, pc * 4 + grp: pc * 4 + grp + 1],
                            p_mm[0:1, 0:512], axis=AX.X, op=ALU.max), "DV", 1)
                    C.v(v)
                    nq_tok[(grp, pc)] = (qn_pe + pc + 1, C.tok("DV"))
                assert C.cnt["PE"] == qn_pe + 8

                def y(e, grp=grp):
                    C.wtok(e, "SC", qk_tok[(grp, 7, True)][1])
                    C.inc(e.dma_start(out=qscr[grp], in_=qgb[:]), "ST", 16)
                C.y(y)
                qship_tok[grp] = C.tok("ST")

            kn_pe, kn_sc, kn_dv = C.cnt["PE"], C.cnt["SC"], C.cnt["DV"]

            def s(e, kn_pe=kn_pe):
                for j in range(32):
                    C.wait(e, "PE", kn_pe + j)
                    pc, g4 = divmod(j, 4)
                    C.inc(e.activation(
                        sqs[:],
                        kp[:, pc * S + g4 * 512: pc * S + (g4 + 1) * 512],
                        AF.Square), "SC", 1)
            C.s(s)
            assert C.cnt["SC"] == kn_sc + 32

            for j in range(32):
                pc, g4 = divmod(j, 4)

                def t(e, j=j, kn_sc=kn_sc, kn_dv=kn_dv):
                    C.wait(e, "SC", kn_sc + j + 1)
                    C.wait(e, "DV", kn_dv + j)
                    C.inc(e.matmul(p_mm[0:1, 0:512], k["ones_col"][:],
                                   sqs[:],
                                   start=True, stop=True), "PE", 1)
                C.t(t)

                def v(e, j=j, pc=pc, g4=g4, kn_pe=kn_pe):
                    C.wait(e, "PE", kn_pe + j + 1)
                    C.inc(e.tensor_reduce(
                        nrm[:, 40 + pc * 4 + g4: 40 + pc * 4 + g4 + 1],
                        p_mm[0:1, 0:512], axis=AX.X, op=ALU.max), "DV", 1)
                C.v(v)
            assert C.cnt["PE"] == kn_pe + 32

            def v(e):
                C.wait(e, "DV")
                e.tensor_reduce(nrm[:, 32:33], nrm[:, 0:32], axis=AX.X,
                                op=ALU.max)
                C.fence(e, e.tensor_reduce(nrm[:, 72:73], nrm[:, 40:72],
                                           axis=AX.X, op=ALU.max), "DV")
                C.inc(e.tensor_tensor(nrm[:, 33:34], nrm[:, 32:33],
                                      nrm[:, 72:73], op=ALU.mult), "DV", 1)
            C.v(v)

            def s(e):
                C.wait(e, "DV")
                C.fence(e, e.activation(nrm[:, 34:35], nrm[:, 33:34],
                                        AF.Ln), "SC")
                C.inc(e.activation(mrow[:], nrm[:, 34:35], AF.Exp, scale=0.5),
                      "SC", 1)
            C.s(s)
            sq_t = C.tok("SC")

            def t(e, sq_t=sq_t):
                C.wtok(e, "SC", sq_t)
                C.inc(e.matmul(p_mm[:, 0:1], k["ones_row"][:], mrow[:],
                               start=True, stop=True), "PE", 1)
            C.t(t)
            mm_t = C.tok("PE")

            def v(e, mm_t=mm_t):
                C.wtok(e, "PE", mm_t)
                C.inc(e.tensor_scalar(negm[:], p_mm[:, 0:1], -1.0, None,
                                      op0=ALU.mult), "DV", 1)
            C.v(v)
        t_negm = C.tok("DV")

        with ExitStack() as pb:
            ecb = pb.enter_context
            wp = ecb(nc.sbuf_tensor(C.u("l1_wp_s"), [128, DC * D], BF16))
            qgrp = ecb(nc.sbuf_tensor(C.u("l1_qgrp"), [128, DC * 512], BF16))
            pt = ecb(nc.sbuf_tensor(C.u("l1_pt"), [128, 2 * 1024], BF16))
            au = ecb(nc.sbuf_tensor(C.u("l1_au"), [65, 2 * 512], BF16))
            an = ecb(nc.sbuf_tensor(C.u("l1_an"), [128, 4 * D], F32))
            ag = ecb(nc.sbuf_tensor(C.u("l1_ag"), [128, 2 * D], BF16))
            af = ecb(nc.sbuf_tensor(C.u("l1_af"), [128, 2 * D], BF16))
            p_sp = ecb(nc.psum_tensor(C.u("l1_p_s"), [128, 2 * 1024], F32))
            p_o = ecb(nc.psum_tensor(C.u("l1_p_o"), [65, 2 * 512], F32))
            p_at = ecb(nc.psum_tensor(C.u("l1_p_at"), [128, 2 * 1024], BF16))
            p_w = p_at[:].bitcast(F32)[:, 0:512]

            def g(e):
                C.wait(e, "PE")
                C.wait(e, "SC")
                C.wait(e, "DV")
                C.wait(e, "ST")
                C.inc(e.dma_start(
                    out=wp[:].rearrange("p (c d) -> p c d", d=D),
                    in_=C.dram["l1_wp"].rearrange("(c p) d -> p c d", p=128)),
                    "LD", 16)
            C.g(g)
            t_wp = C.tok("LD")

            qld_tok = [0] * NGRP
            exp_tok = {}
            pv_tok = {}
            auev_tok = {}
            anrm_tok = {}
            agl_tok = [0] * 4
            aft_tok = [0] * 4
            wpo_tok = [0] * NTILE
            wd0_tok = {}
            wd1_tok = {}
            for grp in range(NGRP):
                if grp == 0:
                    def y(e):
                        C.wait(e, "PE")
                        C.wait(e, "SC")
                        C.wait(e, "DV")
                        C.wait(e, "ST")
                        C.inc(e.dma_start(out=qgrp[:], in_=qscr[0]),
                              "LD2", 16)
                    C.y(y)
                    qld_tok[0] = C.tok("LD2")

                eg_sc0 = C.cnt["SC"]
                st_pe_all = []
                for pc in range(8):
                    sc_base = eg_sc0 + NTILE * pc
                    for tc in range(NTILE):
                        exp_tok[(grp, pc, tc)] = sc_base + tc + 1
                    st_pe_l = [0] * NTILE

                    def mk_st(tc, pc=pc, grp=grp):
                        buf = tc % 2

                        def t(e, grp=grp, pc=pc, tc=tc, buf=buf):
                            if pc == 0 and tc == 0:
                                C.wait(e, "DV", t_negm)
                                if grp == 0:
                                    C.wait(e, "LD2", qld_tok[0])
                                else:
                                    C.wait(e, "WD0", wd0_tok[grp])
                            if pc == 4 and tc == 0 and grp > 0:
                                C.wait(e, "WD1", wd1_tok[grp])
                            prev = exp_tok.get((grp, pc, tc - 2)) or (
                                exp_tok.get((grp, pc - 1, NTILE - 2))
                                if tc < 2 else None)
                            if prev:
                                C.wtok(e, "SC", prev)
                            e.matmul(
                                p_sp[:, buf * 1024: buf * 1024 + 512],
                                kp[0:64, pc * S + tc * 128:
                                   pc * S + (tc + 1) * 128],
                                qgrp[0:64, pc * 512:(pc + 1) * 512],
                                start=True, stop=True, tile_position=(0, 0))
                            C.inc(e.matmul(
                                p_sp[:, buf * 1024 + 512: buf * 1024 + 1024],
                                kp[64:128, pc * S + tc * 128:
                                   pc * S + (tc + 1) * 128],
                                qgrp[64:128, pc * 512:(pc + 1) * 512],
                                start=True, stop=True,
                                tile_position=(64, 0)), "PE", 1)
                        C.t(t)
                        st_pe_l[tc] = C.tok("PE")

                    def mk_pv(tc, pc=pc, grp=grp):
                        buf = tc % 2

                        def t2(e, pc=pc, tc=tc, buf=buf, grp=grp):
                            C.wtok(e, "SC", exp_tok[(grp, pc, tc)])
                            if tc == 0:
                                prev = auev_tok.get((grp, pc - 1)) or (
                                    auev_tok.get((grp - 1, 7)) if grp > 0
                                    else None)
                                if prev:
                                    C.wtok(e, prev[0], prev[1])
                            e.matmul(p_o[0:65, 0:512],
                                     vst[:, tc * H * 65 + (2 * pc) * 65:
                                         tc * H * 65 + (2 * pc) * 65 + 65],
                                     pt[:, buf * 1024: buf * 1024 + 512],
                                     start=(tc == 0), stop=(tc == NTILE - 1),
                                     skip_group_check=True)
                            C.inc(e.matmul(
                                p_o[0:65, 512:1024],
                                vst[:, tc * H * 65 + (2 * pc + 1) * 65:
                                    tc * H * 65 + (2 * pc + 1) * 65 + 65],
                                pt[:, buf * 1024 + 512: buf * 1024 + 1024],
                                start=(tc == 0), stop=(tc == NTILE - 1),
                                skip_group_check=True), "PE", 1)
                        C.t(t2)

                    mk_st(0)
                    for tc in range(1, NTILE):
                        mk_st(tc)
                        mk_pv(tc - 1)
                    mk_pv(NTILE - 1)
                    pv_tok[(grp, pc)] = C.tok("PE")
                    st_pe_all.append(tuple(st_pe_l))

                    if pc == 3 and grp < NGRP - 1:
                        def y1(e, grp=grp, tok=st_pe_all[3][NTILE - 1]):
                            C.wait(e, "PE", tok)
                            C.inc(e.dma_start(
                                out=qgrp[:, 0:4 * 512],
                                in_=qscr[grp + 1][:, 0:4 * 512]), "WD0", 16)
                        C.y(y1)
                        wd0_tok[grp + 1] = C.tok("WD0")

                    def vv(e, pc=pc, grp=grp):
                        C.wtok(e, "PE", pv_tok[(grp, pc)])
                        prev = anrm_tok.get((grp, pc - 1)) or (
                            anrm_tok.get((grp - 1, 7)) if grp > 0 else None)
                        if prev:
                            C.wtok(e, "DV", prev)
                        e.tensor_copy(au[0:65, 0:512], p_o[0:65, 0:512])
                        C.inc(e.tensor_copy(au[0:65, 512:1024],
                                            p_o[0:65, 512:1024]), "DV", 1)
                    C.v(vv)
                    auev_tok[(grp, pc)] = ("DV", C.tok("DV"))

                    for hh_i in range(2):
                        h = 2 * pc + hh_i
                        for sub in range(4):
                            ab = (hh_i * 4 + sub) % 2

                            def t(e, hh_i=hh_i, sub=sub, ab=ab, grp=grp,
                                  pc=pc):
                                nm_, v_ = auev_tok[(grp, pc)]
                                C.wtok(e, nm_, v_)
                                prev = anrm_tok.get((grp, pc, hh_i, sub - 2)) \
                                    or (anrm_tok.get((grp, pc, hh_i - 1,
                                                      sub + 2))
                                        if sub < 2 else None)
                                if prev:
                                    C.wtok(e, "DV", prev)
                                else:
                                    C.wait(e, "DV")
                                C.inc(e.transpose(
                                    p_at[:, ab * 1024: ab * 1024 + 65],
                                    au[0:65, hh_i * 512 + sub * 128:
                                       hh_i * 512 + (sub + 1) * 128],
                                    k["ident65"][:]), "PE", 1)
                            C.t(t)
                            at_t = C.tok("PE")

                            def v(e, sub=sub, h=h, ab=ab, at_t=at_t,
                                  hh_i=hh_i, grp=grp, pc=pc):
                                pr = anrm_tok.get((grp, pc, hh_i, sub - 1)) \
                                    or (anrm_tok.get((grp, pc, hh_i - 1, 3))
                                        if sub == 0 else None)
                                if pr:
                                    C.wtok(e, "DV", pr)
                                C.wtok(e, "PE", at_t)
                                C.fence(e, e.reciprocal(
                                    k["st8"][:, 7:8],
                                    p_at[:, ab * 1024 + 64: ab * 1024 + 65]),
                                    "DV")
                                C.inc(e.tensor_scalar(
                                    an[:, sub * D + h * E:
                                       sub * D + (h + 1) * E],
                                    p_at[:, ab * 1024: ab * 1024 + 64],
                                    k["st8"][:, 7:8], None, op0=ALU.mult),
                                    "DV", 1)
                            C.v(v)
                            anrm_tok[(grp, pc, hh_i, sub)] = C.tok("DV")
                        anrm_tok[(grp, pc)] = C.tok("DV")

                if grp < NGRP - 1:
                    def y2(e, grp=grp, tok=st_pe_all[7][NTILE - 1]):
                        C.wait(e, "PE", tok)
                        C.inc(e.dma_start(
                            out=qgrp[:, 4 * 512: 8 * 512],
                            in_=qscr[grp + 1][:, 4 * 512: 8 * 512]),
                            "WD1", 16)
                    C.y(y2)
                    wd1_tok[grp + 1] = C.tok("WD1")

                assert C.cnt["SC"] == eg_sc0
                def s(e, st_pe_all=tuple(st_pe_all)):
                    for pc in range(8):
                        for tc in range(NTILE):
                            buf = tc % 2
                            C.wait(e, "PE", st_pe_all[pc][tc])
                            C.inc(e.activation(pt[:, buf * 1024:
                                                  (buf + 1) * 1024],
                                               p_sp[:, buf * 1024:
                                                    (buf + 1) * 1024],
                                               AF.Exp, bias=negm[:, 0:1]),
                                  "SC", 1)
                C.s(s)
                assert C.cnt["SC"] == eg_sc0 + 8 * NTILE

                for sub in range(4):
                    i = grp * 4 + sub
                    sb = sub % 2

                    def s(e, sub=sub, sb=sb, grp=grp, i=i,
                          dv_now=C.cnt["DV"]):
                        C.wtok(e, "DV", anrm_tok[(grp, 7)])
                        if sub >= 2:
                            C.wtok(e, "DV", aft_tok[sub - 2])
                        C.inc(e.activation(ag[:, sb * D:(sb + 1) * D],
                                           an[:, sub * D:(sub + 1) * D],
                                           AF.Gelu), "SC", 1)
                        C.wait(e, "DV", dv_now)
                        C.inc(e.activation(k["scr_b"][:],
                                           xs[:, i * D:(i + 1) * D],
                                           AF.Identity,
                                           scale=ap6[:, i, 4:5],
                                           bias=ap6[:, i, 5:6]), "SC", 1)
                    C.s(s)
                    agl_tok[sub] = C.tok("SC") - 1
                    h2r_t = C.tok("SC")

                    def t(e, sub=sub, sb=sb):
                        C.wtok(e, "SC", agl_tok[sub])
                        if sub >= 2:
                            C.wtok(e, "DV", aft_tok[sub - 2])
                        for c in range(8):
                            ins = e.transpose(
                                p_at_tr(p_sp, sb, c),
                                ag[:, sb * D + c * 128: sb * D + (c + 1) * 128],
                                k["ident"][:])
                        C.inc(ins, "PE", 1)
                    C.t(t)
                    trf_t = C.tok("PE")

                    def v(e, sub=sub, sb=sb, trf_t=trf_t):
                        C.wtok(e, "PE", trf_t)
                        for c in range(8):
                            ins = e.tensor_copy(
                                af[:, sb * D + c * 128: sb * D + (c + 1) * 128],
                                p_at_tr(p_sp, sb, c))
                        C.inc(ins, "DV", 1)
                    C.v(v)
                    aft_tok[sub] = C.tok("DV")


                    for hh in range(2):
                        def t2(e, sub=sub, sb=sb, i=i, hh=hh):
                            C.wtok(e, "DV", aft_tok[sub])
                            C.wait(e, "LD", t_wp)
                            if i > 0 or hh > 0:
                                C.wait(e, "DV")
                            for fc in range(8):
                                ins = e.matmul(
                                    p_w,
                                    af[:, sb * D + fc * 128:
                                       sb * D + (fc + 1) * 128],
                                    wp[:, fc * D + hh * 512:
                                       fc * D + (hh + 1) * 512],
                                    start=(fc == 0), stop=(fc == 7))
                            C.inc(ins, "PE", 1)
                        C.t(t2)
                        wp_pe = C.tok("PE")

                        def v2(e, hh=hh, h2r_t=h2r_t, wp_pe=wp_pe):
                            C.wtok(e, "PE", wp_pe)
                            C.wtok(e, "SC", h2r_t)
                            C.inc(e.tensor_tensor(
                                k["scr_a"][:, hh * 512:(hh + 1) * 512], p_w,
                                k["scr_b"][:, hh * 512:(hh + 1) * 512],
                                op=ALU.add), "DV", 1)
                        C.v(v2)

                    emit_ln_post(C, k["scr_a"][:], xs[:, i * D:(i + 1) * D])
                    wpo_tok[i] = C.tok("SC")


def qg_dst(qgb, oc):
    return qgb[:, oc * 512:(oc + 1) * 512]


def p_at_tr(p_sp, sb, c):
    return p_sp[:].bitcast(BF16)[:, sb * 1024 + c * 128:
                                 sb * 1024 + (c + 1) * 128]


def emit_pre_stats(C, lnp):
    k = C.k
    xs = C.xs
    ap6 = lnp[:].rearrange("p (i c) -> p i c", c=6)

    def v(e):
        C.wait(e, "LD2")
        C.wait(e, "DV")
        for i in range(NTILE):
            ins = e.tensor_reduce(lnp[:, 6 * i: 6 * i + 1],
                                  xs[:, i * D:(i + 1) * D], axis=AX.X,
                                  op=ALU.add)
        C.inc(ins, "DV", 1)
    C.v(v)
    tsum = C.tok("DV")

    def s(e, tsum=tsum):
        C.wait(e, "LD2")
        C.wait(e, "SC")
        C.wait(e, "DV", tsum)
        for i in range(NTILE):
            ins = e.activation(k["scr_a" if i % 2 else "scr_b"][:],
                               xs[:, i * D:(i + 1) * D],
                               AF.Square, accum_out=lnp[:, 6 * i + 1:
                                                        6 * i + 2])
            if i < NTILE - 1:
                C.fence(e, ins, "SC")
        C.inc(ins, "SC", 1)
    C.s(s)
    tsq = C.tok("SC")

    def v2(e, tsq=tsq):
        C.wait(e, "SC", tsq)
        C.fence(e, e.tensor_scalar(ap6[:, :, 2:3], ap6[:, :, 0:1], 1.0 / D,
                                   None, op0=ALU.mult), "DV")
        C.fence(e, e.tensor_scalar(ap6[:, :, 3:4], ap6[:, :, 1:2], 1.0 / D,
                                   None, op0=ALU.mult), "DV")
        C.fence(e, e.scalar_tensor_tensor(ap6[:, :, 4:5], ap6[:, :, 2:3],
                                          -1.0, ap6[:, :, 2:3], op0=ALU.mult,
                                          op1=ALU.mult), "DV")
        C.inc(e.tensor_tensor(ap6[:, :, 3:4], ap6[:, :, 3:4], ap6[:, :, 4:5],
                              op=ALU.add), "DV", 1)
    C.v(v2)
    t3 = C.tok("DV")

    def s2(e, t3=t3):
        C.wait(e, "DV", t3)
        C.fence(e, e.activation(ap6[:, :, 4:5], ap6[:, :, 3:4], AF.Ln,
                                bias=k["eps_v"][:, 0:1]), "SC")
        C.inc(e.activation(ap6[:, :, 4:5], ap6[:, :, 4:5], AF.Exp,
                           scale=-0.5), "SC", 1)
    C.s(s2)
    t4 = C.tok("SC")

    def v3(e, t4=t4):
        C.wait(e, "SC", t4)
        C.inc(e.scalar_tensor_tensor(ap6[:, :, 5:6], ap6[:, :, 2:3], -1.0,
                                     ap6[:, :, 4:5], op0=ALU.mult,
                                     op1=ALU.mult), "DV", 1)
    C.v(v3)


def emit_l1f(C, SCR):
    nc = C.nc
    k = C.k
    xs = C.xs
    if getattr(C, "f1_prep_pending", False):
        C.f1_prep_pending = False
        emit_weight_prep(C, SCR, [("f1_wg", HID, D), ("f1_wu", HID, D),
                                  ("f1_wd", D, HID)])
    HH2 = HID // 2
    with ExitStack() as ph:
        ec = ph.enter_context
        wgh = ec(nc.sbuf_tensor(C.u("f1_wgh"), [128, DC * HH2], BF16))
        wuh = ec(nc.sbuf_tensor(C.u("f1_wuh"), [128, DC * HH2], BF16))
        wdb = ec(nc.sbuf_tensor(C.u("f1_wdb"), [128, 2 * 4 * 512], BF16))
        h3f = ec(nc.sbuf_tensor(C.u("f1_h3f"), [128, DC * 512], BF16))
        hh = ec(nc.sbuf_tensor(C.u("f1_hh"), [128, HC * 512], BF16))
        sg = ec(nc.sbuf_tensor(C.u("f1_sg"), [128, 2 * 512], BF16))
        od = ec(nc.sbuf_tensor(C.u("f1_od"), [128, 4 * D], F32))
        lnp = ec(nc.sbuf_tensor(C.u("f1_lnp"), [128, 6 * NTILE], F32))
        htm2 = ec(nc.sbuf_tensor(C.u("f1_htm2"), [128, 2 * D], BF16))
        p_big = ec(nc.psum_tensor(C.u("f1_p_big"), [128, 4 * 512], F32))
        p_g = ec(nc.psum_tensor(C.u("f1_p_g"), [128, 2 * 512], F32))
        p_u = ec(nc.psum_tensor(C.u("f1_p_u"), [128, 2 * 512], F32))
        p_trv = p_big[:].bitcast(BF16)

        emit_pre_stats(C, lnp)
        ap6 = lnp[:].rearrange("p (i c) -> p i c", c=6)
        t_stats = C.tok("DV")

        tr_tok = [0] * NTILE
        dn_tok = {}
        odv_tok = {}
        lnq_tok = {}
        hhev_tok = {}
        wl_tok = {}
        gu_tok = {}
        for grp in range(NGRP):
            for ti in range(4):
                i = grp * 4 + ti

                def s(e, i=i):
                    C.wait(e, "DV", t_stats)
                    if i >= 2:
                        C.wtok(e, "DV", tr_tok[i - 2])
                    C.inc(e.activation(htm2[:, (i % 2) * D:(i % 2 + 1) * D],
                                       xs[:, i * D:(i + 1) * D], AF.Identity,
                                       scale=ap6[:, i, 4:5],
                                       bias=ap6[:, i, 5:6]), "SC", 1)
                C.s(s)
                ap_t = C.tok("SC")

                def t(e, i=i, ti=ti, ap_t=ap_t, grp=grp):
                    C.wtok(e, "SC", ap_t)
                    if i >= 2:
                        C.wtok(e, "DV", tr_tok[i - 2])
                    if grp > 0 and ti < 2:
                        C.wtok(e, "SC", odv_tok[(grp - 1, 3)])
                    for c in range(8):
                        ins = e.transpose(
                            p_trv[:, (i % 2) * 1024 + c * 128:
                                  (i % 2) * 1024 + (c + 1) * 128],
                            htm2[:, (i % 2) * D + c * 128:
                                 (i % 2) * D + (c + 1) * 128],
                            k["ident"][:])
                    C.inc(ins, "PE", 1)
                C.t(t)
                pe_t = C.tok("PE")

                def v(e, i=i, ti=ti, pe_t=pe_t):
                    C.wtok(e, "PE", pe_t)
                    for c in range(8):
                        ins = e.tensor_copy(
                            h3f[:, c * 512 + ti * 128: c * 512 + (ti + 1) * 128],
                            p_trv[:, (i % 2) * 1024 + c * 128:
                                  (i % 2) * 1024 + (c + 1) * 128])
                    C.inc(ins, "DV", 1)
                C.v(v)
                tr_tok[i] = C.tok("DV")

            for half in range(2):
                def y(e, grp=grp, half=half):
                    prev = gu_tok.get((grp, half * HHALF_G - 1)) if half else \
                        (gu_tok.get((grp - 1, HC - 1)) if grp > 0 else None)
                    if prev:
                        C.wtok(e, "PE", prev)
                    for dc in range(8):
                        C.inc(e.dma_start(
                            out=wgh[:, dc * HH2:(dc + 1) * HH2],
                            in_=C.scr["f1_wg_t"][dc * 128:(dc + 1) * 128,
                                                 half * HH2:(half + 1) * HH2]),
                            "WL", 16)
                        C.inc(e.dma_start(
                            out=wuh[:, dc * HH2:(dc + 1) * HH2],
                            in_=C.scr["f1_wu_t"][dc * 128:(dc + 1) * 128,
                                                 half * HH2:(half + 1) * HH2]),
                            "WL", 16)
                C.y(y)
                wl_tok[(grp, half)] = C.tok("WL")

                sc_base = C.cnt["SC"]
                sl_toks = {}
                gu_l = [0] * (HC // 2)
                for hc_l in range(HC // 2):
                    hc = half * (HC // 2) + hc_l
                    sl_toks[hc] = sc_base + hc_l + 1
                for hc_l in range(HC // 2):
                    hc = half * (HC // 2) + hc_l
                    pb = hc % 2

                    def t(e, grp=grp, half=half, hc_l=hc_l, hc=hc, pb=pb):
                        if hc_l == 0:
                            C.wait(e, "WL", wl_tok[(grp, half)])
                            C.wtok(e, "DV", tr_tok[grp * 4 + 3])
                        prev = hhev_tok.get((grp, hc - 2)) or (
                            hhev_tok.get((grp - 1, HC - 2 + hc))
                            if grp > 0 and hc < 2 else None)
                        if prev:
                            C.wtok(e, "DV", prev)
                        for dc in range(8):
                            e.matmul(p_g[:, pb * 512:(pb + 1) * 512],
                                     wgh[:, dc * HH2 + hc_l * 128:
                                         dc * HH2 + (hc_l + 1) * 128],
                                     h3f[:, dc * 512:(dc + 1) * 512],
                                     start=(dc == 0), stop=(dc == 7))
                        for dc in range(8):
                            ins = e.matmul(
                                p_u[:, pb * 512:(pb + 1) * 512],
                                wuh[:, dc * HH2 + hc_l * 128:
                                    dc * HH2 + (hc_l + 1) * 128],
                                h3f[:, dc * 512:(dc + 1) * 512],
                                start=(dc == 0), stop=(dc == 7))
                        C.inc(ins, "PE", 1)
                    C.t(t)
                    gu_tok[(grp, hc)] = C.tok("PE")
                    gu_l[hc_l] = C.tok("PE")

                    def v(e, hc=hc, pb=pb, sl_t=sl_toks[hc]):
                        C.wtok(e, "SC", sl_t)
                        C.inc(e.tensor_tensor(hh[:, hc * 512:(hc + 1) * 512],
                                              sg[:, pb * 512:(pb + 1) * 512],
                                              p_u[:, pb * 512:(pb + 1) * 512],
                                              op=ALU.mult), "DV", 1)
                    C.v(v)
                    hhev_tok[(grp, hc)] = C.tok("DV")

                def s(e, gu_l=tuple(gu_l), half=half, sc_base=sc_base):
                    for hc_l in range(HC // 2):
                        hc = half * (HC // 2) + hc_l
                        pb = hc % 2
                        C.wtok(e, "PE", gu_l[hc_l])
                        C.inc(e.activation(sg[:, pb * 512:(pb + 1) * 512],
                                           p_g[:, pb * 512:(pb + 1) * 512],
                                           AF.Silu, scale=1.0 / 127), "SC", 1)
                        assert C.cnt["SC"] == sc_base + hc_l + 1
                C.s(s)

            for dhalf in range(2):
                for hb in range(HC // 4):
                    buf = hb % 2

                    def y(e, grp=grp, dhalf=dhalf, hb=hb, buf=buf):
                        prev = dn_tok.get((grp, dhalf, hb - 2)) or (
                            dn_tok.get((grp, dhalf - 1, HC // 4 - 2 + hb))
                            if hb < 2 and dhalf > 0 else
                            (dn_tok.get((grp - 1, 1, HC // 4 - 2 + hb))
                             if hb < 2 and dhalf == 0 and grp > 0 else None))
                        if prev:
                            C.wtok(e, "PE", prev)
                        C.inc(e.dma_start(
                            out=wdb[:, buf * 2048:(buf + 1) * 2048].rearrange(
                                "p (c d) -> p c d", d=512),
                            in_=C.scr["f1_wd_t"][hb * 512:(hb + 1) * 512,
                                                 dhalf * 512:(dhalf + 1) * 512
                                                 ].rearrange(
                                "(c p) d -> p c d", p=128)),
                            "WD%d" % buf, 16)
                    C.y(y)
                    wd_t = C.tok("WD%d" % buf)

                    for hc_l in range(4):
                        hc = hb * 4 + hc_l

                        def t(e, grp=grp, dhalf=dhalf, hc=hc, hc_l=hc_l,
                              buf=buf, wd_t=wd_t):
                            if hc_l == 0:
                                C.wait(e, "WD%d" % buf, wd_t)
                            if hc == 0:
                                C.wtok(e, "DV", hhev_tok[(grp, HC - 1)])
                                if dhalf == 0:
                                    C.wtok(e, "DV", tr_tok[grp * 4 + 3])
                                else:
                                    C.wtok(e, "SC", odv_tok[(grp, 3, 0)])
                            for ti in range(4):
                                ins = e.matmul(
                                    p_big[:, ti * 512:(ti + 1) * 512],
                                    hh[:, hc * 512 + ti * 128:
                                       hc * 512 + (ti + 1) * 128],
                                    wdb[:, buf * 2048 + hc_l * 512:
                                        buf * 2048 + (hc_l + 1) * 512],
                                    start=(hc == 0), stop=(hc == HC - 1),
                                    skip_group_check=True)
                            C.inc(ins, "PE", 1)
                        C.t(t)
                    dn_tok[(grp, dhalf, hb)] = C.tok("PE")

                for ti in range(4):
                    def s(e, grp=grp, ti=ti, dhalf=dhalf):
                        C.wtok(e, "PE", dn_tok[(grp, dhalf, HC // 4 - 1)])
                        if grp > 0 and dhalf == 0:
                            C.wtok(e, "DV", lnq_tok[(grp - 1, ti)])
                        C.inc(e.activation(
                            od[:, ti * D + dhalf * 512: ti * D + (dhalf + 1) * 512],
                            p_big[:, ti * 512:(ti + 1) * 512], AF.Identity,
                            scale=1.0 / (127.0 * 127.0)), "SC", 1)
                    C.s(s)
                    odv_tok[(grp, ti, dhalf)] = C.tok("SC")
                odv_tok[(grp, 3)] = C.tok("SC")

            for ti in range(4):
                i = grp * 4 + ti

                def dummy(e, grp=grp, ti=ti):
                    C.wtok(e, "SC", odv_tok[(grp, ti, 1)])
                    C.inc(e.engine_nop(), "DV", 1)
                C.v(dummy)
                emit_ln_post(C, od[:, ti * D:(ti + 1) * D],
                             xs[:, i * D:(i + 1) * D])
                lnq_tok[(grp, ti)] = C.tok("DV")


HHALF_G = HC // 2


def build_program(phases):
    nc = bass.Bass("TRN2", target_bir_lowering=False)

    xin = nc.dram_tensor("x", [NSEQ, S, D], F32, kind="ExternalInput").ap()
    out = nc.dram_tensor("out", [NSEQ, S, D], F32, kind="ExternalOutput").ap()
    dbg = nc.dram_tensor("dbg", [NSEQ, 4, 128, D], F32, kind="ExternalOutput").ap()
    W = {}
    for nm, shp, dt in [
        ("l0_wq", [D, D], F32), ("l0_wk", [D, D], F32), ("l0_wv", [D, D], F32),
        ("l0_bq", [D], F32), ("l0_bk", [D], F32), ("l0_bv", [D], F32),
        ("f0_wr", [D, D], F32), ("f0_wk", [D, D], F32), ("f0_wv", [D, D], F32),
        ("f0_mur", [D], F32), ("f0_muk", [D], F32),
        ("l1_wq", [D, D], F32), ("l1_wk", [D, D], F32), ("l1_wv", [D, D], F32),
        ("l1_wp", [D, D], F32),
        ("f1_wg", [HID, D], I32), ("f1_wu", [HID, D], I32),
        ("f1_wd", [D, HID], I32),
    ]:
        W[nm] = nc.dram_tensor(nm, shp, dt, kind="ExternalInput").ap()

    need_f0 = "l0f" in phases
    need_f1 = "l1f" in phases
    SCR = {}
    if "l1a" in phases:
        SCR["qscr"] = nc.dram_tensor("qscr", [NGRP, 128, DC * 512], BF16,
                                     kind="Internal").ap()
    if need_f0:
        for nm in ("f0_wr", "f0_wk", "f0_wv"):
            SCR[nm + "_s1"] = nc.dram_tensor("s1_" + nm, [D, D], BF16,
                                             kind="Internal").ap()
            SCR[nm + "_t"] = nc.dram_tensor("t_" + nm, [D, D], BF16,
                                            kind="Internal").ap()
    if need_f1:
        for nm, R, Cc in (("f1_wg", HID, D), ("f1_wu", HID, D),
                          ("f1_wd", D, HID)):
            SCR[nm + "_s1"] = nc.dram_tensor("s1_" + nm, [R, Cc], BF16,
                                             kind="Internal").ap()
            SCR[nm + "_t"] = nc.dram_tensor("t_" + nm, [Cc, R], BF16,
                                            kind="Internal").ap()

    with ExitStack() as stack:
        ec = stack.enter_context
        xs = ec(nc.sbuf_tensor("xs", [128, NTILE * D], F32))
        ident = ec(nc.sbuf_tensor("ident", [128, 128], BF16))
        ident65 = ec(nc.sbuf_tensor("ident65", [65, 65], BF16))
        ones_col = ec(nc.sbuf_tensor("ones_col", [128, 1], BF16))
        ones_row = ec(nc.sbuf_tensor("ones_row", [1, 128], BF16))
        cmask = ec(nc.sbuf_tensor("cmask", [128, 128], F32))
        cnt16 = ec(nc.sbuf_tensor("cnt16", [128, NTILE], F32))
        eps_v = ec(nc.sbuf_tensor("eps_v", [128, 1], F32))
        st8 = ec(nc.sbuf_tensor("st8", [128, 8], F32))
        scr_a = ec(nc.sbuf_tensor("scr_a", [128, D], F32))
        scr_b = ec(nc.sbuf_tensor("scr_b", [128, D], F32))

        block = ec(nc.Block())
        C = Ctx(nc, block, stack)
        C.dram = {"x": xin, "out": out, "dbg": dbg, **W}
        C.scr = SCR
        C.xs = xs
        C.k = dict(ident=ident, ident65=ident65, ones_col=ones_col,
                   ones_row=ones_row, cmask=cmask, cnt16=cnt16, eps_v=eps_v,
                   st8=st8, scr_a=scr_a, scr_b=scr_b)

        emit_consts(C)
        C.f0_prep_pending = need_f0
        C.f1_prep_pending = need_f1

        for seq in range(NSEQ):
            C.seq = seq
            emit_load_x(C, seq)
            C.barrier()
            if "l0a" in phases:
                emit_l0a(C)
                C.barrier()
            if "l0f" in phases:
                emit_l0f(C, SCR)
                C.barrier()
            if "l1a" in phases:
                emit_l1a(C)
                C.barrier()
            if "l1f" in phases:
                emit_l1f(C, SCR)
                C.barrier()
            emit_store_x(C, seq)
            C.barrier()
    return nc


def in_map_for_core(inputs, core):
    x = np.ascontiguousarray(inputs["L0_attn_X"][2 * core: 2 * core + 2])
    m = {"x": x}
    ren = {"l0_wq": "L0_attn_Wq", "l0_wk": "L0_attn_Wk", "l0_wv": "L0_attn_Wv",
           "l0_bq": "L0_attn_bq", "l0_bk": "L0_attn_bk", "l0_bv": "L0_attn_bv",
           "f0_wr": "L0_ffn_W_r", "f0_wk": "L0_ffn_W_k", "f0_wv": "L0_ffn_W_v",
           "f0_mur": "L0_ffn_mu_r", "f0_muk": "L0_ffn_mu_k",
           "l1_wq": "L1_attn_Wq", "l1_wk": "L1_attn_Wk", "l1_wv": "L1_attn_Wv",
           "l1_wp": "L1_attn_Wpost",
           "f1_wg": "L1_ffn_w_gate", "f1_wu": "L1_ffn_w_up",
           "f1_wd": "L1_ffn_w_down"}
    for a, b in ren.items():
        m[a] = np.ascontiguousarray(inputs[b])
    return m


def kernel(**inputs):
    nc = build_program(("l0a", "l0f", "l1a", "l1f"))
    in_maps = [in_map_for_core(inputs, c) for c in range(8)]
    res = run_bass_kernel_spmd(nc, in_maps, core_ids=list(range(8)))
    return np.concatenate([np.asarray(r["out"]) for r in res.results], axis=0)
